# version 04
# baseline (speedup 1.0000x reference)
import numpy as np
import ml_dtypes
import concourse.bass as bass
import concourse.mybir as mybir
from concourse.bass import IndirectOffsetOnAxis
from concourse.bass_utils import run_bass_kernel_spmd

F32 = mybir.dt.float32
BF16 = mybir.dt.bfloat16
U32 = mybir.dt.uint32
AF = mybir.ActivationFunctionType
ALU = mybir.AluOpType
AX = mybir.AxisListType
BF = ml_dtypes.bfloat16

B_CORE = 2
L = 2048
NCH = 8
TCOL = 2056
NEG = -3.0e38
NEGM = -1.0e30
S96 = float((64 + 32) ** (-0.5))

def _host_consts():
    c = {}
    t = np.arange(L, dtype=np.float64)
    theta = 1.0 / 10000.0 ** (2.0 * np.arange(16, dtype=np.float64) / 32.0)
    fr32 = np.tile(np.concatenate([theta, theta]), 4)
    c["TQS_C"] = np.cos(t[None, :] * fr32[:, None]).astype(BF)
    c["TQS_S"] = np.sin(t[None, :] * fr32[:, None]).astype(BF)
    ab = (2.0 * t - 63.0)[None, :] * fr32[:, None]
    c["TQB_C"] = np.cos(ab).astype(BF)
    c["TQB_S"] = np.sin(ab).astype(BF)

    P32 = np.zeros((128, 128), np.float32)
    for blk in range(4):
        for j in range(16):
            P32[32 * blk + j + 16, 32 * blk + j] = -1.0
            P32[32 * blk + j, 32 * blk + j + 16] = 1.0
    c["P_ROT32"] = P32.astype(BF)
    P64 = np.zeros((128, 128), np.float32)
    for blk in range(2):
        for j in range(32):
            P64[64 * blk + j + 32, 64 * blk + j] = -1.0
            P64[64 * blk + j, 64 * blk + j + 32] = 1.0
    c["P_ROT64"] = P64.astype(BF)

    halfa = np.arange(0, 64, 2, dtype=np.float64)
    inv_freq = 1.0 / 10000.0 ** (halfa / 64.0)
    fr64 = np.tile(np.concatenate([inv_freq, inv_freq]), 2)
    bs64 = np.tile(np.concatenate([(halfa + 25.6) / 89.6] * 2), 2)
    ang = t[None, :] * fr64[:, None]
    scl = bs64[:, None] ** ((t - L // 2) / 128.0)[None, :]
    c["TF_QC"] = (np.cos(ang) * scl).astype(BF)
    c["TF_QS"] = (np.sin(ang) * scl).astype(BF)
    c["TF_KC"] = (np.cos(ang) / scl).astype(BF)
    c["TF_KS"] = (np.sin(ang) / scl).astype(BF)

    r = np.arange(128)[:, None]
    cc = np.arange(256)[None, :]
    c["M_BAND"] = np.where((cc - r >= 65) & (cc - r <= 128), 0.0, NEG).astype(np.float32)
    p0 = cc - 128
    c["M_BAND0"] = np.where((p0 >= 0) & (r >= 64) & (p0 <= r) & (p0 >= r - 63),
                            0.0, NEG).astype(np.float32)
    c["M_FUT0"] = np.where((p0 >= 0) & (p0 <= 95) &
                           ((r <= 63) | ((r <= 94) & (p0 > r))),
                           0.0, NEG).astype(np.float32)
    c128 = np.arange(128)[None, :]
    c["M_TRI"] = np.where(c128 <= r, 0.0, NEGM).astype(np.float32)
    c["MF0"] = np.where(c128 >= r, 0.0, NEG).astype(np.float32)
    c["MF2"] = np.where(c128 <= r, 0.0, NEG).astype(np.float32)

    slot = 64.0 + np.arange(32, dtype=np.float64)
    a_sl = slot[:, None] * theta[None, :]
    cj, sj = np.cos(a_sl), np.sin(a_sl)
    T1 = np.concatenate([cj, cj], axis=1).reshape(-1).astype(np.float32)
    T2 = np.concatenate([-sj, sj], axis=1).reshape(-1).astype(np.float32)
    c["SLOT_T1"] = np.broadcast_to(T1.astype(BF), (128, 1024)).copy()
    c["SLOT_T2"] = np.broadcast_to(T2.astype(BF), (128, 1024)).copy()

    c["IDENT"] = np.eye(128, dtype=np.float32).astype(BF)
    c["IDF32"] = np.eye(128, dtype=np.float32)
    r1 = np.arange(128)[:, None]
    c1 = np.arange(384)[None, :]
    m = np.zeros((128, 384), np.float32)
    m[:, 0:128][~(c1[:, 0:128] >= r1)] = NEG
    m[:, 256:384][~(c1[:, 0:128] <= r1)] = NEG
    c["M384"] = m
    m0 = np.full((128, 384), NEG, np.float32)
    m0[:, 256:384] = np.where(c1[:, 0:128] <= r1, 0.0, NEG)
    c["M384_W0S0"] = m0
    m1 = np.full((128, 384), NEG, np.float32)
    m1[:, 128:256] = 0.0
    m1[:, 256:384] = np.where(c1[:, 0:128] <= r1, 0.0, NEG)
    c["M384_W0S1"] = m1
    ind = np.zeros((128, 2), np.float32)
    ind[0:64, 0] = 1.0
    ind[64:128, 1] = 1.0
    c["IND2"] = ind.astype(BF)
    c["IND2T"] = np.ascontiguousarray(ind.T).astype(BF)
    return c

CONST_SPECS = {
    "P_ROT32": ([128, 128], BF16), "P_ROT64": ([128, 128], BF16),
    "M_BAND": ([128, 256], F32), "M_BAND0": ([128, 256], F32),
    "M_FUT0": ([128, 256], F32), "M_TRI": ([128, 128], F32),
    "MF0": ([128, 128], F32), "MF2": ([128, 128], F32),
    "SLOT_T1": ([128, 1024], BF16), "SLOT_T2": ([128, 1024], BF16),
    "IDENT": ([128, 128], BF16), "IDF32": ([128, 128], F32),
    "IND2": ([128, 2], BF16), "IND2T": ([2, 128], BF16),
    "M384": ([128, 384], F32), "M384_W0S0": ([128, 384], F32),
    "M384_W0S1": ([128, 384], F32),
}
TABLE_NAMES = ["TQS_C", "TQS_S", "TQB_C", "TQB_S",
               "TF_QC", "TF_QS", "TF_KC", "TF_KS"]


def _host_weight_layouts(inputs):
    m = {}
    def chunked(w):
        din, dout = w.shape
        return np.ascontiguousarray(w.reshape(din // 128, 128, dout))
    for k in ("W_dkv", "W_uk", "W_uv", "W_dq", "W_uq", "W_qr", "W_kr",
              "W_out", "idx_wq", "idx_wk"):
        m["mla_" + k] = chunked(np.asarray(inputs["L0_attn_" + k], np.float32))
    m["mla_raw_delta"] = np.ascontiguousarray(
        np.asarray(inputs["L0_attn_raw_delta"], np.float32).reshape(1, 16))
    m["mla_idx_w"] = np.ascontiguousarray(np.broadcast_to(
        np.asarray(inputs["L0_attn_idx_w"], np.float32)[None, :], (128, 4)))

    qkv = np.asarray(inputs["L0_ffn_Wqkv"], np.float32)
    m["ffn_wqkv"] = np.ascontiguousarray(qkv.reshape(6, 8, 128, 3072))
    wo = np.asarray(inputs["L0_ffn_Wo"], np.float32)
    m["ffn_wo"] = np.ascontiguousarray(wo.reshape(6, 8, 128, 1024))
    w1 = np.asarray(inputs["L0_ffn_W1"], np.float32)
    w1p = np.zeros((6, 1024, 2, 22, 128), np.float32)
    w1p[:, :, 0, :, :] = np.pad(w1[:, :, :2730], ((0, 0), (0, 0), (0, 86))).reshape(6, 1024, 22, 128)
    w1p[:, :, 1, :, :] = np.pad(w1[:, :, 2730:], ((0, 0), (0, 0), (0, 86))).reshape(6, 1024, 22, 128)
    m["ffn_w1"] = np.ascontiguousarray(
        w1p.reshape(6, 8, 128, 2, 22, 128).transpose(0, 4, 3, 1, 2, 5))
    w2 = np.asarray(inputs["L0_ffn_W2"], np.float32)
    w2p = np.pad(w2, ((0, 0), (0, 86), (0, 0))).reshape(6, 22, 128, 8, 128)
    m["ffn_w2"] = np.ascontiguousarray(w2p.transpose(0, 3, 1, 2, 4))
    b1 = np.asarray(inputs["L0_ffn_b1"], np.float32)
    m["ffn_b1a"] = np.ascontiguousarray(
        np.pad(b1[:, :2730], ((0, 0), (0, 86))).reshape(6, 22, 128).transpose(0, 2, 1))
    m["ffn_b1g"] = np.ascontiguousarray(
        np.pad(b1[:, 2730:], ((0, 0), (0, 86))).reshape(6, 22, 128).transpose(0, 2, 1))
    for k, nm in (("b2", "b2"), ("ln_g", "lng"), ("ln_b", "lnb")):
        v = np.asarray(inputs["L0_ffn_" + k], np.float32)
        m["ffn_" + nm] = np.ascontiguousarray(v.reshape(6, 8, 128).transpose(0, 2, 1))
    for k, nm in (("q_scale", "qs"), ("k_scale", "ks")):
        v = np.asarray(inputs["L0_ffn_" + k], np.float32)
        m["ffn_" + nm] = np.ascontiguousarray(np.tile(v, (1, 2)).reshape(6, 128, 1))
    return m


ENGS = ("tensor", "vector", "scalar", "gpsimd", "sync")

class Plan:
    def __init__(self):
        self.ops = {e: [] for e in ENGS}
        self.sig_count = {e: 0 for e in ENGS}
        self.dma_count = {"dmaS": 0, "dmaG": 0, "dmaC": 0}
        self.hwm = {e: {} for e in ENGS}
        self._open_run = {"dmaS": None, "dmaG": None, "dmaC": None}
        self._q_of = {"sync": "dmaS", "gpsimd": "dmaG", "scalar": "dmaC"}

    def op(self, e, fn, sig=False):
        tok = None
        if sig:
            self.sig_count[e] += 1
            tok = (e, self.sig_count[e])
        self.ops[e].append(["op", fn, tok])
        return tok

    def _close_run(self, q):
        box = self._open_run[q]
        if box is not None:
            box["v"] = self.dma_count[q]
            self._open_run[q] = None

    def dma(self, e, fn):
        q = self._q_of[e]
        if self._open_run[q] is None:
            if self.dma_count[q] > 0:
                self.ops[e].append(["wait", (q, self.dma_count[q]), None])
            self._open_run[q] = {"v": None}
        self.dma_count[q] += 16
        tok = (q, self._open_run[q])
        self.ops[e].append(["dma", fn, (q, self.dma_count[q])])
        return tok

    def close_all_runs(self):
        for q in self._open_run:
            self._close_run(q)

    def wait(self, e, tok):
        if tok is None:
            return
        s, v = tok
        if e in self._q_of:
            self._close_run(self._q_of[e])
        if isinstance(v, int) and self.hwm[e].get(s, 0) >= v:
            return
        if isinstance(v, int):
            self.hwm[e][s] = v
        self.ops[e].append(["wait", (s, v), None])

    def sig_last(self, e):
        for ent in reversed(self.ops[e]):
            if ent[0] in ("op", "dma"):
                if ent[2] is not None:
                    return ent[2]
                self.sig_count[e] += 1
                ent[2] = (e, self.sig_count[e])
                return ent[2]
        return None

    def barrier(self, engs=ENGS):
        toks = []
        for e in engs:
            t = self.sig_last(e)
            if t is None:
                t = self.op(e, lambda eng: eng.nop(), sig=True)
            toks.append(t)
        for e in engs:
            for t in toks:
                self.wait(e, t)

    def run(self, block, sems):
        self.close_all_runs()
        def mk(e):
            def body(eng):
                hwm = {}
                for kind, payload, tok in self.ops[e]:
                    if kind == "wait":
                        s, v = payload
                        if not isinstance(v, int):
                            v = v["v"]
                            assert v is not None
                        if hwm.get(s, 0) >= v:
                            continue
                        hwm[s] = v
                        eng.wait_ge(sems[s], v)
                    else:
                        ins = payload(eng)
                        if tok is not None:
                            s, v = tok
                            ins.then_inc(sems[s], 16 if kind == "dma" else 1)
            return body
        block.tensor(mk("tensor"))
        block.vector(mk("vector"))
        block.scalar(mk("scalar"))
        block.gpsimd(mk("gpsimd"))
        block.sync(mk("sync"))


class Slab:
    def __init__(self, tile, ncols):
        self.tile = tile
        self.n = ncols
        self.free_segs = [(0, ncols)]
        self.named = {}

    def alloc(self, name, cols, dtype=BF16, top=False):
        w = cols if dtype == BF16 else 2 * cols
        idxs = range(len(self.free_segs) - 1, -1, -1) if top else range(len(self.free_segs))
        for i in idxs:
            o, ln = self.free_segs[i]
            if ln >= w:
                if top:
                    self.free_segs[i] = (o, ln - w)
                    o = o + ln - w
                else:
                    self.free_segs[i] = (o + w, ln - w)
                if self.free_segs[i][1] == 0:
                    self.free_segs.pop(i)
                self.named[name] = (o, w, dtype, cols)
                return self.view(name)
        raise RuntimeError(f"slab OOM for {name} ({w} cols); free={self.free_segs}")

    def view(self, name):
        o, w, dtype, cols = self.named[name]
        ap = self.tile[:, o:o + w]
        if dtype != BF16:
            ap = ap.bitcast(dtype)
        return ap

    def raw(self, name):
        o, w, _, _ = self.named[name]
        return self.tile[:, o:o + w]

    def free(self, name):
        o, w, _, _ = self.named.pop(name)
        segs = self.free_segs + [(o, w)]
        segs.sort()
        merged = []
        for s in segs:
            if merged and merged[-1][0] + merged[-1][1] == s[0]:
                merged[-1] = (merged[-1][0], merged[-1][1] + s[1])
            else:
                merged.append(list(s) if False else (s[0], s[1]))
        self.free_segs = [(a, b) for (a, b) in merged]


class Builder:
    def __init__(self, debug_taps=()):
        self.nc = bass.Bass("TRN2", target_bir_lowering=False)
        self.debug_taps = list(debug_taps)
        self.taps = []
        self._ctx = []
        self.P = Plan()

    def sb(self, name, shape, dt):
        h = self.nc.sbuf_tensor(name, list(shape), dt)
        t = h.__enter__()
        self._ctx.append(h)
        return t

    def ps_t(self, name, shape, dt=F32):
        h = self.nc.psum_tensor(name, list(shape), dt)
        t = h.__enter__()
        self._ctx.append(h)
        return t

    def sem(self, name):
        h = self.nc.semaphore(name)
        s = h.__enter__()
        self._ctx.append(h)
        return s

    def dram(self, name, shape, dt, kind=None):
        if kind:
            return self.nc.dram_tensor(name, list(shape), dt, kind=kind).ap()
        return self.nc.dram_tensor(name, list(shape), dt).ap()

    def finalize(self):
        for h in reversed(self._ctx):
            h.__exit__(None, None, None)


def build_program(debug_taps=(), n_ffn_layers=6):
    B = Builder(debug_taps)
    nc, P = B.nc, B.P

    XH = B.dram("x", [B_CORE, L, 1024], F32, kind="ExternalInput")
    WD = {}
    for k, s in (("W_dkv", [8, 128, 256]), ("W_uk", [2, 128, 512]),
                 ("W_uv", [2, 128, 512]), ("W_dq", [8, 128, 512]),
                 ("W_uq", [4, 128, 512]), ("W_qr", [4, 128, 256]),
                 ("W_kr", [8, 128, 32]), ("W_out", [4, 128, 1024]),
                 ("idx_wq", [8, 128, 256]), ("idx_wk", [8, 128, 64]),
                 ("idx_w", [128, 4]), ("raw_delta", [1, 16])):
        WD[k] = B.dram("mla_" + k, s, F32, kind="ExternalInput")
    for k, s in (("wqkv", [6, 8, 128, 3072]), ("wo", [6, 8, 128, 1024]),
                 ("w1", [6, 22, 2, 8, 128, 128]), ("w2", [6, 8, 22, 128, 128]),
                 ("b1a", [6, 128, 22]), ("b1g", [6, 128, 22]),
                 ("b2", [6, 128, 8]), ("lng", [6, 128, 8]), ("lnb", [6, 128, 8]),
                 ("qs", [6, 128, 1]), ("ks", [6, 128, 1])):
        WD[k] = B.dram("ffn_" + k, s, F32, kind="ExternalInput")
    CD = {k: B.dram("c_" + k, s, d, kind="ExternalInput")
          for k, (s, d) in CONST_SPECS.items()}
    TD = {k: B.dram("t_" + k, [128, 2048], BF16, kind="ExternalInput")
          for k in TABLE_NAMES}
    OUT = B.dram("o", [B_CORE, L, 1024], F32, kind="ExternalOutput")
    KVMU = B.dram("kvmu", [L, 544], BF16)
    VDR = B.dram("vdr", [L, 512], BF16)
    X2D = B.dram("x2d", [NCH, 128, L], F32)

    CS = {k: B.sb("s_" + k, s, d) for k, (s, d) in CONST_SPECS.items()}
    XI = [B.sb(f"xi{c}", [128, TCOL], BF16) for c in range(NCH)]
    XNs = Slab(B.sb("xn_slab", [128, 16384], BF16), 16384)
    TBL = Slab(B.sb("tbl_slab", [128, 4096], BF16), 4096)
    QB = B.sb("qb_slab", [128, 16384], BF16)
    KBS = B.sb("kb_slab", [128, 16384], BF16)
    WSL = Slab(B.sb("w_slab", [128, 24576], BF16), 24576)
    BSt = B.sb("bst", [128, 12], F32)
    MV = B.sb("mv", [128, 2], F32)
    RST = B.sb("rst", [128, 2], F32)
    M8 = B.sb("m8t", [128, 8], F32)
    TOPV = B.sb("topv", [128, 32], F32)
    TOPI = B.sb("topi", [128, 32], U32)
    TOPF = B.sb("topf", [128, 32], F32)
    SMASK = B.sb("smask", [128, 32], F32)
    IDXW = B.sb("idxw", [128, 4], F32)
    ZROW = B.sb("zrow", [128, 16], F32)
    NMX = B.sb("nmx", [128, 16], F32)
    SLG = B.sb("slg", [6, 128, 1], F32) if False else None
    QSC = B.sb("qsc", [128, 2], F32)
    LNP = B.sb("lnp", [128, 16], F32)
    B1A = B.sb("b1at", [128, 22], F32)
    B1G = B.sb("b1gt", [128, 22], F32)
    B2T = B.sb("b2t", [128, 8], F32)
    SROW = B.sb("srow", [1, 512], F32)
    SROW2 = B.sb("srow2", [1, 512], F32)
    SROWB = B.sb("srowb", [1, 1024], BF16)
    ONER = B.sb("oner", [1, 128], BF16)
    ONESB = B.sb("onesb", [128, 2], BF16)
    CCOL = B.sb("ccol", [128, 8], F32)
    DCOL = B.sb("dcol", [128, 2], F32)

    PS = [B.ps_t(f"ps{i}", [128, 512]) for i in range(8)]
    PSB = [p.ap().bitcast(BF16) if False else p for p in PS]

    sems = {k: B.sem("sm_" + k) for k in
            ("tensor", "vector", "scalar", "gpsimd", "sync", "dmaS", "dmaG", "dmaC")}

    dmaS = lambda fn: P.dma("sync", fn)
    dmaG = lambda fn: P.dma("gpsimd", fn)
    t_op = lambda fn, sig=False: P.op("tensor", fn, sig)
    v_op = lambda fn, sig=False: P.op("vector", fn, sig)
    s_op = lambda fn, sig=False: P.op("scalar", fn, sig)
    g_op = lambda fn, sig=False: P.op("gpsimd", fn, sig)

    def psbf(i):
        return PS[i][:].bitcast(BF16)

    TAPS = []
    _tapseen = {}
    def tap(name, ap_fn, shape, dt=F32):
        if name not in B.debug_taps:
            return
        if name in _tapseen:
            return
        _tapseen[name] = 1
        d = B.dram("tap_" + name, shape, dt, kind="ExternalOutput")
        P.barrier()
        dmaS(lambda e, d=d, f=ap_fn: e.dma_start(out=d, in_=f()))
        P.barrier()
        TAPS.append(name)

    itoks = [dmaS(lambda e, a=CS[k], d=CD[k]: e.dma_start(out=a[:], in_=d))
             for k in CONST_SPECS]
    itoks.append(dmaG(lambda e: e.dma_start(out=IDXW[:], in_=WD["idx_w"])))
    for t in itoks:
        for e in ENGS:
            P.wait(e, t)
    v_op(lambda e: e.memset(ONESB[:], 1.0))
    v_op(lambda e: e.memset(CCOL[:, 0:1], 1e-5))
    v_op(lambda e: e.memset(CCOL[:, 1:2], 1.0))
    v_op(lambda e: e.memset(CCOL[:, 2:3], -1.0))
    v_op(lambda e: e.memset(CCOL[:, 3:4], S96))
    v_op(lambda e: e.memset(CCOL[:, 4:5], 8.0))
    v_op(lambda e: e.memset(CCOL[:, 5:6], -0.5))
    v_op(lambda e: e.memset(CCOL[:, 6:7], float(np.pi)))
    v_op(lambda e: e.memset(CCOL[:, 7:8], float(-np.pi / 2)))
    v_op(lambda e: e.memset(ONER[:], 1.0))
    for c in range(NCH):
        v_op(lambda e, a=XI[c]: e.memset(a[:, 0:1], 0.0))
        v_op(lambda e, a=XI[c]: e.memset(a[:, 2049:TCOL], 0.0))
    P.barrier()
    IDENT = CS["IDENT"]

    def proj_fm(lhs_fn, srcs, n_oc, evac, T=2048, tgw=512, banks=(0, 1),
                ocw=128):
        pend = {i: [] for i in banks}
        k = 0
        for oc in range(n_oc):
            for tg in range(T // tgw):
                bk = banks[k % len(banks)]
                k += 1
                for tk in pend[bk]:
                    P.wait("tensor", tk)
                pend[bk] = []
                n = len(srcs)
                for c in range(n):
                    tok = t_op(lambda e, w=lhs_fn(c, oc), s=srcs[c],
                               t0=tg * tgw, tw=tgw, p=PS[bk],
                               st=(c == 0), sp=(c == n - 1), ocw=ocw:
                               e.matmul(p[0:ocw, 0:tw], w, s[:, t0:t0 + tw],
                                        start=st, stop=sp), sig=(c == n - 1))
                P.wait("vector", tok)
                P.wait("scalar", tok)
                pend[bk] = evac(oc, tg * tgw, tgw, bk)
        P.barrier()


    def phase_delta():
        DSCR = XNs.tile[:, 0:256].bitcast(F32)
        ld = dmaS(lambda e: e.dma_start(out=DSCR[0:1, 0:16], in_=WD["raw_delta"]))
        mz = v_op(lambda e: e.memset(DSCR[:, 16:128], 0.0), sig=True)
        P.wait("scalar", ld)
        P.wait("scalar", mz)
        s1 = s_op(lambda e: e.activation(out=DSCR[0:1, 16:32], in_=DSCR[0:1, 0:16],
                                         func=AF.Sigmoid), sig=True)
        P.wait("scalar", s1)
        s2 = s_op(lambda e: e.activation(out=DSCR[0:1, 32:48], in_=DSCR[0:1, 16:32],
                                         func=AF.Sin, scale=CCOL[0:1, 6:7],
                                         bias=CCOL[0:1, 7:8]), sig=True)
        P.wait("scalar", s2)
        P.wait("vector", s2)
        s3 = s_op(lambda e: e.activation(out=DSCR[0:1, 48:64], in_=DSCR[0:1, 32:48],
                                         func=AF.Square), sig=True)
        P.wait("vector", s3)
        v1 = v_op(lambda e: e.tensor_scalar(DSCR[0:1, 64:80], DSCR[0:1, 48:64],
                                            2.0, -1.0, op0=ALU.mult, op1=ALU.add),
                  sig=True)
        v2 = v_op(lambda e: e.tensor_scalar(DSCR[0:1, 80:96], DSCR[0:1, 48:64],
                                            -1.0, 1.0, op0=ALU.mult, op1=ALU.add),
                  sig=True)
        P.wait("scalar", v2)
        s4 = s_op(lambda e: e.activation(out=DSCR[0:1, 96:112], in_=DSCR[0:1, 80:96],
                                         func=AF.Sqrt), sig=True)
        P.wait("vector", s4)
        v3 = v_op(lambda e: e.tensor_mul(DSCR[0:1, 112:128], DSCR[0:1, 96:112],
                                         DSCR[0:1, 32:48]), sig=True)
        P.wait("vector", v3)
        v4 = v_op(lambda e: e.tensor_scalar_mul(DSCR[0:1, 112:128],
                                                DSCR[0:1, 112:128], 2.0), sig=True)
        P.wait("vector", v1)
        P.wait("vector", v4)
        v5 = v_op(lambda e: e.tensor_copy(DSCR[0:1, 0:16], DSCR[0:1, 64:80]), sig=True)
        v6 = v_op(lambda e: e.tensor_copy(DSCR[0:1, 16:32], DSCR[0:1, 112:128]), sig=True)
        P.wait("tensor", v5)
        P.wait("tensor", v6)
        tt = t_op(lambda e: e.transpose(PS[5][:, 0:128], DSCR[:, 0:128],
                                        CS["IDF32"][:]), sig=True)
        P.wait("vector", tt)
        ev = v_op(lambda e: e.tensor_copy(DCOL[0:32, 0:1],
                                          PS[5][0:32, 0:1]), sig=True)
        P.wait("gpsimd", ev)
        g1 = dmaG(lambda e: e.dma_start(out=DCOL[0:16, 1:2], in_=DCOL[16:32, 0:1]))
        P.wait("gpsimd", g1)
        for b0 in (16, 32, 64):
            g2 = dmaG(lambda e, b0=b0: e.dma_start(
                out=DCOL[b0:2 * b0, 0:2], in_=DCOL[0:b0, 0:2]))
            P.wait("gpsimd", g2)
        for en in ENGS:
            P.wait(en, g2)


    def phase_ln1(b, xn_v):
        XTa = TBL.alloc("xt", 2048, F32)
        XNT = QB[:, 0:2048]
        prev_tp = [None, None]
        prev_use = [None, None]
        for j in range(16):
            o = j % 2
            xt = XTa[:, o * 1024:(o + 1) * 1024]
            if prev_use[o] is not None:
                for tk in prev_use[o]:
                    P.wait("sync", tk)
            ld = dmaS(lambda e, b=b, j=j, xt=xt: e.dma_start(
                out=xt, in_=XH[b, 128 * j:128 * j + 128, :]))
            P.wait("vector", ld)
            if prev_tp[o] is not None:
                P.wait("vector", prev_tp[o])
            v1 = v_op(lambda e, xt=xt: e.bn_stats(BSt[:, 0:6], xt[:, 0:512]), sig=True)
            v2 = v_op(lambda e, xt=xt: e.bn_stats(BSt[:, 6:12], xt[:, 512:1024]), sig=True)
            P.wait("vector", v1)
            P.wait("vector", v2)
            v3 = v_op(lambda e: e.bn_aggr(MV[:], BSt[:]), sig=True)
            P.wait("scalar", v3)
            s1 = s_op(lambda e: e.activation(out=RST[:, 0:1], in_=MV[:, 1:2],
                                             func=AF.Sqrt, bias=CCOL[:, 0:1]), sig=True)
            P.wait("vector", s1)
            v4 = v_op(lambda e: e.reciprocal(RST[:, 1:2], RST[:, 0:1]), sig=True)
            v5 = v_op(lambda e: e.tensor_scalar_mul(MV[:, 0:1], MV[:, 0:1], -1.0), sig=True)
            P.wait("vector", v4)
            P.wait("vector", v5)
            xnt = XNT[:, o * 1024:(o + 1) * 1024]
            v6 = v_op(lambda e, xt=xt, xnt=xnt: e.tensor_scalar(
                xnt, xt, MV[:, 0:1], RST[:, 1:2], op0=ALU.add, op1=ALU.mult),
                sig=True)
            P.wait("tensor", v6)
            last_ev = None
            last_tt2 = None
            for c in range(NCH):
                pb = 6 + (c % 2)
                tt = t_op(lambda e, xnt=xnt, c=c, pb=pb: e.transpose(
                    psbf(pb)[:, 0:128], xnt[:, 128 * c:128 * c + 128], IDENT[:]),
                    sig=True)
                P.wait("vector", tt)
                last_ev = v_op(lambda e, c=c, j=j, pb=pb: e.tensor_copy(
                    xn_v[c][:, 128 * j:128 * j + 128], psbf(pb)[:, 0:128]), sig=True)
                P.wait("tensor", last_ev)
                tt2 = t_op(lambda e, xt=xt, c=c: e.transpose(
                    PS[5][:, 0:128], xt[:, 128 * c:128 * c + 128],
                    CS["IDF32"][:]), sig=True)
                P.wait("vector", tt2)
                last_ev = v_op(lambda e, c=c, j=j: e.tensor_copy(
                    XI[c][:, 1 + 128 * j:1 + 128 * j + 128], PS[5][:, 0:128]),
                    sig=True)
                P.wait("tensor", last_ev)
                last_tt2 = tt2
            prev_tp[o] = last_ev
            prev_use[o] = (v6, last_tt2)
        TBL.free("xt")
        P.barrier()

    def phase_mla_proj(b, xn_v):
        segs = [("WOUT", 4096), ("WUV", 1024), ("WDQ", 4096), ("WUQ", 2048),
                ("WQR", 1024), ("WDKV", 2048), ("WUK", 1024), ("WKR", 256)]
        V = {}
        for nm, cols in segs:
            V[nm] = WSL.alloc(nm, cols)
        wmap = {"WOUT": "W_out", "WUV": "W_uv", "WDKV": "W_dkv", "WUK": "W_uk",
                "WKR": "W_kr", "WDQ": "W_dq", "WUQ": "W_uq", "WQR": "W_qr"}
        wt = []
        for nm, _ in segs:
            d0 = WD[wmap[nm]]
            nch_, _, nn = d0.shape
            wt.append(dmaG(lambda e, d=d0, a=V[nm], c=nch_, n=nn: e.dma_start(
                out=a.rearrange("p (c n) -> p c n", c=c),
                in_=d.transpose([1, 0, 2]))))
        for t in wt:
            P.wait("tensor", t)
        P.barrier()

        CKV = WSL.alloc("CKV", 4096)
        ckv = [CKV[:, 0:2048], CKV[:, 2048:4096]]

        def ev_ckv(oc, t0, tw, bk):
            return [v_op(lambda e, oc=oc, t0=t0, tw=tw, p=PS[bk]: e.tensor_copy(
                ckv[oc][:, t0:t0 + tw], p[:, 0:tw]), sig=True)]
        proj_fm(lambda c, oc: V["WDKV"][:, c * 256 + oc * 128:c * 256 + oc * 128 + 128],
                xn_v, 2, ev_ckv)
        tap("ckv0", lambda: ckv[0][:], [128, 2048], BF16)

        for h in range(8):
            v_op(lambda e, h=h: e.memset(KBS[64:128, 2048 * h:2048 * (h + 1)], 0.0))
        def ev_kfm(oc, t0, tw, bk):
            t1 = v_op(lambda e, h=2 * oc, t0=t0, tw=tw, p=PS[bk]: e.tensor_copy(
                KBS[0:64, 2048 * h + t0:2048 * h + t0 + tw], p[0:64, 0:tw]), sig=True)
            t2 = v_op(lambda e, h=2 * oc + 1, t0=t0, tw=tw, p=PS[bk]: e.tensor_copy(
                KBS[0:64, 2048 * h + t0:2048 * h + t0 + tw], p[64:128, 0:tw]), sig=True)
            return [t1, t2]
        proj_fm(lambda c, oc: V["WUK"][:, c * 512 + oc * 128:c * 512 + oc * 128 + 128],
                ckv, 4, ev_kfm)

        STG = WSL.alloc("STG", 1280)
        vtt, ktt, ctt = STG[:, 0:512], STG[:, 512:1024], STG[:, 1024:1280]
        pdma = []
        prev_ev = []
        for j in range(16):
            for t in pdma:
                P.wait("vector", t)
            for t in prev_ev:
                P.wait("tensor", t)
            pdma = []
            toks = []
            for tgt, ps_i, rhs, nsrc, lhs_src in (
                    ("v", 2, V["WUV"], 2, ckv), ("k", 3, V["WUK"], 2, ckv)):
                for c in range(nsrc):
                    tok = t_op(lambda e, c=c, j=j, p=PS[ps_i], rhs=rhs,
                               st=(c == 0), sp=(c == nsrc - 1), ls=lhs_src:
                               e.matmul(p[:], ls[c][:, 128 * j:128 * j + 128],
                                        rhs[:, 512 * c:512 * c + 512],
                                        start=st, stop=sp), sig=(c == nsrc - 1))
                toks.append(tok)
            for t in toks:
                P.wait("vector", t)
            e1 = v_op(lambda e: e.tensor_copy(vtt[:], PS[2][:]), sig=True)
            e2 = v_op(lambda e: e.tensor_copy(ktt[:], PS[3][:]), sig=True)
            prev_ev = [e1, e2]
            P.wait("sync", e1)
            P.wait("sync", e2)
            d1 = dmaS(lambda e, j=j: e.dma_start(
                out=VDR[128 * j:128 * j + 128, :], in_=vtt[:]))
            d2 = dmaS(lambda e, j=j: e.dma_start(
                out=KVMU[128 * j:128 * j + 128, 0:512], in_=ktt[:]))
            pdma = [d1, d2]
        for t in pdma:
            P.wait("vector", t)
        P.barrier()

        tq_c = TBL.alloc("tq_c", 2048)
        tq_s = TBL.alloc("tq_s", 2048)
        lt1 = dmaS(lambda e: e.dma_start(out=tq_c, in_=TD["TQS_C"]))
        lt2 = dmaS(lambda e: e.dma_start(out=tq_s, in_=TD["TQS_S"]))
        MK = WSL.alloc("MK", 2048)
        TMPK = QB[:, 2048:4096]
        TMPK2 = QB[:, 4096:6144]
        v_op(lambda e: e.memset(MK[64:128, :], 0.0))
        for tg in range(4):
            toks = []
            for c in range(NCH):
                tok = t_op(lambda e, c=c, tg=tg: e.matmul(
                    PS[0][0:32, 0:512], V["WKR"][:, 32 * c:32 * c + 32],
                    xn_v[c][:, 512 * tg:512 * tg + 512],
                    start=(c == 0), stop=(c == NCH - 1)), sig=(c == NCH - 1))
            P.wait("scalar", tok)
            P.wait("vector", tok)
            a1 = s_op(lambda e: e.activation(out=PS[1][0:32, 0:512],
                                             in_=PS[0][0:32, 0:512], func=AF.Abs), sig=True)
            P.wait("scalar", a1)
            a2 = s_op(lambda e: e.activation(out=PS[2][0:32, 0:512],
                                             in_=PS[1][0:32, 0:512], func=AF.Exp,
                                             scale=CCOL[0:32, 2:3]), sig=True)
            P.wait("scalar", a2)
            a3 = s_op(lambda e, tg=tg: e.activation(out=QB[0:32, 0:512],
                                                    in_=PS[2][0:32, 0:512], func=AF.Ln,
                                                    bias=CCOL[0:32, 1:2]), sig=True)
            P.wait("vector", a3)
            v1 = v_op(lambda e, tg=tg: e.scalar_tensor_tensor(
                MK[0:32, 512 * tg:512 * tg + 512], PS[0][0:32, 0:512], 0.0,
                QB[0:32, 0:512], op0=ALU.max, op1=ALU.add), sig=True)
            P.wait("tensor", v1)
            P.wait("scalar", v1)
        P.wait("vector", lt1)
        P.wait("vector", lt2)
        P.wait("tensor", lt1)
        mz = v_op(lambda e: e.memset(MK[32:64, :], 0.0), sig=True)
        P.wait("tensor", mz)
        for tg in range(4):
            tok = t_op(lambda e, tg=tg: e.matmul(
                PS[3][:], CS["P_ROT32"][:], MK[:, 512 * tg:512 * tg + 512],
                start=True, stop=True), sig=True)
            P.wait("vector", tok)
            d1 = v_op(lambda e, tg=tg: e.tensor_scalar_mul(
                TMPK[0:32, 512 * tg:512 * tg + 512],
                MK[0:32, 512 * tg:512 * tg + 512], DCOL[0:32, 0:1]), sig=True)
            P.wait("vector", d1)
            d2 = v_op(lambda e, tg=tg: e.scalar_tensor_tensor(
                MK[0:32, 512 * tg:512 * tg + 512], PS[3][0:32, 0:512],
                DCOL[0:32, 1:2], TMPK[0:32, 512 * tg:512 * tg + 512],
                op0=ALU.mult, op1=ALU.add), sig=True)
            P.wait("tensor", d2)
        for tg in range(4):
            tok = t_op(lambda e, tg=tg: e.matmul(
                PS[3][:], CS["P_ROT32"][:], MK[:, 512 * tg:512 * tg + 512],
                start=True, stop=True), sig=True)
            P.wait("vector", tok)
            v1 = v_op(lambda e, tg=tg: e.tensor_tensor(
                TMPK[0:32, 512 * tg:512 * tg + 512], MK[0:32, 512 * tg:512 * tg + 512],
                tq_c[0:32, 512 * tg:512 * tg + 512], op=ALU.mult), sig=True)
            v2 = v_op(lambda e, tg=tg: e.tensor_tensor(
                TMPK2[0:32, 512 * tg:512 * tg + 512],
                PS[3][0:32, 0:512],
                tq_s[0:32, 512 * tg:512 * tg + 512], op=ALU.mult), sig=True)
            P.wait("vector", v1)
            P.wait("vector", v2)
            v3 = v_op(lambda e, tg=tg: e.tensor_tensor(
                KBS[64:96, 512 * tg:512 * tg + 512], TMPK2[0:32, 512 * tg:512 * tg + 512],
                TMPK[0:32, 512 * tg:512 * tg + 512], op=ALU.add), sig=True)
            P.wait("tensor", v3)
        P.wait("vector", v3)
        ctk = []
        for h in range(1, 8):
            ctk.append(v_op(lambda e, h=h: e.tensor_copy(
                KBS[64:96, 2048 * h:2048 * h + 2048], KBS[64:96, 0:2048]), sig=True))
        mut = STG[:, 0:512]
        for j in range(16):
            tt = t_op(lambda e, j=j: e.transpose(
                psbf(5)[:, 0:128], MK[:, 128 * j:128 * j + 128], IDENT[:]), sig=True)
            P.wait("vector", tt)
            ev = v_op(lambda e, j=j: e.tensor_copy(
                mut[:, 0:32], psbf(5)[:, 0:32]), sig=True)
            P.wait("sync", ev)
            P.wait("tensor", ev)
            dd = dmaS(lambda e, j=j: e.dma_start(
                out=KVMU[128 * j:128 * j + 128, 512:544], in_=mut[:, 0:32]))
            P.wait("tensor", dd)
            P.wait("vector", dd)
        for t in ctk:
            P.wait("tensor", t)
        P.barrier()
        tap("kb8_0", lambda: KBS[:, 0:2048], [128, 2048], BF16)
        WSL.free("MK"); WSL.free("WKR"); WSL.free("WDKV"); WSL.free("WUK")
        WSL.free("CKV"); WSL.free("STG")
        TBL.free("tq_c"); TBL.free("tq_s")

        CQ = WSL.alloc("CQ", 8192)
        cq = [CQ[:, 2048 * i:2048 * (i + 1)] for i in range(4)]
        def ev_cq(oc, t0, tw, bk):
            return [v_op(lambda e, oc=oc, t0=t0, tw=tw, p=PS[bk]: e.tensor_copy(
                cq[oc][:, t0:t0 + tw], p[:, 0:tw]), sig=True)]
        proj_fm(lambda c, oc: V["WDQ"][:, c * 512 + oc * 128:c * 512 + oc * 128 + 128],
                xn_v, 4, ev_cq)

        MQ = WSL.alloc("MQ", 4096)
        mq = [MQ[:, 0:2048], MQ[:, 2048:4096]]
        spq2 = TBL.tile[:, 2048:3072]
        spq = TBL.alloc("spq", 1024, F32)
        def ev_qr(oc, t0, tw, bk):
            a1 = s_op(lambda e, p=PS[bk], tw=tw: e.activation(
                out=PS[2][:, 0:tw], in_=p[:, 0:tw], func=AF.Abs), sig=True)
            P.wait("scalar", a1)
            a2 = s_op(lambda e, tw=tw: e.activation(
                out=PS[3][:, 0:tw], in_=PS[2][:, 0:tw], func=AF.Exp,
                scale=CCOL[:, 2:3]), sig=True)
            P.wait("scalar", a2)
            a3 = s_op(lambda e, tw=tw: e.activation(
                out=spq[:, 0:tw], in_=PS[3][:, 0:tw], func=AF.Ln,
                bias=CCOL[:, 1:2]), sig=True)
            P.wait("vector", a3)
            v1 = v_op(lambda e, oc=oc, t0=t0, tw=tw, p=PS[bk]: e.scalar_tensor_tensor(
                mq[oc][:, t0:t0 + tw], p[:, 0:tw], 0.0, spq[:, 0:tw],
                op0=ALU.max, op1=ALU.add), sig=True)
            P.wait("scalar", v1)
            return [v1]
        proj_fm(lambda c, oc: V["WQR"][:, c * 256 + oc * 128:c * 256 + oc * 128 + 128],
                cq, 2, ev_qr, banks=(0, 1))
        WSL.free("WQR")
        TBL.free("spq")
        for ch in range(2):
            for tg in range(4):
                sl = slice(512 * tg, 512 * tg + 512)
                tok = t_op(lambda e, ch=ch, sl=sl: e.matmul(
                    PS[3][:], CS["P_ROT32"][:], mq[ch][:, sl],
                    start=True, stop=True), sig=True)
                P.wait("vector", tok)
                d1 = v_op(lambda e, ch=ch, sl=sl: e.tensor_scalar_mul(
                    spq2[:, 0:512], mq[ch][:, sl], DCOL[:, 0:1]), sig=True)
                P.wait("vector", d1)
                d2 = v_op(lambda e, ch=ch, sl=sl: e.scalar_tensor_tensor(
                    mq[ch][:, sl], PS[3][:], DCOL[:, 1:2], spq2[:, 0:512],
                    op0=ALU.mult, op1=ALU.add), sig=True)
                P.wait("tensor", d2)
                P.wait("vector", d2)
        tap("mq0rot", lambda: mq[0][:], [128, 2048], BF16)

        for h in range(8):
            v_op(lambda e, h=h: e.memset(QB[64:128, 2048 * h:2048 * (h + 1)], 0.0))
        def ev_qc(oc, t0, tw, bk):
            t1 = v_op(lambda e, h=2 * oc, t0=t0, tw=tw, p=PS[bk]: e.tensor_copy(
                QB[0:64, 2048 * h + t0:2048 * h + t0 + tw], p[0:64, 0:tw]), sig=True)
            t2 = v_op(lambda e, h=2 * oc + 1, t0=t0, tw=tw, p=PS[bk]: e.tensor_copy(
                QB[0:64, 2048 * h + t0:2048 * h + t0 + tw], p[64:128, 0:tw]), sig=True)
            return [t1, t2]
        proj_fm(lambda c, oc: V["WUQ"][:, c * 512 + oc * 128:c * 512 + oc * 128 + 128],
                cq, 4, ev_qc)
        WSL.free("CQ"); WSL.free("WDQ"); WSL.free("WUQ")
        tap("qb8_0", lambda: QB[:, 0:2048], [128, 2048], BF16)

        TMPB = WSL.alloc("TMPB", 4096)
        tq_c = TBL.alloc("tq_c", 2048)
        tq_s = TBL.alloc("tq_s", 2048)

        def build_rot(tc_name, ts_name, dst_fn):
            l1 = dmaS(lambda e: e.dma_start(out=tq_c, in_=TD[tc_name]))
            l2 = dmaS(lambda e: e.dma_start(out=tq_s, in_=TD[ts_name]))
            P.wait("vector", l1)
            P.wait("vector", l2)
            for ch in range(2):
                dst = dst_fn(ch)
                for tg in range(4):
                    sl = slice(512 * tg, 512 * tg + 512)
                    tok = t_op(lambda e, ch=ch, sl=sl: e.matmul(
                        PS[3][:], CS["P_ROT32"][:], mq[ch][:, sl],
                        start=True, stop=True), sig=True)
                    P.wait("vector", tok)
                    v1 = v_op(lambda e, ch=ch, sl=sl: e.tensor_tensor(
                        TMPB[:, sl], mq[ch][:, sl], tq_c[:, sl], op=ALU.mult),
                        sig=True)
                    v2 = v_op(lambda e, sl=sl, dst=dst: e.tensor_tensor(
                        dst[:, sl], PS[3][:, 0:512], tq_s[:, sl], op=ALU.mult),
                        sig=True)
                    P.wait("vector", v1)
                    P.wait("vector", v2)
                    v3 = v_op(lambda e, sl=sl, dst=dst: e.tensor_tensor(
                        dst[:, sl], dst[:, sl], TMPB[:, sl], op=ALU.add), sig=True)
                    P.wait("tensor", v3)
            P.barrier()

        QS = WSL.alloc("QS", 4096, top=True)
        build_rot("TQB_C", "TQB_S", lambda ch: QS[:, 2048 * ch:2048 * (ch + 1)])
        for h in range(8):
            src = QS[:, 2048 * (h // 4):2048 * (h // 4) + 2048]
            r0 = 32 * (h % 4)
            dg = dmaG(lambda e, h=h, src=src, r0=r0: e.dma_start(
                out=QB[64:96, 2048 * h:2048 * h + 2048], in_=src[r0:r0 + 32, :]))
            P.wait("tensor", dg)
            P.wait("vector", dg)
        P.barrier()
        build_rot("TQS_C", "TQS_S", lambda ch: QS[:, 2048 * ch:2048 * (ch + 1)])
        WSL.free("MQ"); WSL.free("TMPB")

        TBL.free("tq_c"); TBL.free("tq_s")
        tap("qb8_full_0", lambda: QB[:, 0:2048], [128, 2048], BF16)
        tap("qhs_0", lambda: QS[:, 0:2048], [128, 2048], BF16)

        WIQ = WSL.alloc("WIQ", 2048)
        WIK = WSL.alloc("WIK", 512)
        l1 = dmaG(lambda e: e.dma_start(
            out=WIQ.rearrange("p (c n) -> p c n", c=8),
            in_=WD["idx_wq"].transpose([1, 0, 2])))
        l2 = dmaG(lambda e: e.dma_start(
            out=WIK.rearrange("p (c n) -> p c n", c=8),
            in_=WD["idx_wk"].transpose([1, 0, 2])))
        P.wait("tensor", l1)
        P.wait("tensor", l2)
        QI = WSL.alloc("QI", 4096, top=True)
        qi = [QI[:, 0:2048], QI[:, 2048:4096]]
        def ev_qi(oc, t0, tw, bk):
            return [v_op(lambda e, oc=oc, t0=t0, tw=tw, p=PS[bk]: e.tensor_copy(
                qi[oc][:, t0:t0 + tw], p[:, 0:tw]), sig=True)]
        proj_fm(lambda c, oc: WIQ[:, c * 256 + oc * 128:c * 256 + oc * 128 + 128],
                xn_v, 2, ev_qi)
        KI = WSL.alloc("KI", 2048, top=True)
        def ev_ki(oc, t0, tw, bk):
            t1 = v_op(lambda e, t0=t0, tw=tw, p=PS[bk]: e.tensor_copy(
                KI[0:64, t0:t0 + tw], p[0:64, 0:tw]), sig=True)
            t2 = v_op(lambda e, t0=t0, tw=tw, p=PS[bk]: e.tensor_copy(
                KI[64:128, t0:t0 + tw], p[0:64, 0:tw]), sig=True)
            return [t1, t2]
        proj_fm(lambda c, oc: WIK[:, c * 64:c * 64 + 64], xn_v, 1, ev_ki, ocw=64)
        WSL.free("WIQ"); WSL.free("WIK")
        tap("qi0", lambda: qi[0][:], [128, 2048], BF16)
        tap("qi1", lambda: qi[1][:], [128, 2048], BF16)
        tap("idxw", lambda: IDXW[:], [128, 4], F32)
        tap("ki", lambda: KI[:], [128, 2048], BF16)
        P.barrier()
        return {"QS": QS, "QI": QI, "KI": KI, "V": V}

    def phase_mla_attend(b, H):
        QSv, QIv, KIv = H["QS"], H["QI"], H["KI"]
        V = H["V"]
        GKV = WSL.alloc("GKV", 6400)
        QCTM = WSL.alloc("QCTM", 512)
        QSTMt = WSL.alloc("QSTMt", 256)
        QFUT = WSL.alloc("QFUT", 1024)
        VST = WSL.alloc("VST", 1024)
        ITA = XNs.alloc("itA", 4096)
        ITB = XNs.alloc("itB", 4096)
        IT = [ITA.bitcast(F32), ITB.bitcast(F32)]
        SFUT8 = IT[0]
        AFUT8 = ITB[:, 0:2048]
        SELACC = ITB[:, 2048:4096].bitcast(F32)
        SBND8 = XNs.alloc("sbnd8", 2048, F32)
        ABND8 = XNs.alloc("abnd8", 2048)
        SSEL = XNs.alloc("ssel", 256, F32)
        ASEL8 = XNs.alloc("asel8", 256)
        ATP = XNs.alloc("atp", 512)
        PRD = XNs.alloc("prd", 256)
        PRD8 = XNs.raw("sbnd8")
        AB_RAW = XNs.raw("abnd8")
        P3R = AB_RAW[:, 0:768]
        GMR = ATP[:, 0:256]
        ATM = AB_RAW[:, 0:512]
        AFM = AB_RAW[:, 512:1024]

        v_op(lambda e: e.memset(GKV[:], 0.0))
        for h in range(8):
            v_op(lambda e, h=h: e.memset(QFUT[96:128, 128 * h:128 * h + 128], 0.0))
            v_op(lambda e, h=h: e.tensor_copy(
                QFUT[0:64, 128 * h:128 * h + 128], QB[0:64, 2048 * h:2048 * h + 128]))
            dg = dmaG(lambda e, h=h: e.dma_start(
                out=QFUT[64:96, 128 * h:128 * h + 128],
                in_=QSv[32 * (h % 4):32 * (h % 4) + 32,
                        2048 * (h // 4):2048 * (h // 4) + 128]))
            P.wait("tensor", dg)
        P.barrier()

        def emit_tau(tau):
            q0 = 128 * tau
            RW = 64 + 128 * tau
            cur = IT[0]
            nrc = (RW + 511) // 512
            for rc in range(nrc):
                c0 = 512 * rc
                cw = min(512, RW - c0)
                mtk = []
                for h in range(4):
                    r0 = 64 * (h % 2)
                    mtk.append(t_op(lambda e, h=h, r0=r0, c0=c0, cw=cw, q0=q0: e.matmul(
                        PS[h][0:128, 0:cw],
                        QIv[r0:r0 + 64, 2048 * (h // 2) + q0:2048 * (h // 2) + q0 + 128],
                        KIv[r0:r0 + 64, c0:c0 + cw],
                        start=True, stop=True), sig=True))
                for t in mtk:
                    P.wait("scalar", t)
                rtk = []
                for h in range(4):
                    rtk.append(s_op(lambda e, h=h, cw=cw: e.activation(
                        out=PS[4 + h][:, 0:cw], in_=PS[h][:, 0:cw], func=AF.Relu),
                        sig=True))
                for t in rtk:
                    P.wait("vector", t)
                if tau == 3 and c0 == 0:
                    _dbg = XNs.raw("sbnd8").bitcast(F32)
                    for t in mtk:
                        P.wait("vector", t)
                    dv = v_op(lambda e, cw=cw, _dbg=_dbg: e.tensor_copy(
                        _dbg[:, 0:cw], PS[0][:, 0:cw]), sig=True)
                    P.wait("vector", dv)
                    for t in rtk:
                        P.wait("vector", t)
                    dv2 = v_op(lambda e, cw=cw, _dbg=_dbg: e.tensor_copy(
                        _dbg[:, 512:512 + cw], PS[4][:, 0:cw]), sig=True)
                    P.wait("vector", dv2)
                    tap("dbg_s0_t3", lambda _dbg=_dbg: _dbg, [128, 2048], F32)
                v1 = v_op(lambda e, c0=c0, cw=cw: e.tensor_scalar_mul(
                    cur[:, c0:c0 + cw], PS[4][:, 0:cw], IDXW[:, 0:1]), sig=True)
                P.wait("vector", v1)
                for h in range(1, 4):
                    v1 = v_op(lambda e, h=h, c0=c0, cw=cw: e.scalar_tensor_tensor(
                        cur[:, c0:c0 + cw], PS[4 + h][:, 0:cw], IDXW[:, h:h + 1],
                        cur[:, c0:c0 + cw], op0=ALU.mult, op1=ALU.add), sig=True)
                    P.wait("vector", v1)
                P.wait("tensor", v1)
            if tau == 0:
                vm = v_op(lambda e: e.tensor_tensor(
                    cur[:, 0:64], cur[:, 0:64], CS["M_TRI"][:, 64:128], op=ALU.add),
                    sig=True)
            else:
                vm = v_op(lambda e, RW=RW: e.tensor_tensor(
                    cur[:, RW - 128:RW], cur[:, RW - 128:RW], CS["M_TRI"][:],
                    op=ALU.add), sig=True)
            P.wait("vector", vm)

            if tau == 3:
                _it = IT[0]
                tap("ireg_t3", lambda _it=_it: _it, [128, 2048], F32)
                tap("qi0_t3", lambda: QIv[:, 0:2048], [128, 2048], BF16)
                tap("ki_t3", lambda: KIv[:, 0:2048], [128, 2048], BF16)
            for rd in range(4):
                nxt = IT[(rd + 1) % 2]
                src = IT[rd % 2]
                m1 = v_op(lambda e, src=src, RW=RW: e.max(
                    out=M8[:], in_=src[:, 0:RW]), sig=True)
                P.wait("vector", m1)
                m2 = v_op(lambda e, src=src, rd=rd: e.max_index(
                    out=TOPI[:, 8 * rd:8 * rd + 8].bitcast(U32) if False else TOPI[:, 8 * rd:8 * rd + 8],
                    in_max=M8[:], in_values=src[:, 0:RW]), sig=True)
                m3 = v_op(lambda e, rd=rd: e.tensor_copy(
                    TOPV[:, 8 * rd:8 * rd + 8], M8[:]), sig=True)
                P.wait("vector", m2)
                P.wait("vector", m3)
                if rd < 4 - 1:
                    m4 = v_op(lambda e, src=src, nxt=nxt, RW=RW: e.match_replace(
                        out=nxt[:, 0:RW], in_to_replace=M8[:], in_values=src[:, 0:RW],
                        imm_value=NEGM), sig=True)
                    P.wait("vector", m4)
            k1 = v_op(lambda e: e.tensor_copy(TOPF[:], TOPI[:]), sig=True)
            P.wait("vector", k1)
            k2 = v_op(lambda e: e.tensor_scalar_min(TOPF[:], TOPF[:], 2047.0), sig=True)
            P.wait("vector", k2)
            k3 = v_op(lambda e: e.tensor_copy(TOPI[:], TOPF[:]), sig=True)
            P.wait("vector", k3)
            c2 = v_op(lambda e: e.tensor_scalar(
                SMASK[:], TOPV[:], -1e29, 3.0e38, op0=ALU.is_gt, op1=ALU.bypass),
                sig=True)
            P.wait("vector", c2)
            c3 = v_op(lambda e: e.tensor_scalar(
                SMASK[:], SMASK[:], -1.0, 3.0e38, op0=ALU.add, op1=ALU.mult), sig=True)
            P.wait("gpsimd", c3)

            for h in range(8):
                tt = t_op(lambda e, h=h, q0=q0: e.transpose(
                    psbf(6)[:, 0:128], QB[:, 2048 * h + q0:2048 * h + q0 + 128],
                    IDENT[:]), sig=True)
                P.wait("vector", tt)
                ev = v_op(lambda e, h=h: e.tensor_copy(
                    QCTM[:, 64 * h:64 * h + 64], psbf(6)[:, 0:64]), sig=True)
                P.wait("tensor", ev)
            for ch in range(2):
                tt = t_op(lambda e, ch=ch, q0=q0: e.transpose(
                    psbf(6)[:, 0:128], QSv[:, 2048 * ch + q0:2048 * ch + q0 + 128],
                    IDENT[:]), sig=True)
                P.wait("vector", tt)
                ev = v_op(lambda e, ch=ch: e.tensor_copy(
                    QSTMt[:, 128 * ch:128 * ch + 128], psbf(6)[:, 0:128]), sig=True)
                P.wait("tensor", ev)
            P.barrier()

            G3 = GKV[:, 0:4352].rearrange("p (j e) -> p j e", j=8)
            prev_grp = None
            for g in range(4):
                if prev_grp is not None:
                    P.wait("gpsimd", prev_grp)
                gts = []
                for j in range(8):
                    sj = 8 * g + j
                    gts.append(dmaG(lambda e, sj=sj, j=j: e.indirect_dma_start(
                        out=GKV[:, 544 * j:544 * j + 544], out_offset=None,
                        in_=KVMU,
                        in_offset=IndirectOffsetOnAxis(ap=TOPI[:, sj:sj + 1], axis=0))))
                for t in gts:
                    P.wait("vector", t)
                T1 = CS["SLOT_T1"][:, 256 * g:256 * g + 256].rearrange(
                    "p (j d) -> p j d", j=8)
                T2 = CS["SLOT_T2"][:, 256 * g:256 * g + 256].rearrange(
                    "p (j d) -> p j d", j=8)
                GM3 = GMR[:].rearrange("p (j d) -> p j d", j=8)
                r1 = v_op(lambda e, T1=T1, GM3=GM3: e.tensor_tensor(
                    GM3, G3[:, :, 512:544], T1, op=ALU.mult), sig=True)
                r2 = v_op(lambda e, T2=T2: e.tensor_tensor(
                    PRD[:].rearrange("p (j d) -> p j d", j=8)[:, :, 0:16],
                    G3[:, :, 528:544], T2[:, :, 0:16], op=ALU.mult), sig=True)
                r3 = v_op(lambda e, T2=T2: e.tensor_tensor(
                    PRD[:].rearrange("p (j d) -> p j d", j=8)[:, :, 16:32],
                    G3[:, :, 512:528], T2[:, :, 16:32], op=ALU.mult), sig=True)
                P.wait("vector", r1)
                P.wait("vector", r2)
                P.wait("vector", r3)
                r4 = v_op(lambda e: e.tensor_tensor(
                    GMR[:], GMR[:], PRD[:], op=ALU.add), sig=True)
                P.wait("vector", r4)
                P3 = P3R.rearrange("p (j d) -> p j d", j=8)
                for h in range(8):
                    d1 = v_op(lambda e, h=h: e.tensor_tensor(
                        P3[:, :, 0:64], G3[:, :, 64 * h:64 * h + 64],
                        QCTM[:, 64 * h:64 * h + 64].unsqueeze(1).broadcast_to(
                            [128, 8, 64]),
                        op=ALU.mult), sig=True)
                    d2 = v_op(lambda e, h=h: e.tensor_tensor(
                        P3[:, :, 64:96], GM3,
                        QSTMt[:, 32 * h:32 * h + 32].unsqueeze(1).broadcast_to(
                            [128, 8, 32]),
                        op=ALU.mult), sig=True)
                    P.wait("vector", d1)
                    P.wait("vector", d2)
                    d3 = v_op(lambda e, h=h, g=g: e.tensor_reduce(
                        SSEL[:, 32 * h + 8 * g:32 * h + 8 * g + 8], P3,
                        axis=AX.X, op=ALU.add), sig=True)
                    P.wait("vector", d3)
                prev_grp = d3
            sm = v_op(lambda e: e.tensor_tensor(
                SSEL[:].rearrange("p (h j) -> p h j", h=8),
                SSEL[:].rearrange("p (h j) -> p h j", h=8),
                SMASK[:].unsqueeze(1).broadcast_to([128, 8, 32]),
                op=ALU.add), sig=True)
            P.wait("vector", sm)

            if tau == 3:
                tap("topi_t3", lambda: TOPI[:], [128, 32], U32)
                tap("topv_t3", lambda: TOPV[:], [128, 32], F32)
                tap("ssel_t3", lambda: SSEL[:], [128, 256], F32)
                tap("qctm_t3", lambda: QCTM[:], [128, 512], BF16)
                tap("qstm_t3", lambda: QSTMt[:], [128, 256], BF16)
                tap("gmr_t3", lambda: GMR[:], [128, 256], BF16)
            vl1 = dmaS(lambda e, tau=tau: e.dma_start(
                out=VST[:, 512:1024], in_=VDR[128 * tau:128 * tau + 128, :]))
            if tau > 0:
                vl0 = dmaS(lambda e, tau=tau: e.dma_start(
                    out=VST[:, 0:512], in_=VDR[128 * (tau - 1):128 * tau, :]))
            ws = q0 - 128
            btks = []
            for h in range(8):
                bank, colh = h % 4, 256 * (h // 4)
                if tau == 0:
                    btks.append(t_op(lambda e, h=h, bank=bank, colh=colh, q0=q0: e.matmul(
                        PS[bank][:, colh + 128:colh + 256],
                        QB[:, 2048 * h + q0:2048 * h + q0 + 128],
                        KBS[:, 2048 * h:2048 * h + 128],
                        start=True, stop=True), sig=True))
                    btks.append(t_op(lambda e, h=h: e.matmul(
                        PS[4 + h % 4][:, 256 * (h // 4) + 128:256 * (h // 4) + 256],
                        QFUT[:, 128 * h:128 * h + 128],
                        KBS[:, 2048 * h:2048 * h + 128],
                        start=True, stop=True), sig=True))
                else:
                    btks.append(t_op(lambda e, h=h, bank=bank, colh=colh, ws=ws, q0=q0: e.matmul(
                        PS[bank][:, colh:colh + 256],
                        QB[:, 2048 * h + q0:2048 * h + q0 + 128],
                        KBS[:, 2048 * h + ws:2048 * h + ws + 256],
                        start=True, stop=True), sig=True))
            for t in btks:
                P.wait("vector", t)

            MB = CS["M_BAND0"] if tau == 0 else CS["M_BAND"]
            for h in range(8):
                bank, colh = h % 4, 256 * (h // 4)
                if tau == 0:
                    v_op(lambda e, h=h: e.memset(SBND8[:, 256 * h:256 * h + 128], NEG))
                    v_op(lambda e, h=h, bank=bank, colh=colh, MB=MB: e.tensor_tensor(
                        SBND8[:, 256 * h + 128:256 * h + 256],
                        PS[bank][:, colh + 128:colh + 256], MB[:, 128:256],
                        op=ALU.add), sig=True)
                    v_op(lambda e, h=h: e.memset(SFUT8[:, 256 * h:256 * h + 128], NEG))
                    v_op(lambda e, h=h: e.tensor_tensor(
                        SFUT8[:, 256 * h + 128:256 * h + 256],
                        PS[4 + h % 4][:, 256 * (h // 4) + 128:256 * (h // 4) + 256],
                        CS["M_FUT0"][:, 128:256], op=ALU.add), sig=True)
                else:
                    v_op(lambda e, h=h, bank=bank, colh=colh, MB=MB: e.tensor_tensor(
                        SBND8[:, 256 * h:256 * h + 256],
                        PS[bank][:, colh:colh + 256], MB[:], op=ALU.add), sig=True)
            vv = P.sig_last("vector")
            P.wait("vector", vv)
            n1 = v_op(lambda e: e.tensor_reduce(
                NMX[:, 0:8], SBND8[:].rearrange("p (h c) -> p h c", h=8),
                axis=AX.X, op=ALU.max, negate=True), sig=True)
            n2 = v_op(lambda e: e.tensor_reduce(
                NMX[:, 8:16], SSEL[:].rearrange("p (h j) -> p h j", h=8),
                axis=AX.X, op=ALU.max, negate=True), sig=True)
            P.wait("vector", n1)
            P.wait("vector", n2)
            n3 = v_op(lambda e: e.tensor_tensor(
                NMX[:, 0:8], NMX[:, 0:8], NMX[:, 8:16], op=ALU.min), sig=True)
            if tau == 0:
                P.wait("vector", n3)
                nf = v_op(lambda e: e.tensor_reduce(
                    NMX[:, 8:16], SFUT8[:].rearrange("p (h c) -> p h c", h=8),
                    axis=AX.X, op=ALU.max, negate=True), sig=True)
                P.wait("vector", nf)
                n3 = v_op(lambda e: e.tensor_tensor(
                    NMX[:, 0:8], NMX[:, 0:8], NMX[:, 8:16], op=ALU.min), sig=True)
            P.wait("vector", n3)
            n4 = v_op(lambda e: e.tensor_scalar_mul(NMX[:, 0:8], NMX[:, 0:8], S96),
                      sig=True)
            P.wait("scalar", n4)
            for h in range(8):
                s_op(lambda e, h=h: e.activation(
                    out=ABND8[:, 256 * h:256 * h + 256],
                    in_=SBND8[:, 256 * h:256 * h + 256], func=AF.Exp,
                    bias=NMX[:, h:h + 1], scale=CCOL[:, 3:4],
                    accum_out=ZROW[:, h:h + 1]), sig=True)
                s_op(lambda e, h=h: e.activation(
                    out=ASEL8[:, 32 * h:32 * h + 32],
                    in_=SSEL[:, 32 * h:32 * h + 32], func=AF.Exp,
                    bias=NMX[:, h:h + 1], scale=CCOL[:, 3:4],
                    accum_out=ZROW[:, 8 + h:8 + h + 1]), sig=True)
                if tau == 0:
                    s_op(lambda e, h=h: e.activation(
                        out=AFUT8[:, 256 * h:256 * h + 256],
                        in_=SFUT8[:, 256 * h:256 * h + 256], func=AF.Exp,
                        bias=NMX[:, h:h + 1], scale=CCOL[:, 3:4],
                        accum_out=NMX[:, 8 + h:8 + h + 1]), sig=True)
            sv = P.sig_last("scalar")
            P.wait("vector", sv)
            z1 = v_op(lambda e: e.tensor_tensor(
                ZROW[:, 0:8], ZROW[:, 0:8], ZROW[:, 8:16], op=ALU.add), sig=True)
            P.wait("vector", z1)
            if tau == 0:
                z1 = v_op(lambda e: e.tensor_tensor(
                    ZROW[:, 0:8], ZROW[:, 0:8], NMX[:, 8:16], op=ALU.add), sig=True)
                P.wait("vector", z1)
            z2 = v_op(lambda e: e.reciprocal(ZROW[:, 8:16], ZROW[:, 0:8]), sig=True)
            P.wait("vector", z2)
            for h in range(8):
                v_op(lambda e, h=h: e.tensor_scalar_mul(
                    ABND8[:, 256 * h:256 * h + 256], ABND8[:, 256 * h:256 * h + 256],
                    ZROW[:, 8 + h:8 + h + 1]), sig=True)
                v_op(lambda e, h=h: e.tensor_scalar_mul(
                    ASEL8[:, 32 * h:32 * h + 32], ASEL8[:, 32 * h:32 * h + 32],
                    ZROW[:, 8 + h:8 + h + 1]), sig=True)
                if tau == 0:
                    v_op(lambda e, h=h: e.tensor_scalar_mul(
                        AFUT8[:, 256 * h:256 * h + 256],
                        AFUT8[:, 256 * h:256 * h + 256],
                        ZROW[:, 8 + h:8 + h + 1]), sig=True)
            va = P.sig_last("vector")
            P.wait("tensor", va)
            P.wait("tensor", vl1)
            if tau > 0:
                P.wait("tensor", vl0)

            if tau == 3:
                _sb = XNs.raw("sbnd8").bitcast(F32)
                tap("sbnd_t3", lambda _sb=_sb: _sb, [128, 2048], F32)
                tap("abnd_t3", lambda: ABND8[:], [128, 2048], BF16)
                tap("asel_t3", lambda: ASEL8[:], [128, 256], BF16)
                tap("zrow_t3", lambda: ZROW[:], [128, 16], F32)
            av_items = []
            for h in range(8):
                blks = (1,) if tau == 0 else (0, 1)
                for blk in blks:
                    av_items.append((
                        ABND8[:, 256 * h + 128 * blk:256 * h + 128 * blk + 128],
                        VST[:, 512 * blk + 64 * h:512 * blk + 64 * h + 64], h))
                if tau == 0:
                    av_items.append((
                        AFUT8[:, 256 * h + 128:256 * h + 256],
                        VST[:, 512 + 64 * h:512 + 64 * h + 64], h))
            n_av = len(av_items)
            prev_mm = {0: None, 1: None, 2: None}
            for i, (src, vsl, h) in enumerate(av_items):
                slot = i % 3
                tt = t_op(lambda e, src=src: e.transpose(
                    psbf(6)[:, 0:128], src, IDENT[:]), sig=True)
                P.wait("vector", tt)
                if prev_mm[slot] is not None:
                    P.wait("vector", prev_mm[slot])
                ev = v_op(lambda e, slot=slot: e.tensor_copy(
                    ATP[:, 128 * slot:128 * slot + 128], psbf(6)[:, 0:128]),
                    sig=True)
                P.wait("tensor", ev)
                mm = t_op(lambda e, slot=slot, vsl=vsl, h=h,
                          st=(i == 0), sp=(i == n_av - 1): e.matmul(
                    PS[4][:, 64 * h:64 * h + 64], ATP[:, 128 * slot:128 * slot + 128],
                    vsl, start=st, stop=sp), sig=True)
                prev_mm[slot] = mm
            tv = P.sig_last("tensor")
            P.wait("vector", tv)

            GV3 = GKV[:, 0:4096].rearrange("p (j d) -> p j d", j=8)
            sel_first = [True]
            P.wait("gpsimd", prev_grp)
            for g in range(4):
                if not sel_first[0] or g > 0:
                    P.wait("gpsimd", prev_grp)
                gts = []
                for j in range(8):
                    sj = 8 * g + j
                    gts.append(dmaG(lambda e, sj=sj, j=j: e.indirect_dma_start(
                        out=GKV[:, 512 * j:512 * j + 512], out_offset=None,
                        in_=VDR,
                        in_offset=IndirectOffsetOnAxis(ap=TOPI[:, sj:sj + 1], axis=0))))
                for t in gts:
                    P.wait("vector", t)
                in1 = (ASEL8[:].rearrange("p (h j) -> p h j", h=8)[:, :, 8 * g:8 * g + 8]
                       .transpose([0, 2, 1]).unsqueeze(3)
                       .broadcast_to([128, 8, 8, 64]))
                m1 = v_op(lambda e, in1=in1: e.tensor_tensor(
                    PRD8[:].rearrange("p (j h d) -> p j h d", j=8, h=8),
                    GV3.rearrange("p j (h d) -> p j h d", h=8), in1,
                    op=ALU.mult), sig=True)
                P.wait("vector", m1)
                red_in = PRD8[:].rearrange("p (j m) -> p j m", j=8).transpose([0, 2, 1])
                if sel_first[0]:
                    m2 = v_op(lambda e, red_in=red_in: e.tensor_reduce(
                        SELACC[:, 0:512], red_in, axis=AX.X, op=ALU.add), sig=True)
                    sel_first[0] = False
                else:
                    m2 = v_op(lambda e, red_in=red_in: e.tensor_reduce(
                        SELACC[:, 512:1024], red_in, axis=AX.X, op=ALU.add), sig=True)
                    P.wait("vector", m2)
                    m2 = v_op(lambda e: e.tensor_tensor(
                        SELACC[:, 0:512], SELACC[:, 0:512], SELACC[:, 512:1024],
                        op=ALU.add), sig=True)
                P.wait("vector", m2)
                prev_grp = m1

            av = v_op(lambda e: e.tensor_tensor(
                ATM[:], PS[4][:], SELACC[:, 0:512], op=ALU.add), sig=True)
            P.wait("tensor", av)
            if tau == 3:
                tap("atm_t3", lambda: ATM[:], [128, 512], BF16)
                tap("selacc_t3", lambda: SELACC[:], [128, 1024], F32)
            for c4 in range(4):
                tt = t_op(lambda e, c4=c4: e.transpose(
                    psbf(6)[:, 0:128], ATM[:, 128 * c4:128 * c4 + 128], IDENT[:]),
                    sig=True)
                P.wait("vector", tt)
                ev = v_op(lambda e, c4=c4: e.tensor_copy(
                    AFM[:, 128 * c4:128 * c4 + 128], psbf(6)[:, 0:128]), sig=True)
                P.wait("tensor", ev)
            va = P.sig_last("vector")
            for oc in range(8):
                for c4 in range(4):
                    tok = t_op(lambda e, oc=oc, c4=c4: e.matmul(
                        PS[5][:, 0:128],
                        V["WOUT"][:, 1024 * c4 + 128 * oc:1024 * c4 + 128 * oc + 128],
                        AFM[:, 128 * c4:128 * c4 + 128],
                        start=(c4 == 0), stop=(c4 == 3)), sig=(c4 == 3))
                P.wait("vector", tok)
                rv = v_op(lambda e, oc=oc, q0=q0: e.tensor_tensor(
                    XI[oc][:, 1 + q0:1 + q0 + 128], XI[oc][:, 1 + q0:1 + q0 + 128],
                    PS[5][:, 0:128], op=ALU.add), sig=True)
                P.wait("tensor", rv)
            P.barrier()

        for tau in range(16):
            emit_tau(tau)
        for nm in ("GKV", "QCTM", "QSTMt", "QFUT", "VST", "QS", "QI", "KI",
                   "WOUT", "WUV"):
            WSL.free(nm)
        for nm in ("itA", "itB", "sbnd8", "abnd8", "ssel", "asel8",
                   "atp", "prd"):
            XNs.free(nm)
        P.barrier()



    def phase_ffn(b, n_layers=6):
        WQKV = WSL.alloc("WQKV", 24576)
        WOsl = XNs.alloc("WO", 8192)
        XN2 = XNs.alloc("XN2", 8192)
        QW = QB[:, 0:2048]; QN = QB[:, 2048:4096]; QRT = QB[:, 4096:6144]
        SMF = QB[:, 6144:9216].bitcast(F32)
        AT = QB[:, 9216:10752]
        ATPF = QB[:, 10752:11136]
        AOW = QB[:, 11136:13184]
        AVE = QB[:, 13184:14208]
        KW = KBS[:, 0:2048]; KN = KBS[:, 2048:4096]
        KRT = KBS[:, 4096:8192]
        VW = KBS[:, 8192:12288]
        TBLQ = TBL.alloc("tblq", 1024)
        RST2 = KBS[:, 12288:16384].bitcast(F32)
        RST2B = TBL.tile[:, 1024:3072]
        SS2 = XNs.tile[:, 8192:12288].bitcast(F32)

        def emit_ln_one_tg(tg, shift, g_ap, b_ap, dst, scr):
            sl0 = 1 + 512 * tg
            prev_t2 = None
            for c in range(NCH):
                off = sl0 - 1 if (shift and c >= 4) else sl0
                src = XI[c][:, off:off + 512]
                if prev_t2 is not None:
                    P.wait("scalar", prev_t2)
                sq = s_op(lambda e, src=src, scr=scr: e.activation(
                    out=scr, in_=src, func=AF.Square), sig=True)
                P.wait("tensor", sq)
                t1 = t_op(lambda e, src=src, c=c: e.matmul(
                    PS[5][0:1, 0:512], ONESB[:, 0:1], src,
                    start=(c == 0), stop=(c == NCH - 1)), sig=(c == NCH - 1))
                t2 = t_op(lambda e, scr=scr, c=c: e.matmul(
                    PS[6][0:1, 0:512], ONESB[:, 1:2], scr,
                    start=(c == 0), stop=(c == NCH - 1)), sig=True)
                prev_t2 = t2
            P.wait("vector", t1)
            P.wait("vector", t2)
            r1 = v_op(lambda e: e.tensor_scalar_mul(
                SROW[0:1, :], PS[5][0:1, 0:512], -1.0 / 1024.0), sig=True)
            r2 = v_op(lambda e: e.tensor_scalar_mul(
                SROW2[0:1, :], PS[6][0:1, 0:512], 1.0 / 1024.0), sig=True)
            P.wait("scalar", r1)
            P.wait("vector", r2)
            r3 = s_op(lambda e: e.activation(out=PS[5][0:1, 0:512],
                                             in_=SROW[0:1, :], func=AF.Square),
                      sig=True)
            P.wait("vector", r3)
            r4 = v_op(lambda e: e.tensor_sub(SROW2[0:1, :], SROW2[0:1, :],
                                             PS[5][0:1, 0:512]), sig=True)
            P.wait("scalar", r4)
            r5 = s_op(lambda e: e.activation(out=SROW2[0:1, :], in_=SROW2[0:1, :],
                                             func=AF.Ln, bias=CCOL[0:1, 0:1]),
                      sig=True)
            P.wait("scalar", r5)
            r6 = s_op(lambda e: e.activation(out=SROW2[0:1, :], in_=SROW2[0:1, :],
                                             func=AF.Exp, scale=CCOL[0:1, 5:6]),
                      sig=True)
            P.wait("vector", r6)
            r7 = v_op(lambda e: e.tensor_copy(SROWB[0:1, 0:512], SROW[0:1, :]),
                      sig=True)
            P.wait("vector", r7)
            r8 = v_op(lambda e: e.tensor_copy(SROWB[0:1, 512:1024], SROW2[0:1, :]),
                      sig=True)
            P.wait("tensor", r7)
            P.wait("tensor", r8)
            b1t = t_op(lambda e: e.matmul(PS[5][:, 0:512], ONER[0:1, :],
                                          SROWB[0:1, 0:512], start=True, stop=True),
                       sig=True)
            b2t = t_op(lambda e: e.matmul(PS[6][:, 0:512], ONER[0:1, :],
                                          SROWB[0:1, 512:1024], start=True,
                                          stop=True), sig=True)
            P.wait("vector", b1t)
            P.wait("vector", b2t)
            lastv = None
            for c in range(NCH):
                off = sl0 - 1 if (shift and c >= 4) else sl0
                src = XI[c][:, off:off + 512]
                d = dst(c, tg)
                v1 = v_op(lambda e, src=src, scr=scr: e.tensor_add(
                    scr, src, PS[5][:, 0:512]), sig=True)
                P.wait("vector", v1)
                if g_ap is None:
                    v2 = v_op(lambda e, d=d, scr=scr: e.tensor_mul(
                        d, scr, PS[6][:, 0:512]), sig=True)
                else:
                    v2 = v_op(lambda e, scr=scr: e.tensor_mul(
                        scr, scr, PS[6][:, 0:512]), sig=True)
                    P.wait("vector", v2)
                    v2 = v_op(lambda e, d=d, scr=scr, c=c, g_ap=g_ap, b_ap=b_ap:
                              e.tensor_scalar(
                        d, scr, g_ap[:, c:c + 1], b_ap[:, c:c + 1],
                        op0=ALU.mult, op1=ALU.add), sig=True)
                P.wait("vector", v2)
                lastv = v2
            P.wait("tensor", lastv)
            P.wait("scalar", lastv)

        def load_layer_weights(l):
            toks = [dmaG(lambda e, l=l: e.dma_start(
                out=WQKV.rearrange("p (c n) -> p c n", c=8),
                in_=WD["wqkv"][l].transpose([1, 0, 2])))]
            toks.append(dmaG(lambda e, l=l: e.dma_start(
                out=WOsl.rearrange("p (c n) -> p c n", c=8),
                in_=WD["wo"][l].transpose([1, 0, 2]))))
            toks.append(dmaG(lambda e, l=l: e.dma_start(out=QSC[:, 0:1],
                                                        in_=WD["qs"][l])))
            toks.append(dmaG(lambda e, l=l: e.dma_start(out=QSC[:, 1:2],
                                                        in_=WD["ks"][l])))
            toks.append(dmaG(lambda e, l=l: e.dma_start(out=B1A[:], in_=WD["b1a"][l])))
            toks.append(dmaG(lambda e, l=l: e.dma_start(out=B1G[:], in_=WD["b1g"][l])))
            toks.append(dmaG(lambda e, l=l: e.dma_start(out=B2T[:], in_=WD["b2"][l])))
            toks.append(dmaG(lambda e, l=l: e.dma_start(out=LNP[:, 0:8],
                                                        in_=WD["lng"][l])))
            toks.append(dmaG(lambda e, l=l: e.dma_start(out=LNP[:, 8:16],
                                                        in_=WD["lnb"][l])))
            return toks

        def emit_attn_sub(l, w, sub, hg, first_avp, prev_mask, tstate):
            par = w % 2
            prev_par = 1 - par
            q0p = 128 * sub
            mmt = []
            if prev_mask[0] is not None:
                P.wait("tensor", prev_mask[0])
            for hh in range(4):
                h = 4 * hg + hh
                c = h // 2
                r0 = 64 * (h % 2)
                qs0 = 256 * c + q0p
                wA = 256 - 128 * sub
                pieces = []
                if w > 0:
                    pieces.append((512 * c + 256 * prev_par + 128 * sub, wA, 0))
                pieces.append((512 * c + 256 * par, 128 + 128 * sub, wA))
                for (kc, kw_, oc_) in pieces:
                    tok = t_op(lambda e, hh=hh, r0=r0, qs0=qs0, kc=kc, kw_=kw_,
                               oc_=oc_: e.matmul(
                        PS[hh][:, oc_:oc_ + kw_],
                        QRT[r0:r0 + 64, qs0:qs0 + 128],
                        KRT[r0:r0 + 64, kc:kc + kw_],
                        start=True, stop=True), sig=True)
                    mmt.append(tok)
            for t in mmt:
                P.wait("vector", t)
            mknm = "M384" if w > 0 else ("M384_W0S0" if sub == 0 else "M384_W0S1")
            MK = CS[mknm]
            v1 = None
            for hh in range(4):
                v1 = v_op(lambda e, hh=hh, MK=MK: e.tensor_add(
                    SMF[:, 384 * hh:384 * hh + 384], PS[hh][:, 0:384], MK[:]),
                    sig=True)
            prev_mask[0] = v1
            P.wait("vector", v1)
            n1 = v_op(lambda e: e.tensor_reduce(
                NMX[:, 0:4], SMF[:].rearrange("p (h c) -> p h c", h=4),
                axis=AX.X, op=ALU.max, negate=True), sig=True)
            P.wait("vector", n1)
            n2 = v_op(lambda e: e.tensor_scalar_mul(NMX[:, 0:4], NMX[:, 0:4], 8.0),
                      sig=True)
            P.wait("scalar", n2)
            ex = None
            for hh in range(4):
                ex = s_op(lambda e, hh=hh: e.activation(
                    out=AT[:, 384 * hh:384 * hh + 384],
                    in_=SMF[:, 384 * hh:384 * hh + 384], func=AF.Exp,
                    bias=NMX[:, hh:hh + 1], scale=CCOL[:, 4:5],
                    accum_out=ZROW[:, hh:hh + 1]), sig=True)
            P.wait("vector", ex)
            rz = v_op(lambda e: e.reciprocal(ZROW[:, 4:8], ZROW[:, 0:4]), sig=True)
            P.wait("vector", rz)
            sc = None
            for hh in range(4):
                sc = v_op(lambda e, hh=hh: e.tensor_scalar_mul(
                    AT[:, 384 * hh:384 * hh + 384], AT[:, 384 * hh:384 * hh + 384],
                    ZROW[:, 4 + hh:4 + hh + 1]), sig=True)
            P.wait("tensor", sc)
            blk0 = 2 * w - 2 + sub
            close_bank = hg in (1, 3)
            kk = 0
            last_mm = None
            for hh in range(4):
                h = 4 * hg + hh
                avbank = 4 + (h // 8)
                blks = [bi for bi in range(3) if blk0 + bi >= 0]
                for bn, bi in enumerate(blks):
                    blk = blk0 + bi
                    p = kk % 2
                    tb = 6 + p
                    if tstate["ev"][p] is not None:
                        P.wait("tensor", tstate["ev"][p])
                    tt = t_op(lambda e, hh=hh, bi=bi, tb=tb: e.transpose(
                        psbf(tb)[:, 0:128],
                        AT[:, 384 * hh + 128 * bi:384 * hh + 128 * bi + 128],
                        IDENT[:]), sig=True)
                    P.wait("vector", tt)
                    if tstate["mm"][p] is not None:
                        P.wait("vector", tstate["mm"][p])
                    ev = v_op(lambda e, p=p, tb=tb: e.tensor_copy(
                        ATPF[:, 128 * p:128 * p + 128], psbf(tb)[:, 0:128]),
                        sig=True)
                    tstate["ev"][p] = ev
                    P.wait("tensor", ev)
                    slot = blk % 4
                    st = first_avp[h // 8]
                    first_avp[h // 8] = False
                    sp = close_bank and hh == 3 and bn == len(blks) - 1
                    mm = t_op(lambda e, h=h, slot=slot, st=st, sp=sp, p=p,
                              avbank=avbank: e.matmul(
                        PS[avbank][:, 64 * (h % 8):64 * (h % 8) + 64],
                        ATPF[:, 128 * p:128 * p + 128],
                        VW[:, 1024 * slot + 64 * h:1024 * slot + 64 * h + 64],
                        start=st, stop=sp), sig=True)
                    tstate["mm"][p] = mm
                    last_mm = mm
                    kk += 1
            return last_mm

        def emit_window(l, w):
            par = w % 2
            t0 = 1 + 256 * w
            last_qk_evac = [None]
            ttk = []
            for i, nm in enumerate(("TF_QC", "TF_QS", "TF_KC", "TF_KS")):
                ttk.append(dmaS(lambda e, nm=nm, i=i, w=w: e.dma_start(
                    out=TBLQ[:, 256 * i:256 * i + 256],
                    in_=TD[nm][:, 256 * w:256 * w + 256])))
            last_ev = {6: None, 7: None, 3: None, 2: None}
            for oc in range(8):
                for part, ps_i, dst in ((0, 6, QW), (1024, 7, KW)):
                    if last_ev[ps_i] is not None:
                        P.wait("tensor", last_ev[ps_i])
                    for c in range(NCH):
                        off = t0 - 1 if c >= 4 else t0
                        tok = t_op(lambda e, c=c, oc=oc, part=part, ps_i=ps_i,
                                   off=off: e.matmul(
                            PS[ps_i][:, 0:256],
                            WQKV[:, 3072 * c + part + 128 * oc:
                                 3072 * c + part + 128 * oc + 128],
                            XI[c][:, off:off + 256],
                            start=(c == 0), stop=(c == NCH - 1)),
                            sig=(c == NCH - 1))
                    P.wait("vector", tok)
                    ev = v_op(lambda e, oc=oc, ps_i=ps_i, dst=dst: e.tensor_copy(
                        dst[:, 256 * oc:256 * oc + 256], PS[ps_i][:, 0:256]),
                        sig=True)
                    last_ev[ps_i] = ev
                    last_qk_evac[0] = ev
            for sub in range(2):
                for nh in range(2):
                    ps_i = 3 if (2 * sub + nh) % 2 == 0 else 2
                    if last_ev[ps_i] is not None:
                        P.wait("tensor", last_ev[ps_i])
                    for c in range(NCH):
                        off = t0 - 1 if c >= 4 else t0
                        tok = t_op(lambda e, c=c, sub=sub, nh=nh, ps_i=ps_i,
                                   off=off: e.matmul(
                            PS[ps_i][:, 0:512],
                            XI[c][:, off + 128 * sub:off + 128 * sub + 128],
                            WQKV[:, 3072 * c + 2048 + 512 * nh:
                                 3072 * c + 2048 + 512 * nh + 512],
                            start=(c == 0), stop=(c == NCH - 1)),
                            sig=(c == NCH - 1))
                    P.wait("vector", tok)
                    slot = (2 * w + sub) % 4
                    ev = v_op(lambda e, slot=slot, nh=nh, sub=sub,
                              ps_i=ps_i: e.tensor_copy(
                        VW[:, 1024 * slot + 512 * nh:1024 * slot + 512 * nh + 512],
                        PS[ps_i][:, 0:512]), sig=True)
                    last_ev[ps_i] = ev
            for t in ttk:
                P.wait("vector", t)
            P.wait("scalar", last_qk_evac[0])
            last_sum_ev = None
            last_sq_mm = [None, None]
            last_stt = {5: None, 6: None}
            last_r2 = {5: None, 6: None}
            last_r3 = None
            for qk, SRCW, DSTN, scl_col, tc0 in (
                    (0, QW, QN, 0, 0), (1, KW, KN, 1, 512)):
                if last_r3 is not None:
                    P.wait("scalar", last_r3)
                for i in range(4):
                    sl = 512 * (i % 2)
                    if last_sq_mm[i % 2] is not None:
                        P.wait("scalar", last_sq_mm[i % 2])
                    sq = s_op(lambda e, i=i, sl=sl, SRCW=SRCW: e.activation(
                        out=AVE[:, sl:sl + 512],
                        in_=SRCW[:, 512 * i:512 * i + 512],
                        func=AF.Square), sig=True)
                    P.wait("tensor", sq)
                    if last_sum_ev is not None:
                        P.wait("tensor", last_sum_ev)
                    mm = t_op(lambda e, sl=sl: e.matmul(
                        PS[4][0:2, 0:512], CS["IND2"][:], AVE[:, sl:sl + 512],
                        start=True, stop=True), sig=True)
                    last_sq_mm[i % 2] = mm
                    P.wait("vector", mm)
                    ev = v_op(lambda e, i=i: e.tensor_copy(
                        SS2[0:2, 512 * i:512 * i + 512], PS[4][0:2, 0:512]),
                        sig=True)
                    last_sum_ev = ev
                P.wait("vector", last_sum_ev)
                s2 = v_op(lambda e: e.tensor_scalar_max(SS2[0:2, :], SS2[0:2, :],
                                                        1e-24), sig=True)
                P.wait("scalar", s2)
                s3 = s_op(lambda e: e.activation(out=RST2[0:2, :], in_=SS2[0:2, :],
                                                 func=AF.Ln), sig=True)
                P.wait("scalar", s3)
                s4 = s_op(lambda e: e.activation(out=RST2B[0:2, :],
                                                 in_=RST2[0:2, :], func=AF.Exp,
                                                 scale=CCOL[0:2, 5:6]), sig=True)
                P.wait("tensor", s4)
                for c in range(NCH):
                    bank = 5 + (c % 2)
                    if last_stt[bank] is not None:
                        P.wait("tensor", last_stt[bank])
                    bc = t_op(lambda e, c=c, bank=bank: e.matmul(
                        PS[bank][:, 0:256], CS["IND2T"][:],
                        RST2B[0:2, 256 * c:256 * c + 256], start=True, stop=True),
                        sig=True)
                    P.wait("vector", bc)
                    stt = v_op(lambda e, c=c, bank=bank, SRCW=SRCW, DSTN=DSTN,
                               scl_col=scl_col: e.scalar_tensor_tensor(
                        DSTN[:, 256 * c:256 * c + 256], PS[bank][:, 0:256],
                        QSC[:, scl_col:scl_col + 1],
                        SRCW[:, 256 * c:256 * c + 256],
                        op0=ALU.mult, op1=ALU.mult), sig=True)
                    last_stt[bank] = stt
                    P.wait("tensor", stt)
                    if last_r2[bank] is not None:
                        P.wait("tensor", last_r2[bank])
                    rt = t_op(lambda e, c=c, bank=bank, DSTN=DSTN: e.matmul(
                        PS[bank][:, 256:512], CS["P_ROT64"][:],
                        DSTN[:, 256 * c:256 * c + 256], start=True, stop=True),
                        sig=True)
                    a0 = 512 * (c % 2)
                    P.wait("vector", stt)
                    r1 = v_op(lambda e, c=c, a0=a0, DSTN=DSTN, tc0=tc0:
                              e.tensor_tensor(
                        AVE[:, a0:a0 + 256], DSTN[:, 256 * c:256 * c + 256],
                        TBLQ[:, tc0:tc0 + 256], op=ALU.mult), sig=True)
                    P.wait("vector", rt)
                    r2 = v_op(lambda e, a0=a0, bank=bank, tc0=tc0:
                              e.tensor_tensor(
                        AVE[:, a0 + 256:a0 + 512], PS[bank][:, 256:512],
                        TBLQ[:, tc0 + 256:tc0 + 512], op=ALU.mult), sig=True)
                    last_r2[bank] = r2
                    dcol = (512 * c + 256 * par) if qk == 1 else 256 * c
                    DSTR = KRT if qk == 1 else QRT
                    P.wait("vector", r2)
                    r3 = v_op(lambda e, dcol=dcol, a0=a0, DSTR=DSTR: e.tensor_add(
                        DSTR[:, dcol:dcol + 256], AVE[:, a0:a0 + 256],
                        AVE[:, a0 + 256:a0 + 512]), sig=True)
                    last_r3 = r3
            if l == 0 and w == 1:
                tap("qrt_w1", lambda: QRT[:], [128, 2048], BF16)
                tap("krt_w1", lambda: KRT[:], [128, 4096], BF16)
                tap("vw_w1", lambda: VW[:], [128, 4096], BF16)
                tap("qn_w1", lambda: QN[:], [128, 2048], BF16)
            P.barrier()
            prev_mask = [None]
            tstate = {"ev": [None, None], "mm": [None, None]}
            for sub in range(2):
                first_avp = [True, True]
                last_mm = None
                for hg in range(4):
                    last_mm = emit_attn_sub(l, w, sub, hg, first_avp, prev_mask,
                                            tstate)
                P.wait("vector", last_mm)
                e1 = v_op(lambda e: e.tensor_copy(AVE[:, 0:512], PS[4][:]),
                          sig=True)
                e2 = v_op(lambda e: e.tensor_copy(AVE[:, 512:1024], PS[5][:]),
                          sig=True)
                P.wait("vector", e1)
                P.wait("vector", e2)
                P.wait("tensor", e2)
                for c4 in range(8):
                    p = c4 % 2
                    tb = 6 + p
                    if tstate["ev"][p] is not None:
                        P.wait("tensor", tstate["ev"][p])
                    tt = t_op(lambda e, c4=c4, tb=tb: e.transpose(
                        psbf(tb)[:, 0:128], AVE[:, 128 * c4:128 * c4 + 128],
                        IDENT[:]), sig=True)
                    P.wait("vector", tt)
                    e3 = v_op(lambda e, c4=c4, sub=sub, tb=tb: e.tensor_copy(
                        AOW[:, 256 * c4 + 128 * sub:256 * c4 + 128 * sub + 128],
                        psbf(tb)[:, 0:128]), sig=True)
                    tstate["ev"][p] = e3
            if l == 0 and w == 1:
                tap("aow_w1", lambda: AOW[:], [128, 2048], BF16)
            P.barrier()
            last_rv = {6: None, 7: None}
            for oc in range(8):
                wb = 6 + oc % 2
                if last_rv[wb] is not None:
                    P.wait("tensor", last_rv[wb])
                for ic in range(8):
                    tok = t_op(lambda e, oc=oc, ic=ic, wb=wb: e.matmul(
                        PS[wb][:, 0:256],
                        WOsl[:, 1024 * ic + 128 * oc:1024 * ic + 128 * oc + 128],
                        AOW[:, 256 * ic:256 * ic + 256],
                        start=(ic == 0), stop=(ic == 7)), sig=(ic == 7))
                P.wait("vector", tok)
                rv = v_op(lambda e, oc=oc, t0=t0, wb=wb: e.tensor_add(
                    XI[oc][:, t0:t0 + 256], XI[oc][:, t0:t0 + 256],
                    PS[wb][:, 0:256]), sig=True)
                last_rv[wb] = rv
            P.barrier()

        def emit_mlp_group(l, g):
            MSCR = XNs.tile[:, 0:2048]
            lnscr2 = MSCR[:, 0:512]
            def dstfn(c, tg):
                return XN2[:, 1024 * c + 512 * (tg - 2 * g):
                           1024 * c + 512 * (tg - 2 * g) + 512]
            for tg in (2 * g, 2 * g + 1):
                emit_ln_one_tg(tg, True, LNP[:, 0:8], LNP[:, 8:16], dstfn, lnscr2)
            P.barrier()
            if l == 0 and g == 0:
                tap("xn2_g0", lambda: XN2[:], [128, 8192], BF16)
            w1ring = [KBS[:, 6144 + 1024 * i:6144 + 1024 * (i + 1)]
                      for i in range(4)]
            w2ring = [KBS[:, 10240 + 2816 * i:10240 + 2816 * (i + 1)]
                      for i in range(2)]
            def hsrc_ap(cc, th):
                if cc < 16:
                    return QB[:, 1024 * cc + 512 * th:1024 * cc + 512 * th + 512]
                return KBS[:, 1024 * (cc - 16) + 512 * th:
                           1024 * (cc - 16) + 512 * th + 512]
            def load_w1(cc):
                slot = cc % 2
                tA = dmaG(lambda e, cc=cc, slot=slot: e.dma_start(
                    out=w1ring[2 * slot].rearrange("p (c n) -> p c n", c=8),
                    in_=WD["w1"][l, cc, 0].transpose([1, 0, 2])))
                tG = dmaG(lambda e, cc=cc, slot=slot: e.dma_start(
                    out=w1ring[2 * slot + 1].rearrange("p (c n) -> p c n", c=8),
                    in_=WD["w1"][l, cc, 1].transpose([1, 0, 2])))
                return (tA, tG)
            w1tok = {0: load_w1(0)}
            prev_cons = None
            ABANKS = (6, 2)
            GBANKS = (7, 3)
            GSLOT = (slice(512, 1024), slice(1536, 2048))
            last_hv = {0: None, 1: None}
            for cc in range(22):
                slot = cc % 2
                tA, tG = w1tok[cc]
                P.wait("tensor", tA)
                P.wait("tensor", tG)
                for th in range(2):
                    par = th
                    ab, gb = ABANKS[par], GBANKS[par]
                    if last_hv[par] is not None:
                        P.wait("tensor", last_hv[par])
                    for ag, ps_i in ((0, ab), (1, gb)):
                        ring = w1ring[2 * slot + ag]
                        for c in range(NCH):
                            tok = t_op(lambda e, c=c, ring=ring, ps_i=ps_i, th=th:
                                       e.matmul(
                                PS[ps_i][:, 0:512],
                                ring[:, 128 * c:128 * c + 128],
                                XN2[:, 1024 * c + 512 * th:
                                    1024 * c + 512 * th + 512],
                                start=(c == 0), stop=(c == NCH - 1)),
                                sig=(c == NCH - 1))
                        if ag == 0:
                            atok = tok
                        else:
                            gtok = tok
                    P.wait("scalar", gtok)
                    if last_hv[par] is not None:
                        P.wait("scalar", last_hv[par])
                    gl = s_op(lambda e, cc=cc, gb=gb, gs=GSLOT[par]: e.activation(
                        out=MSCR[:, gs], in_=PS[gb][:, 0:512],
                        func=AF.Gelu, bias=B1G[:, cc:cc + 1]), sig=True)
                    P.wait("vector", gl)
                    P.wait("vector", atok)
                    hv = v_op(lambda e, cc=cc, th=th, ab=ab, gs=GSLOT[par]:
                              e.scalar_tensor_tensor(
                        hsrc_ap(cc, th), PS[ab][:, 0:512], B1A[:, cc:cc + 1],
                        MSCR[:, gs], op0=ALU.add, op1=ALU.mult), sig=True)
                    last_hv[par] = hv
                    prev_cons = atok
                if cc + 1 < 22:
                    P.wait("gpsimd", prev_cons)
                    w1tok[cc + 1] = load_w1(cc + 1)
            P.wait("tensor", last_hv[0])
            P.wait("tensor", last_hv[1])
            P.barrier()
            def load_w2(o):
                slot = o % 2
                return dmaG(lambda e, o=o, slot=slot: e.dma_start(
                    out=w2ring[slot].rearrange("p (c n) -> p c n", c=22),
                    in_=WD["w2"][l, o].transpose([1, 0, 2])))
            w2tok = {0: load_w2(0)}
            prev_w2 = None
            last_rc = {0: None, 1: None}
            W2SLOT = (slice(1024, 1536), slice(0, 512))
            for o in range(8):
                slot = o % 2
                P.wait("tensor", w2tok[o])
                for th in range(2):
                    par = th
                    wb = 4 + par
                    tcol = 1 + 1024 * g + 512 * th
                    if last_rc[par] is not None:
                        P.wait("tensor", last_rc[par])
                    for cc in range(22):
                        tok = t_op(lambda e, cc=cc, slot=slot, th=th, wb=wb: e.matmul(
                            PS[wb][:, 0:512],
                            w2ring[slot][:, 128 * cc:128 * cc + 128],
                            hsrc_ap(cc, th), start=(cc == 0), stop=(cc == 21)),
                            sig=(cc == 21))
                    P.wait("vector", tok)
                    rv = v_op(lambda e, o=o, tcol=tcol, wb=wb, ws=W2SLOT[par]:
                              e.scalar_tensor_tensor(
                        MSCR[:, ws], PS[wb][:, 0:512], B2T[:, o:o + 1],
                        XI[o][:, tcol:tcol + 512], op0=ALU.add, op1=ALU.add),
                        sig=True)
                    P.wait("vector", rv)
                    rc = v_op(lambda e, o=o, tcol=tcol, ws=W2SLOT[par]: e.tensor_copy(
                        XI[o][:, tcol:tcol + 512], MSCR[:, ws]), sig=True)
                    last_rc[par] = rc
                    prev_w2 = tok
                if o + 1 < 8:
                    P.wait("gpsimd", prev_w2)
                    w2tok[o + 1] = load_w2(o + 1)
            P.wait("tensor", last_rc[0])
            P.wait("tensor", last_rc[1])
            P.barrier()

        ptoks = []
        for c in range(NCH):
            ptoks.append(dmaG(lambda e, c=c: e.dma_start(
                out=X2D[c], in_=XI[c][:, 1:2049])))
        for t in ptoks:
            P.wait("vector", t)
        lnscr = AVE[:, 0:512]
        for tg in range(4):
            emit_ln_one_tg(tg, False, None, None,
                           lambda c, tg: XI[c][:, 1 + 512 * tg:1 + 512 * tg + 512],
                           lnscr)
        P.barrier()
        for c in range(NCH):
            tap("xinner_c%d_b%d" % (c, b),
                lambda c=c: XI[c][:, 1:2049], [128, 2048], BF16)
        wtoks = load_layer_weights(0)
        for t in wtoks:
            for en in ("tensor", "vector", "scalar"):
                P.wait(en, t)
        for l in range(n_layers):
            for w in range(8):
                emit_window(l, w)
            for c in range(NCH):
                tap("attn_l%d_c%d_b%d" % (l, c, b),
                    lambda c=c: XI[c][:, 1:2049], [128, 2048], BF16)
            for g in range(2):
                emit_mlp_group(l, g)
            if l + 1 < n_layers:
                wtoks = load_layer_weights(l + 1)
                for t in wtoks:
                    for en in ("tensor", "vector", "scalar"):
                        P.wait(en, t)
            tap("ffn_l%d_b%d" % (l, b), lambda: XI[0][:, 1:2049], [128, 2048], BF16)
        WSL.free("WQKV")
        XNs.free("WO"); XNs.free("XN2")
        TBL.free("tblq")
        P.barrier()

    def phase_out(b):
        oscr = XNs.alloc("oscr", 2048, F32)
        x2t = XNs.alloc("x2t", 512, F32)
        prev = None
        prev_tt = None
        for j in range(16):
            for c in range(NCH):
                if prev is not None:
                    P.wait("sync", prev)
                ld = dmaS(lambda e, c=c, j=j: e.dma_start(
                    out=x2t[:, 0:128], in_=X2D[c][:, 128 * j:128 * j + 128]))
                P.wait("vector", ld)
                if prev_tt is not None:
                    P.wait("vector", prev_tt)
                ad = v_op(lambda e, c=c, j=j: e.tensor_add(
                    x2t[:, 128:256], x2t[:, 0:128],
                    XI[c][:, 1 + 128 * j:1 + 128 * j + 128]), sig=True)
                prev = ad
                P.wait("tensor", ad)
                tt = t_op(lambda e: e.transpose(
                    PS[5][:, 0:128], x2t[:, 128:256], CS["IDF32"][:]), sig=True)
                prev_tt = tt
                P.wait("vector", tt)
                ev = v_op(lambda e, c=c: e.tensor_copy(
                    oscr[:, 128 * c:128 * c + 128], PS[5][:, 0:128]), sig=True)
                P.wait("tensor", ev)
                P.wait("sync", ev)
            dd = dmaS(lambda e, b=b, j=j: e.dma_start(
                out=OUT[b, 128 * j:128 * j + 128, :], in_=oscr[:, 0:1024]))
            P.wait("vector", dd)
        XNs.free("oscr"); XNs.free("x2t")
        P.barrier()

    xn_slab_v = XNs.alloc("xn", 16384)
    xn_v = [xn_slab_v[:, 2048 * c:2048 * (c + 1)] for c in range(NCH)]
    XNs.free("xn")

    phase_delta()
    tap("dcol", lambda: DCOL[:], [128, 2], F32)
    for b in range(B_CORE):
        XNs.alloc("xn", 16384)
        phase_ln1(b, xn_v)
        tap("xn0_b%d" % b, lambda: xn_v[0][:], [128, 2048], BF16)
        tap("xiraw0_b%d" % b, lambda: XI[0][:, 1:2049], [128, 2048], BF16)
        H = phase_mla_proj(b, xn_v)
        XNs.free("xn")
        phase_mla_attend(b, H)
        tap("xi0_b%d" % b, lambda: XI[0][:, 1:2049], [128, 2048], BF16)
        phase_ffn(b, n_layers=n_ffn_layers)
        phase_out(b)
    fin = P.sig_last("sync")
    P.wait("sync", fin)

    with nc.Block() as block:
        P.run(block, sems)
    B.finalize()
    return B, TAPS


def _in_maps(inputs, consts=None):
    consts = consts or _host_consts()
    lay = _host_weight_layouts(inputs)
    x = np.ascontiguousarray(np.asarray(inputs["L0_attn_x"], np.float32))
    maps = []
    for k in range(8):
        m = {"x": x[2 * k:2 * k + 2]}
        for key, v in lay.items():
            m[key] = v
        for key, v in consts.items():
            if key in CONST_SPECS:
                m["c_" + key] = v
            else:
                m["t_" + key] = v
        maps.append(m)
    return maps


def kernel(**inputs):
    Bobj, _ = build_program()
    res = run_bass_kernel_spmd(Bobj.nc, _in_maps(inputs), core_ids=list(range(8)))
    return np.concatenate([r["o"] for r in res.results], axis=0)


# version 05 (final)
# speedup vs baseline: 1.1829x; 1.1829x over previous
import numpy as np
import ml_dtypes
import concourse.bass as bass
import concourse.mybir as mybir
from concourse.bass import IndirectOffsetOnAxis
from concourse.bass_utils import run_bass_kernel_spmd

F32 = mybir.dt.float32
BF16 = mybir.dt.bfloat16
U32 = mybir.dt.uint32
AF = mybir.ActivationFunctionType
ALU = mybir.AluOpType
AX = mybir.AxisListType
BF = ml_dtypes.bfloat16

B_CORE = 2
L = 2048
NCH = 8
TCOL = 2056
NEG = -3.0e38
NEGM = -1.0e30
S96 = float((64 + 32) ** (-0.5))

def _host_consts():
    c = {}
    t = np.arange(L, dtype=np.float64)
    theta = 1.0 / 10000.0 ** (2.0 * np.arange(16, dtype=np.float64) / 32.0)
    fr32 = np.tile(np.concatenate([theta, theta]), 4)
    c["TQS_C"] = np.cos(t[None, :] * fr32[:, None]).astype(BF)
    c["TQS_S"] = np.sin(t[None, :] * fr32[:, None]).astype(BF)
    ab = (2.0 * t - 63.0)[None, :] * fr32[:, None]
    c["TQB_C"] = np.cos(ab).astype(BF)
    c["TQB_S"] = np.sin(ab).astype(BF)

    P32 = np.zeros((128, 128), np.float32)
    for blk in range(4):
        for j in range(16):
            P32[32 * blk + j + 16, 32 * blk + j] = -1.0
            P32[32 * blk + j, 32 * blk + j + 16] = 1.0
    c["P_ROT32"] = P32.astype(BF)
    P64 = np.zeros((128, 128), np.float32)
    for blk in range(2):
        for j in range(32):
            P64[64 * blk + j + 32, 64 * blk + j] = -1.0
            P64[64 * blk + j, 64 * blk + j + 32] = 1.0
    c["P_ROT64"] = P64.astype(BF)

    halfa = np.arange(0, 64, 2, dtype=np.float64)
    inv_freq = 1.0 / 10000.0 ** (halfa / 64.0)
    fr64 = np.tile(np.concatenate([inv_freq, inv_freq]), 2)
    bs64 = np.tile(np.concatenate([(halfa + 25.6) / 89.6] * 2), 2)
    ang = t[None, :] * fr64[:, None]
    scl = bs64[:, None] ** ((t - L // 2) / 128.0)[None, :]
    c["TF_QC"] = (np.cos(ang) * scl).astype(BF)
    c["TF_QS"] = (np.sin(ang) * scl).astype(BF)
    c["TF_KC"] = (np.cos(ang) / scl).astype(BF)
    c["TF_KS"] = (np.sin(ang) / scl).astype(BF)

    r = np.arange(128)[:, None]
    cc = np.arange(256)[None, :]
    c["M_BAND"] = np.where((cc - r >= 65) & (cc - r <= 128), 0.0, NEG).astype(np.float32)
    p0 = cc - 128
    c["M_BAND0"] = np.where((p0 >= 0) & (r >= 64) & (p0 <= r) & (p0 >= r - 63),
                            0.0, NEG).astype(np.float32)
    c["M_FUT0"] = np.where((p0 >= 0) & (p0 <= 95) &
                           ((r <= 63) | ((r <= 94) & (p0 > r))),
                           0.0, NEG).astype(np.float32)
    c128 = np.arange(128)[None, :]
    c["M_TRI"] = np.where(c128 <= r, 0.0, NEGM).astype(np.float32)
    c["MF0"] = np.where(c128 >= r, 0.0, NEG).astype(np.float32)
    c["MF2"] = np.where(c128 <= r, 0.0, NEG).astype(np.float32)

    slot = 64.0 + np.arange(32, dtype=np.float64)
    a_sl = slot[:, None] * theta[None, :]
    cj, sj = np.cos(a_sl), np.sin(a_sl)
    T1 = np.concatenate([cj, cj], axis=1).reshape(-1).astype(np.float32)
    T2 = np.concatenate([-sj, sj], axis=1).reshape(-1).astype(np.float32)
    c["SLOT_T1"] = np.broadcast_to(T1.astype(BF), (128, 1024)).copy()
    c["SLOT_T2"] = np.broadcast_to(T2.astype(BF), (128, 1024)).copy()

    c["IDENT"] = np.eye(128, dtype=np.float32).astype(BF)
    c["IDF32"] = np.eye(128, dtype=np.float32)
    r1 = np.arange(128)[:, None]
    c1 = np.arange(384)[None, :]
    m = np.zeros((128, 384), np.float32)
    m[:, 0:128][~(c1[:, 0:128] >= r1)] = NEG
    m[:, 256:384][~(c1[:, 0:128] <= r1)] = NEG
    c["M384"] = m
    m0 = np.full((128, 384), NEG, np.float32)
    m0[:, 256:384] = np.where(c1[:, 0:128] <= r1, 0.0, NEG)
    c["M384_W0S0"] = m0
    m1 = np.full((128, 384), NEG, np.float32)
    m1[:, 128:256] = 0.0
    m1[:, 256:384] = np.where(c1[:, 0:128] <= r1, 0.0, NEG)
    c["M384_W0S1"] = m1
    ind = np.zeros((128, 2), np.float32)
    ind[0:64, 0] = 1.0
    ind[64:128, 1] = 1.0
    c["IND2"] = ind.astype(BF)
    c["IND2T"] = np.ascontiguousarray(ind.T).astype(BF)
    return c

CONST_SPECS = {
    "P_ROT32": ([128, 128], BF16), "P_ROT64": ([128, 128], BF16),
    "M_BAND": ([128, 256], F32), "M_BAND0": ([128, 256], F32),
    "M_FUT0": ([128, 256], F32), "M_TRI": ([128, 128], F32),
    "MF0": ([128, 128], F32), "MF2": ([128, 128], F32),
    "SLOT_T1": ([128, 1024], BF16), "SLOT_T2": ([128, 1024], BF16),
    "IDENT": ([128, 128], BF16), "IDF32": ([128, 128], F32),
    "IND2": ([128, 2], BF16), "IND2T": ([2, 128], BF16),
    "M384": ([128, 384], F32), "M384_W0S0": ([128, 384], F32),
    "M384_W0S1": ([128, 384], F32),
}
TABLE_NAMES = ["TQS_C", "TQS_S", "TQB_C", "TQB_S",
               "TF_QC", "TF_QS", "TF_KC", "TF_KS"]


def _host_weight_layouts(inputs):
    m = {}
    def chunked(w):
        din, dout = w.shape
        return np.ascontiguousarray(w.reshape(din // 128, 128, dout))
    for k in ("W_dkv", "W_uk", "W_uv", "W_dq", "W_uq", "W_qr", "W_kr",
              "W_out", "idx_wq", "idx_wk"):
        m["mla_" + k] = chunked(np.asarray(inputs["L0_attn_" + k], np.float32))
    m["mla_raw_delta"] = np.ascontiguousarray(
        np.asarray(inputs["L0_attn_raw_delta"], np.float32).reshape(1, 16))
    m["mla_idx_w"] = np.ascontiguousarray(np.broadcast_to(
        np.asarray(inputs["L0_attn_idx_w"], np.float32)[None, :], (128, 4)))

    qkv = np.asarray(inputs["L0_ffn_Wqkv"], np.float32)
    m["ffn_wqkv"] = np.ascontiguousarray(qkv.reshape(6, 8, 128, 3072))
    wo = np.asarray(inputs["L0_ffn_Wo"], np.float32)
    m["ffn_wo"] = np.ascontiguousarray(wo.reshape(6, 8, 128, 1024))
    w1 = np.asarray(inputs["L0_ffn_W1"], np.float32)
    w1p = np.zeros((6, 1024, 2, 22, 128), np.float32)
    w1p[:, :, 0, :, :] = np.pad(w1[:, :, :2730], ((0, 0), (0, 0), (0, 86))).reshape(6, 1024, 22, 128)
    w1p[:, :, 1, :, :] = np.pad(w1[:, :, 2730:], ((0, 0), (0, 0), (0, 86))).reshape(6, 1024, 22, 128)
    m["ffn_w1"] = np.ascontiguousarray(
        w1p.reshape(6, 8, 128, 2, 22, 128).transpose(0, 4, 3, 1, 2, 5))
    w2 = np.asarray(inputs["L0_ffn_W2"], np.float32)
    w2p = np.pad(w2, ((0, 0), (0, 86), (0, 0))).reshape(6, 22, 128, 8, 128)
    m["ffn_w2"] = np.ascontiguousarray(w2p.transpose(0, 3, 1, 2, 4))
    b1 = np.asarray(inputs["L0_ffn_b1"], np.float32)
    m["ffn_b1a"] = np.ascontiguousarray(
        np.pad(b1[:, :2730], ((0, 0), (0, 86))).reshape(6, 22, 128).transpose(0, 2, 1))
    m["ffn_b1g"] = np.ascontiguousarray(
        np.pad(b1[:, 2730:], ((0, 0), (0, 86))).reshape(6, 22, 128).transpose(0, 2, 1))
    for k, nm in (("b2", "b2"), ("ln_g", "lng"), ("ln_b", "lnb")):
        v = np.asarray(inputs["L0_ffn_" + k], np.float32)
        m["ffn_" + nm] = np.ascontiguousarray(v.reshape(6, 8, 128).transpose(0, 2, 1))
    for k, nm in (("q_scale", "qs"), ("k_scale", "ks")):
        v = np.asarray(inputs["L0_ffn_" + k], np.float32)
        m["ffn_" + nm] = np.ascontiguousarray(np.tile(v, (1, 2)).reshape(6, 128, 1))
    return m


ENGS = ("tensor", "vector", "scalar", "gpsimd", "sync")

class Plan:
    def __init__(self):
        self.ops = {e: [] for e in ENGS}
        self.sig_count = {e: 0 for e in ENGS}
        self.dma_count = {"dmaS": 0, "dmaG": 0, "dmaC": 0}
        self.hwm = {e: {} for e in ENGS}
        self._open_run = {"dmaS": None, "dmaG": None, "dmaC": None}
        self._q_of = {"sync": "dmaS", "gpsimd": "dmaG", "scalar": "dmaC"}

    def op(self, e, fn, sig=False):
        tok = None
        if sig:
            self.sig_count[e] += 1
            tok = (e, self.sig_count[e])
        self.ops[e].append(["op", fn, tok])
        return tok

    def _close_run(self, q):
        box = self._open_run[q]
        if box is not None:
            box["v"] = self.dma_count[q]
            self._open_run[q] = None

    def dma(self, e, fn):
        q = self._q_of[e]
        if self._open_run[q] is None:
            if self.dma_count[q] > 0:
                self.ops[e].append(["wait", (q, self.dma_count[q]), None])
            self._open_run[q] = {"v": None}
        self.dma_count[q] += 16
        tok = (q, self._open_run[q])
        self.ops[e].append(["dma", fn, (q, self.dma_count[q])])
        return tok

    def close_all_runs(self):
        for q in self._open_run:
            self._close_run(q)

    def wait(self, e, tok):
        if tok is None:
            return
        s, v = tok
        if e in self._q_of:
            self._close_run(self._q_of[e])
        if isinstance(v, int) and self.hwm[e].get(s, 0) >= v:
            return
        if isinstance(v, int):
            self.hwm[e][s] = v
        self.ops[e].append(["wait", (s, v), None])

    def sig_last(self, e):
        for ent in reversed(self.ops[e]):
            if ent[0] in ("op", "dma"):
                if ent[2] is not None:
                    return ent[2]
                self.sig_count[e] += 1
                ent[2] = (e, self.sig_count[e])
                return ent[2]
        return None

    def barrier(self, engs=ENGS):
        toks = []
        for e in engs:
            t = self.sig_last(e)
            if t is None:
                t = self.op(e, lambda eng: eng.nop(), sig=True)
            toks.append(t)
        for e in engs:
            for t in toks:
                self.wait(e, t)

    def run(self, block, sems):
        self.close_all_runs()
        def mk(e):
            def body(eng):
                hwm = {}
                for kind, payload, tok in self.ops[e]:
                    if kind == "wait":
                        s, v = payload
                        if not isinstance(v, int):
                            v = v["v"]
                            assert v is not None
                        if hwm.get(s, 0) >= v:
                            continue
                        hwm[s] = v
                        eng.wait_ge(sems[s], v)
                    else:
                        ins = payload(eng)
                        if tok is not None:
                            s, v = tok
                            ins.then_inc(sems[s], 16 if kind == "dma" else 1)
            return body
        block.tensor(mk("tensor"))
        block.vector(mk("vector"))
        block.scalar(mk("scalar"))
        block.gpsimd(mk("gpsimd"))
        block.sync(mk("sync"))


class Slab:
    def __init__(self, tile, ncols):
        self.tile = tile
        self.n = ncols
        self.free_segs = [(0, ncols)]
        self.named = {}

    def alloc(self, name, cols, dtype=BF16, top=False):
        w = cols if dtype == BF16 else 2 * cols
        idxs = range(len(self.free_segs) - 1, -1, -1) if top else range(len(self.free_segs))
        for i in idxs:
            o, ln = self.free_segs[i]
            if ln >= w:
                if top:
                    self.free_segs[i] = (o, ln - w)
                    o = o + ln - w
                else:
                    self.free_segs[i] = (o + w, ln - w)
                if self.free_segs[i][1] == 0:
                    self.free_segs.pop(i)
                self.named[name] = (o, w, dtype, cols)
                return self.view(name)
        raise RuntimeError(f"slab OOM for {name} ({w} cols); free={self.free_segs}")

    def view(self, name):
        o, w, dtype, cols = self.named[name]
        ap = self.tile[:, o:o + w]
        if dtype != BF16:
            ap = ap.bitcast(dtype)
        return ap

    def raw(self, name):
        o, w, _, _ = self.named[name]
        return self.tile[:, o:o + w]

    def free(self, name):
        o, w, _, _ = self.named.pop(name)
        segs = self.free_segs + [(o, w)]
        segs.sort()
        merged = []
        for s in segs:
            if merged and merged[-1][0] + merged[-1][1] == s[0]:
                merged[-1] = (merged[-1][0], merged[-1][1] + s[1])
            else:
                merged.append(list(s) if False else (s[0], s[1]))
        self.free_segs = [(a, b) for (a, b) in merged]


class Builder:
    def __init__(self, debug_taps=()):
        self.nc = bass.Bass("TRN2", target_bir_lowering=False)
        self.debug_taps = list(debug_taps)
        self.taps = []
        self._ctx = []
        self.P = Plan()

    def sb(self, name, shape, dt):
        h = self.nc.sbuf_tensor(name, list(shape), dt)
        t = h.__enter__()
        self._ctx.append(h)
        return t

    def ps_t(self, name, shape, dt=F32):
        h = self.nc.psum_tensor(name, list(shape), dt)
        t = h.__enter__()
        self._ctx.append(h)
        return t

    def sem(self, name):
        h = self.nc.semaphore(name)
        s = h.__enter__()
        self._ctx.append(h)
        return s

    def dram(self, name, shape, dt, kind=None):
        if kind:
            return self.nc.dram_tensor(name, list(shape), dt, kind=kind).ap()
        return self.nc.dram_tensor(name, list(shape), dt).ap()

    def finalize(self):
        for h in reversed(self._ctx):
            h.__exit__(None, None, None)


def build_program(debug_taps=(), n_ffn_layers=6):
    B = Builder(debug_taps)
    nc, P = B.nc, B.P

    XH = B.dram("x", [B_CORE, L, 1024], F32, kind="ExternalInput")
    WD = {}
    for k, s in (("W_dkv", [8, 128, 256]), ("W_uk", [2, 128, 512]),
                 ("W_uv", [2, 128, 512]), ("W_dq", [8, 128, 512]),
                 ("W_uq", [4, 128, 512]), ("W_qr", [4, 128, 256]),
                 ("W_kr", [8, 128, 32]), ("W_out", [4, 128, 1024]),
                 ("idx_wq", [8, 128, 256]), ("idx_wk", [8, 128, 64]),
                 ("idx_w", [128, 4]), ("raw_delta", [1, 16])):
        WD[k] = B.dram("mla_" + k, s, F32, kind="ExternalInput")
    for k, s in (("wqkv", [6, 8, 128, 3072]), ("wo", [6, 8, 128, 1024]),
                 ("w1", [6, 22, 2, 8, 128, 128]), ("w2", [6, 8, 22, 128, 128]),
                 ("b1a", [6, 128, 22]), ("b1g", [6, 128, 22]),
                 ("b2", [6, 128, 8]), ("lng", [6, 128, 8]), ("lnb", [6, 128, 8]),
                 ("qs", [6, 128, 1]), ("ks", [6, 128, 1])):
        WD[k] = B.dram("ffn_" + k, s, F32, kind="ExternalInput")
    CD = {k: B.dram("c_" + k, s, d, kind="ExternalInput")
          for k, (s, d) in CONST_SPECS.items()}
    TD = {k: B.dram("t_" + k, [128, 2048], BF16, kind="ExternalInput")
          for k in TABLE_NAMES}
    OUT = B.dram("o", [B_CORE, L, 1024], F32, kind="ExternalOutput")
    KVMU = B.dram("kvmu", [L, 544], BF16)
    VDR = B.dram("vdr", [L, 512], BF16)
    X2D = B.dram("x2d", [NCH, 128, L], F32)

    CS = {k: B.sb("s_" + k, s, d) for k, (s, d) in CONST_SPECS.items()}
    XI = [B.sb(f"xi{c}", [128, TCOL], BF16) for c in range(NCH)]
    XNs = Slab(B.sb("xn_slab", [128, 16384], BF16), 16384)
    TBL = Slab(B.sb("tbl_slab", [128, 4096], BF16), 4096)
    QB = B.sb("qb_slab", [128, 16384], BF16)
    KBS = B.sb("kb_slab", [128, 16384], BF16)
    WSL = Slab(B.sb("w_slab", [128, 24576], BF16), 24576)
    BSt = B.sb("bst", [128, 12], F32)
    MV = B.sb("mv", [128, 2], F32)
    RST = B.sb("rst", [128, 2], F32)
    M8 = B.sb("m8t", [128, 8], F32)
    TOPV = B.sb("topv", [128, 32], F32)
    TOPI = B.sb("topi", [128, 32], U32)
    TOPF = B.sb("topf", [128, 32], F32)
    SMASK = B.sb("smask", [128, 32], F32)
    IDXW = B.sb("idxw", [128, 4], F32)
    ZROW = B.sb("zrow", [128, 16], F32)
    NMX = B.sb("nmx", [128, 16], F32)
    SLG = B.sb("slg", [6, 128, 1], F32) if False else None
    QSC = B.sb("qsc", [128, 2], F32)
    LNP = B.sb("lnp", [128, 16], F32)
    B1A = B.sb("b1at", [128, 22], F32)
    B1G = B.sb("b1gt", [128, 22], F32)
    B2T = B.sb("b2t", [128, 8], F32)
    SROW = B.sb("srow", [1, 512], F32)
    SROW2 = B.sb("srow2", [1, 512], F32)
    SROWB = B.sb("srowb", [1, 1024], BF16)
    ONER = B.sb("oner", [1, 128], BF16)
    ONESB = B.sb("onesb", [128, 2], BF16)
    CCOL = B.sb("ccol", [128, 8], F32)
    DCOL = B.sb("dcol", [128, 2], F32)

    PS = [B.ps_t(f"ps{i}", [128, 512]) for i in range(8)]
    PSB = [p.ap().bitcast(BF16) if False else p for p in PS]

    sems = {k: B.sem("sm_" + k) for k in
            ("tensor", "vector", "scalar", "gpsimd", "sync", "dmaS", "dmaG", "dmaC")}

    dmaS = lambda fn: P.dma("sync", fn)
    dmaG = lambda fn: P.dma("gpsimd", fn)
    t_op = lambda fn, sig=False: P.op("tensor", fn, sig)
    v_op = lambda fn, sig=False: P.op("vector", fn, sig)
    s_op = lambda fn, sig=False: P.op("scalar", fn, sig)
    g_op = lambda fn, sig=False: P.op("gpsimd", fn, sig)

    def psbf(i):
        return PS[i][:].bitcast(BF16)

    TAPS = []
    _tapseen = {}
    def tap(name, ap_fn, shape, dt=F32):
        if name not in B.debug_taps:
            return
        if name in _tapseen:
            return
        _tapseen[name] = 1
        d = B.dram("tap_" + name, shape, dt, kind="ExternalOutput")
        P.barrier()
        dmaS(lambda e, d=d, f=ap_fn: e.dma_start(out=d, in_=f()))
        P.barrier()
        TAPS.append(name)

    itoks = [dmaS(lambda e, a=CS[k], d=CD[k]: e.dma_start(out=a[:], in_=d))
             for k in CONST_SPECS]
    itoks.append(dmaG(lambda e: e.dma_start(out=IDXW[:], in_=WD["idx_w"])))
    for t in itoks:
        for e in ENGS:
            P.wait(e, t)
    v_op(lambda e: e.memset(ONESB[:], 1.0))
    v_op(lambda e: e.memset(CCOL[:, 0:1], 1e-5))
    v_op(lambda e: e.memset(CCOL[:, 1:2], 1.0))
    v_op(lambda e: e.memset(CCOL[:, 2:3], -1.0))
    v_op(lambda e: e.memset(CCOL[:, 3:4], S96))
    v_op(lambda e: e.memset(CCOL[:, 4:5], 8.0))
    v_op(lambda e: e.memset(CCOL[:, 5:6], -0.5))
    v_op(lambda e: e.memset(CCOL[:, 6:7], float(np.pi)))
    v_op(lambda e: e.memset(CCOL[:, 7:8], float(-np.pi / 2)))
    v_op(lambda e: e.memset(ONER[:], 1.0))
    for c in range(NCH):
        v_op(lambda e, a=XI[c]: e.memset(a[:, 0:1], 0.0))
        v_op(lambda e, a=XI[c]: e.memset(a[:, 2049:TCOL], 0.0))
    P.barrier()
    IDENT = CS["IDENT"]

    def proj_fm(lhs_fn, srcs, n_oc, evac, T=2048, tgw=512, banks=(0, 1),
                ocw=128):
        pend = {i: [] for i in banks}
        k = 0
        for oc in range(n_oc):
            for tg in range(T // tgw):
                bk = banks[k % len(banks)]
                k += 1
                for tk in pend[bk]:
                    P.wait("tensor", tk)
                pend[bk] = []
                n = len(srcs)
                for c in range(n):
                    tok = t_op(lambda e, w=lhs_fn(c, oc), s=srcs[c],
                               t0=tg * tgw, tw=tgw, p=PS[bk],
                               st=(c == 0), sp=(c == n - 1), ocw=ocw:
                               e.matmul(p[0:ocw, 0:tw], w, s[:, t0:t0 + tw],
                                        start=st, stop=sp), sig=(c == n - 1))
                P.wait("vector", tok)
                P.wait("scalar", tok)
                pend[bk] = evac(oc, tg * tgw, tgw, bk)
        P.barrier()


    def phase_delta():
        DSCR = XNs.tile[:, 0:256].bitcast(F32)
        ld = dmaS(lambda e: e.dma_start(out=DSCR[0:1, 0:16], in_=WD["raw_delta"]))
        mz = v_op(lambda e: e.memset(DSCR[:, 16:128], 0.0), sig=True)
        P.wait("scalar", ld)
        P.wait("scalar", mz)
        s1 = s_op(lambda e: e.activation(out=DSCR[0:1, 16:32], in_=DSCR[0:1, 0:16],
                                         func=AF.Sigmoid), sig=True)
        P.wait("scalar", s1)
        s2 = s_op(lambda e: e.activation(out=DSCR[0:1, 32:48], in_=DSCR[0:1, 16:32],
                                         func=AF.Sin, scale=CCOL[0:1, 6:7],
                                         bias=CCOL[0:1, 7:8]), sig=True)
        P.wait("scalar", s2)
        P.wait("vector", s2)
        s3 = s_op(lambda e: e.activation(out=DSCR[0:1, 48:64], in_=DSCR[0:1, 32:48],
                                         func=AF.Square), sig=True)
        P.wait("vector", s3)
        v1 = v_op(lambda e: e.tensor_scalar(DSCR[0:1, 64:80], DSCR[0:1, 48:64],
                                            2.0, -1.0, op0=ALU.mult, op1=ALU.add),
                  sig=True)
        v2 = v_op(lambda e: e.tensor_scalar(DSCR[0:1, 80:96], DSCR[0:1, 48:64],
                                            -1.0, 1.0, op0=ALU.mult, op1=ALU.add),
                  sig=True)
        P.wait("scalar", v2)
        s4 = s_op(lambda e: e.activation(out=DSCR[0:1, 96:112], in_=DSCR[0:1, 80:96],
                                         func=AF.Sqrt), sig=True)
        P.wait("vector", s4)
        v3 = v_op(lambda e: e.tensor_mul(DSCR[0:1, 112:128], DSCR[0:1, 96:112],
                                         DSCR[0:1, 32:48]), sig=True)
        P.wait("vector", v3)
        v4 = v_op(lambda e: e.tensor_scalar_mul(DSCR[0:1, 112:128],
                                                DSCR[0:1, 112:128], 2.0), sig=True)
        P.wait("vector", v1)
        P.wait("vector", v4)
        v5 = v_op(lambda e: e.tensor_copy(DSCR[0:1, 0:16], DSCR[0:1, 64:80]), sig=True)
        v6 = v_op(lambda e: e.tensor_copy(DSCR[0:1, 16:32], DSCR[0:1, 112:128]), sig=True)
        P.wait("tensor", v5)
        P.wait("tensor", v6)
        tt = t_op(lambda e: e.transpose(PS[5][:, 0:128], DSCR[:, 0:128],
                                        CS["IDF32"][:]), sig=True)
        P.wait("vector", tt)
        ev = v_op(lambda e: e.tensor_copy(DCOL[0:32, 0:1],
                                          PS[5][0:32, 0:1]), sig=True)
        P.wait("gpsimd", ev)
        g1 = dmaG(lambda e: e.dma_start(out=DCOL[0:16, 1:2], in_=DCOL[16:32, 0:1]))
        P.wait("gpsimd", g1)
        for b0 in (16, 32, 64):
            g2 = dmaG(lambda e, b0=b0: e.dma_start(
                out=DCOL[b0:2 * b0, 0:2], in_=DCOL[0:b0, 0:2]))
            P.wait("gpsimd", g2)
        for en in ENGS:
            P.wait(en, g2)


    def phase_ln1(b, xn_v):
        XTa = TBL.alloc("xt", 2048, F32)
        XNT = QB[:, 0:2048]
        prev_tp = [None, None]
        prev_use = [None, None]
        for j in range(16):
            o = j % 2
            xt = XTa[:, o * 1024:(o + 1) * 1024]
            if prev_use[o] is not None:
                for tk in prev_use[o]:
                    P.wait("sync", tk)
            ld = dmaS(lambda e, b=b, j=j, xt=xt: e.dma_start(
                out=xt, in_=XH[b, 128 * j:128 * j + 128, :]))
            P.wait("vector", ld)
            if prev_tp[o] is not None:
                P.wait("vector", prev_tp[o])
            v1 = v_op(lambda e, xt=xt: e.bn_stats(BSt[:, 0:6], xt[:, 0:512]), sig=True)
            v2 = v_op(lambda e, xt=xt: e.bn_stats(BSt[:, 6:12], xt[:, 512:1024]), sig=True)
            P.wait("vector", v1)
            P.wait("vector", v2)
            v3 = v_op(lambda e: e.bn_aggr(MV[:], BSt[:]), sig=True)
            P.wait("scalar", v3)
            s1 = s_op(lambda e: e.activation(out=RST[:, 0:1], in_=MV[:, 1:2],
                                             func=AF.Sqrt, bias=CCOL[:, 0:1]), sig=True)
            P.wait("vector", s1)
            v4 = v_op(lambda e: e.reciprocal(RST[:, 1:2], RST[:, 0:1]), sig=True)
            v5 = v_op(lambda e: e.tensor_scalar_mul(MV[:, 0:1], MV[:, 0:1], -1.0), sig=True)
            P.wait("vector", v4)
            P.wait("vector", v5)
            xnt = XNT[:, o * 1024:(o + 1) * 1024]
            v6 = v_op(lambda e, xt=xt, xnt=xnt: e.tensor_scalar(
                xnt, xt, MV[:, 0:1], RST[:, 1:2], op0=ALU.add, op1=ALU.mult),
                sig=True)
            P.wait("tensor", v6)
            last_ev = None
            last_tt2 = None
            for c in range(NCH):
                pb = 6 + (c % 2)
                tt = t_op(lambda e, xnt=xnt, c=c, pb=pb: e.transpose(
                    psbf(pb)[:, 0:128], xnt[:, 128 * c:128 * c + 128], IDENT[:]),
                    sig=True)
                P.wait("vector", tt)
                last_ev = v_op(lambda e, c=c, j=j, pb=pb: e.tensor_copy(
                    xn_v[c][:, 128 * j:128 * j + 128], psbf(pb)[:, 0:128]), sig=True)
                P.wait("tensor", last_ev)
                tt2 = t_op(lambda e, xt=xt, c=c: e.transpose(
                    PS[5][:, 0:128], xt[:, 128 * c:128 * c + 128],
                    CS["IDF32"][:]), sig=True)
                P.wait("vector", tt2)
                last_ev = v_op(lambda e, c=c, j=j: e.tensor_copy(
                    XI[c][:, 1 + 128 * j:1 + 128 * j + 128], PS[5][:, 0:128]),
                    sig=True)
                P.wait("tensor", last_ev)
                last_tt2 = tt2
            prev_tp[o] = last_ev
            prev_use[o] = (v6, last_tt2)
        TBL.free("xt")
        P.barrier()

    def phase_mla_proj(b, xn_v):
        segs = [("WOUT", 4096), ("WUV", 1024), ("WDQ", 4096), ("WUQ", 2048),
                ("WQR", 1024), ("WDKV", 2048), ("WUK", 1024), ("WKR", 256)]
        V = {}
        for nm, cols in segs:
            V[nm] = WSL.alloc(nm, cols)
        wmap = {"WOUT": "W_out", "WUV": "W_uv", "WDKV": "W_dkv", "WUK": "W_uk",
                "WKR": "W_kr", "WDQ": "W_dq", "WUQ": "W_uq", "WQR": "W_qr"}
        wt = []
        for nm, _ in segs:
            d0 = WD[wmap[nm]]
            nch_, _, nn = d0.shape
            wt.append(dmaG(lambda e, d=d0, a=V[nm], c=nch_, n=nn: e.dma_start(
                out=a.rearrange("p (c n) -> p c n", c=c),
                in_=d.transpose([1, 0, 2]))))
        for t in wt:
            P.wait("tensor", t)
        P.barrier()

        CKV = WSL.alloc("CKV", 4096)
        ckv = [CKV[:, 0:2048], CKV[:, 2048:4096]]

        def ev_ckv(oc, t0, tw, bk):
            return [v_op(lambda e, oc=oc, t0=t0, tw=tw, p=PS[bk]: e.tensor_copy(
                ckv[oc][:, t0:t0 + tw], p[:, 0:tw]), sig=True)]
        proj_fm(lambda c, oc: V["WDKV"][:, c * 256 + oc * 128:c * 256 + oc * 128 + 128],
                xn_v, 2, ev_ckv)
        tap("ckv0", lambda: ckv[0][:], [128, 2048], BF16)

        for h in range(8):
            v_op(lambda e, h=h: e.memset(KBS[64:128, 2048 * h:2048 * (h + 1)], 0.0))
        def ev_kfm(oc, t0, tw, bk):
            t1 = v_op(lambda e, h=2 * oc, t0=t0, tw=tw, p=PS[bk]: e.tensor_copy(
                KBS[0:64, 2048 * h + t0:2048 * h + t0 + tw], p[0:64, 0:tw]), sig=True)
            t2 = v_op(lambda e, h=2 * oc + 1, t0=t0, tw=tw, p=PS[bk]: e.tensor_copy(
                KBS[0:64, 2048 * h + t0:2048 * h + t0 + tw], p[64:128, 0:tw]), sig=True)
            return [t1, t2]
        proj_fm(lambda c, oc: V["WUK"][:, c * 512 + oc * 128:c * 512 + oc * 128 + 128],
                ckv, 4, ev_kfm)

        STG = WSL.alloc("STG", 1280)
        vtt, ktt, ctt = STG[:, 0:512], STG[:, 512:1024], STG[:, 1024:1280]
        pdma = []
        prev_ev = []
        for j in range(16):
            for t in pdma:
                P.wait("vector", t)
            for t in prev_ev:
                P.wait("tensor", t)
            pdma = []
            toks = []
            for tgt, ps_i, rhs, nsrc, lhs_src in (
                    ("v", 2, V["WUV"], 2, ckv), ("k", 3, V["WUK"], 2, ckv)):
                for c in range(nsrc):
                    tok = t_op(lambda e, c=c, j=j, p=PS[ps_i], rhs=rhs,
                               st=(c == 0), sp=(c == nsrc - 1), ls=lhs_src:
                               e.matmul(p[:], ls[c][:, 128 * j:128 * j + 128],
                                        rhs[:, 512 * c:512 * c + 512],
                                        start=st, stop=sp), sig=(c == nsrc - 1))
                toks.append(tok)
            for t in toks:
                P.wait("vector", t)
            e1 = v_op(lambda e: e.tensor_copy(vtt[:], PS[2][:]), sig=True)
            e2 = v_op(lambda e: e.tensor_copy(ktt[:], PS[3][:]), sig=True)
            prev_ev = [e1, e2]
            P.wait("sync", e1)
            P.wait("sync", e2)
            d1 = dmaS(lambda e, j=j: e.dma_start(
                out=VDR[128 * j:128 * j + 128, :], in_=vtt[:]))
            d2 = dmaS(lambda e, j=j: e.dma_start(
                out=KVMU[128 * j:128 * j + 128, 0:512], in_=ktt[:]))
            pdma = [d1, d2]
        for t in pdma:
            P.wait("vector", t)
        P.barrier()

        tq_c = TBL.alloc("tq_c", 2048)
        tq_s = TBL.alloc("tq_s", 2048)
        lt1 = dmaS(lambda e: e.dma_start(out=tq_c, in_=TD["TQS_C"]))
        lt2 = dmaS(lambda e: e.dma_start(out=tq_s, in_=TD["TQS_S"]))
        MK = WSL.alloc("MK", 2048)
        TMPK = QB[:, 2048:4096]
        TMPK2 = QB[:, 4096:6144]
        v_op(lambda e: e.memset(MK[64:128, :], 0.0))
        for tg in range(4):
            toks = []
            for c in range(NCH):
                tok = t_op(lambda e, c=c, tg=tg: e.matmul(
                    PS[0][0:32, 0:512], V["WKR"][:, 32 * c:32 * c + 32],
                    xn_v[c][:, 512 * tg:512 * tg + 512],
                    start=(c == 0), stop=(c == NCH - 1)), sig=(c == NCH - 1))
            P.wait("scalar", tok)
            P.wait("vector", tok)
            a1 = s_op(lambda e: e.activation(out=PS[1][0:32, 0:512],
                                             in_=PS[0][0:32, 0:512], func=AF.Abs), sig=True)
            P.wait("scalar", a1)
            a2 = s_op(lambda e: e.activation(out=PS[2][0:32, 0:512],
                                             in_=PS[1][0:32, 0:512], func=AF.Exp,
                                             scale=CCOL[0:32, 2:3]), sig=True)
            P.wait("scalar", a2)
            a3 = s_op(lambda e, tg=tg: e.activation(out=QB[0:32, 0:512],
                                                    in_=PS[2][0:32, 0:512], func=AF.Ln,
                                                    bias=CCOL[0:32, 1:2]), sig=True)
            P.wait("vector", a3)
            v1 = v_op(lambda e, tg=tg: e.scalar_tensor_tensor(
                MK[0:32, 512 * tg:512 * tg + 512], PS[0][0:32, 0:512], 0.0,
                QB[0:32, 0:512], op0=ALU.max, op1=ALU.add), sig=True)
            P.wait("tensor", v1)
            P.wait("scalar", v1)
        P.wait("vector", lt1)
        P.wait("vector", lt2)
        P.wait("tensor", lt1)
        mz = v_op(lambda e: e.memset(MK[32:64, :], 0.0), sig=True)
        P.wait("tensor", mz)
        for tg in range(4):
            tok = t_op(lambda e, tg=tg: e.matmul(
                PS[3][:], CS["P_ROT32"][:], MK[:, 512 * tg:512 * tg + 512],
                start=True, stop=True), sig=True)
            P.wait("vector", tok)
            d1 = v_op(lambda e, tg=tg: e.tensor_scalar_mul(
                TMPK[0:32, 512 * tg:512 * tg + 512],
                MK[0:32, 512 * tg:512 * tg + 512], DCOL[0:32, 0:1]), sig=True)
            P.wait("vector", d1)
            d2 = v_op(lambda e, tg=tg: e.scalar_tensor_tensor(
                MK[0:32, 512 * tg:512 * tg + 512], PS[3][0:32, 0:512],
                DCOL[0:32, 1:2], TMPK[0:32, 512 * tg:512 * tg + 512],
                op0=ALU.mult, op1=ALU.add), sig=True)
            P.wait("tensor", d2)
        for tg in range(4):
            tok = t_op(lambda e, tg=tg: e.matmul(
                PS[3][:], CS["P_ROT32"][:], MK[:, 512 * tg:512 * tg + 512],
                start=True, stop=True), sig=True)
            P.wait("vector", tok)
            v1 = v_op(lambda e, tg=tg: e.tensor_tensor(
                TMPK[0:32, 512 * tg:512 * tg + 512], MK[0:32, 512 * tg:512 * tg + 512],
                tq_c[0:32, 512 * tg:512 * tg + 512], op=ALU.mult), sig=True)
            v2 = v_op(lambda e, tg=tg: e.tensor_tensor(
                TMPK2[0:32, 512 * tg:512 * tg + 512],
                PS[3][0:32, 0:512],
                tq_s[0:32, 512 * tg:512 * tg + 512], op=ALU.mult), sig=True)
            P.wait("vector", v1)
            P.wait("vector", v2)
            v3 = v_op(lambda e, tg=tg: e.tensor_tensor(
                KBS[64:96, 512 * tg:512 * tg + 512], TMPK2[0:32, 512 * tg:512 * tg + 512],
                TMPK[0:32, 512 * tg:512 * tg + 512], op=ALU.add), sig=True)
            P.wait("tensor", v3)
        P.wait("vector", v3)
        ctk = []
        for h in range(1, 8):
            ctk.append(v_op(lambda e, h=h: e.tensor_copy(
                KBS[64:96, 2048 * h:2048 * h + 2048], KBS[64:96, 0:2048]), sig=True))
        mut = STG[:, 0:512]
        for j in range(16):
            tt = t_op(lambda e, j=j: e.transpose(
                psbf(5)[:, 0:128], MK[:, 128 * j:128 * j + 128], IDENT[:]), sig=True)
            P.wait("vector", tt)
            ev = v_op(lambda e, j=j: e.tensor_copy(
                mut[:, 0:32], psbf(5)[:, 0:32]), sig=True)
            P.wait("sync", ev)
            P.wait("tensor", ev)
            dd = dmaS(lambda e, j=j: e.dma_start(
                out=KVMU[128 * j:128 * j + 128, 512:544], in_=mut[:, 0:32]))
            P.wait("tensor", dd)
            P.wait("vector", dd)
        for t in ctk:
            P.wait("tensor", t)
        P.barrier()
        tap("kb8_0", lambda: KBS[:, 0:2048], [128, 2048], BF16)
        WSL.free("MK"); WSL.free("WKR"); WSL.free("WDKV"); WSL.free("WUK")
        WSL.free("CKV"); WSL.free("STG")
        TBL.free("tq_c"); TBL.free("tq_s")

        CQ = WSL.alloc("CQ", 8192)
        cq = [CQ[:, 2048 * i:2048 * (i + 1)] for i in range(4)]
        def ev_cq(oc, t0, tw, bk):
            return [v_op(lambda e, oc=oc, t0=t0, tw=tw, p=PS[bk]: e.tensor_copy(
                cq[oc][:, t0:t0 + tw], p[:, 0:tw]), sig=True)]
        proj_fm(lambda c, oc: V["WDQ"][:, c * 512 + oc * 128:c * 512 + oc * 128 + 128],
                xn_v, 4, ev_cq)

        MQ = WSL.alloc("MQ", 4096)
        mq = [MQ[:, 0:2048], MQ[:, 2048:4096]]
        spq2 = TBL.tile[:, 2048:3072]
        spq = TBL.alloc("spq", 1024, F32)
        def ev_qr(oc, t0, tw, bk):
            a1 = s_op(lambda e, p=PS[bk], tw=tw: e.activation(
                out=PS[2][:, 0:tw], in_=p[:, 0:tw], func=AF.Abs), sig=True)
            P.wait("scalar", a1)
            a2 = s_op(lambda e, tw=tw: e.activation(
                out=PS[3][:, 0:tw], in_=PS[2][:, 0:tw], func=AF.Exp,
                scale=CCOL[:, 2:3]), sig=True)
            P.wait("scalar", a2)
            a3 = s_op(lambda e, tw=tw: e.activation(
                out=spq[:, 0:tw], in_=PS[3][:, 0:tw], func=AF.Ln,
                bias=CCOL[:, 1:2]), sig=True)
            P.wait("vector", a3)
            v1 = v_op(lambda e, oc=oc, t0=t0, tw=tw, p=PS[bk]: e.scalar_tensor_tensor(
                mq[oc][:, t0:t0 + tw], p[:, 0:tw], 0.0, spq[:, 0:tw],
                op0=ALU.max, op1=ALU.add), sig=True)
            P.wait("scalar", v1)
            return [v1]
        proj_fm(lambda c, oc: V["WQR"][:, c * 256 + oc * 128:c * 256 + oc * 128 + 128],
                cq, 2, ev_qr, banks=(0, 1))
        WSL.free("WQR")
        TBL.free("spq")
        for ch in range(2):
            for tg in range(4):
                sl = slice(512 * tg, 512 * tg + 512)
                tok = t_op(lambda e, ch=ch, sl=sl: e.matmul(
                    PS[3][:], CS["P_ROT32"][:], mq[ch][:, sl],
                    start=True, stop=True), sig=True)
                P.wait("vector", tok)
                d1 = v_op(lambda e, ch=ch, sl=sl: e.tensor_scalar_mul(
                    spq2[:, 0:512], mq[ch][:, sl], DCOL[:, 0:1]), sig=True)
                P.wait("vector", d1)
                d2 = v_op(lambda e, ch=ch, sl=sl: e.scalar_tensor_tensor(
                    mq[ch][:, sl], PS[3][:], DCOL[:, 1:2], spq2[:, 0:512],
                    op0=ALU.mult, op1=ALU.add), sig=True)
                P.wait("tensor", d2)
                P.wait("vector", d2)
        tap("mq0rot", lambda: mq[0][:], [128, 2048], BF16)

        for h in range(8):
            v_op(lambda e, h=h: e.memset(QB[64:128, 2048 * h:2048 * (h + 1)], 0.0))
        def ev_qc(oc, t0, tw, bk):
            t1 = v_op(lambda e, h=2 * oc, t0=t0, tw=tw, p=PS[bk]: e.tensor_copy(
                QB[0:64, 2048 * h + t0:2048 * h + t0 + tw], p[0:64, 0:tw]), sig=True)
            t2 = v_op(lambda e, h=2 * oc + 1, t0=t0, tw=tw, p=PS[bk]: e.tensor_copy(
                QB[0:64, 2048 * h + t0:2048 * h + t0 + tw], p[64:128, 0:tw]), sig=True)
            return [t1, t2]
        proj_fm(lambda c, oc: V["WUQ"][:, c * 512 + oc * 128:c * 512 + oc * 128 + 128],
                cq, 4, ev_qc)
        WSL.free("CQ"); WSL.free("WDQ"); WSL.free("WUQ")
        tap("qb8_0", lambda: QB[:, 0:2048], [128, 2048], BF16)

        TMPB = WSL.alloc("TMPB", 4096)
        tq_c = TBL.alloc("tq_c", 2048)
        tq_s = TBL.alloc("tq_s", 2048)

        def build_rot(tc_name, ts_name, dst_fn):
            l1 = dmaS(lambda e: e.dma_start(out=tq_c, in_=TD[tc_name]))
            l2 = dmaS(lambda e: e.dma_start(out=tq_s, in_=TD[ts_name]))
            P.wait("vector", l1)
            P.wait("vector", l2)
            for ch in range(2):
                dst = dst_fn(ch)
                for tg in range(4):
                    sl = slice(512 * tg, 512 * tg + 512)
                    tok = t_op(lambda e, ch=ch, sl=sl: e.matmul(
                        PS[3][:], CS["P_ROT32"][:], mq[ch][:, sl],
                        start=True, stop=True), sig=True)
                    P.wait("vector", tok)
                    v1 = v_op(lambda e, ch=ch, sl=sl: e.tensor_tensor(
                        TMPB[:, sl], mq[ch][:, sl], tq_c[:, sl], op=ALU.mult),
                        sig=True)
                    v2 = v_op(lambda e, sl=sl, dst=dst: e.tensor_tensor(
                        dst[:, sl], PS[3][:, 0:512], tq_s[:, sl], op=ALU.mult),
                        sig=True)
                    P.wait("vector", v1)
                    P.wait("vector", v2)
                    v3 = v_op(lambda e, sl=sl, dst=dst: e.tensor_tensor(
                        dst[:, sl], dst[:, sl], TMPB[:, sl], op=ALU.add), sig=True)
                    P.wait("tensor", v3)
            P.barrier()

        QS = WSL.alloc("QS", 4096, top=True)
        build_rot("TQB_C", "TQB_S", lambda ch: QS[:, 2048 * ch:2048 * (ch + 1)])
        for h in range(8):
            src = QS[:, 2048 * (h // 4):2048 * (h // 4) + 2048]
            r0 = 32 * (h % 4)
            dg = dmaG(lambda e, h=h, src=src, r0=r0: e.dma_start(
                out=QB[64:96, 2048 * h:2048 * h + 2048], in_=src[r0:r0 + 32, :]))
            P.wait("tensor", dg)
            P.wait("vector", dg)
        P.barrier()
        build_rot("TQS_C", "TQS_S", lambda ch: QS[:, 2048 * ch:2048 * (ch + 1)])
        WSL.free("MQ"); WSL.free("TMPB")

        TBL.free("tq_c"); TBL.free("tq_s")
        tap("qb8_full_0", lambda: QB[:, 0:2048], [128, 2048], BF16)
        tap("qhs_0", lambda: QS[:, 0:2048], [128, 2048], BF16)

        WIQ = WSL.alloc("WIQ", 2048)
        WIK = WSL.alloc("WIK", 512)
        l1 = dmaG(lambda e: e.dma_start(
            out=WIQ.rearrange("p (c n) -> p c n", c=8),
            in_=WD["idx_wq"].transpose([1, 0, 2])))
        l2 = dmaG(lambda e: e.dma_start(
            out=WIK.rearrange("p (c n) -> p c n", c=8),
            in_=WD["idx_wk"].transpose([1, 0, 2])))
        P.wait("tensor", l1)
        P.wait("tensor", l2)
        QI = WSL.alloc("QI", 4096, top=True)
        qi = [QI[:, 0:2048], QI[:, 2048:4096]]
        def ev_qi(oc, t0, tw, bk):
            return [v_op(lambda e, oc=oc, t0=t0, tw=tw, p=PS[bk]: e.tensor_copy(
                qi[oc][:, t0:t0 + tw], p[:, 0:tw]), sig=True)]
        proj_fm(lambda c, oc: WIQ[:, c * 256 + oc * 128:c * 256 + oc * 128 + 128],
                xn_v, 2, ev_qi)
        KI = WSL.alloc("KI", 2048, top=True)
        def ev_ki(oc, t0, tw, bk):
            t1 = v_op(lambda e, t0=t0, tw=tw, p=PS[bk]: e.tensor_copy(
                KI[0:64, t0:t0 + tw], p[0:64, 0:tw]), sig=True)
            t2 = v_op(lambda e, t0=t0, tw=tw, p=PS[bk]: e.tensor_copy(
                KI[64:128, t0:t0 + tw], p[0:64, 0:tw]), sig=True)
            return [t1, t2]
        proj_fm(lambda c, oc: WIK[:, c * 64:c * 64 + 64], xn_v, 1, ev_ki, ocw=64)
        WSL.free("WIQ"); WSL.free("WIK")
        tap("qi0", lambda: qi[0][:], [128, 2048], BF16)
        tap("qi1", lambda: qi[1][:], [128, 2048], BF16)
        tap("idxw", lambda: IDXW[:], [128, 4], F32)
        tap("ki", lambda: KI[:], [128, 2048], BF16)
        P.barrier()
        return {"QS": QS, "QI": QI, "KI": KI, "V": V}

    def phase_mla_attend(b, H):
        QSv, QIv, KIv = H["QS"], H["QI"], H["KI"]
        V = H["V"]
        GKV = WSL.alloc("GKV", 6400)
        QCTM = WSL.alloc("QCTM", 512)
        QSTMt = WSL.alloc("QSTMt", 256)
        QFUT = WSL.alloc("QFUT", 1024)
        VST = WSL.alloc("VST", 1024)
        ITA = XNs.alloc("itA", 4096)
        ITB = XNs.alloc("itB", 4096)
        IT = [ITA.bitcast(F32), ITB.bitcast(F32)]
        SFUT8 = IT[0]
        AFUT8 = ITB[:, 0:2048]
        SELACC = ITB[:, 2048:4096].bitcast(F32)
        SBND8 = XNs.alloc("sbnd8", 2048, F32)
        ABND8 = XNs.alloc("abnd8", 2048)
        SSEL = XNs.alloc("ssel", 256, F32)
        ASEL8 = XNs.alloc("asel8", 256)
        ATP = XNs.alloc("atp", 512)
        PRD = XNs.alloc("prd", 256)
        PRD8 = XNs.raw("sbnd8")
        AB_RAW = XNs.raw("abnd8")
        P3R = AB_RAW[:, 0:768]
        GMR = ATP[:, 0:256]
        ATM = AB_RAW[:, 0:512]
        AFM = AB_RAW[:, 512:1024]

        v_op(lambda e: e.memset(GKV[:], 0.0))
        for h in range(8):
            v_op(lambda e, h=h: e.memset(QFUT[96:128, 128 * h:128 * h + 128], 0.0))
            v_op(lambda e, h=h: e.tensor_copy(
                QFUT[0:64, 128 * h:128 * h + 128], QB[0:64, 2048 * h:2048 * h + 128]))
            dg = dmaG(lambda e, h=h: e.dma_start(
                out=QFUT[64:96, 128 * h:128 * h + 128],
                in_=QSv[32 * (h % 4):32 * (h % 4) + 32,
                        2048 * (h // 4):2048 * (h // 4) + 128]))
            P.wait("tensor", dg)
        P.barrier()

        def emit_tau(tau):
            q0 = 128 * tau
            RW = 64 + 128 * tau
            cur = IT[0]
            nrc = (RW + 511) // 512
            for rc in range(nrc):
                c0 = 512 * rc
                cw = min(512, RW - c0)
                mtk = []
                for h in range(4):
                    r0 = 64 * (h % 2)
                    mtk.append(t_op(lambda e, h=h, r0=r0, c0=c0, cw=cw, q0=q0: e.matmul(
                        PS[h][0:128, 0:cw],
                        QIv[r0:r0 + 64, 2048 * (h // 2) + q0:2048 * (h // 2) + q0 + 128],
                        KIv[r0:r0 + 64, c0:c0 + cw],
                        start=True, stop=True), sig=True))
                for t in mtk:
                    P.wait("scalar", t)
                rtk = []
                for h in range(4):
                    rtk.append(s_op(lambda e, h=h, cw=cw: e.activation(
                        out=PS[4 + h][:, 0:cw], in_=PS[h][:, 0:cw], func=AF.Relu),
                        sig=True))
                for t in rtk:
                    P.wait("vector", t)
                if tau == 3 and c0 == 0:
                    _dbg = XNs.raw("sbnd8").bitcast(F32)
                    for t in mtk:
                        P.wait("vector", t)
                    dv = v_op(lambda e, cw=cw, _dbg=_dbg: e.tensor_copy(
                        _dbg[:, 0:cw], PS[0][:, 0:cw]), sig=True)
                    P.wait("vector", dv)
                    for t in rtk:
                        P.wait("vector", t)
                    dv2 = v_op(lambda e, cw=cw, _dbg=_dbg: e.tensor_copy(
                        _dbg[:, 512:512 + cw], PS[4][:, 0:cw]), sig=True)
                    P.wait("vector", dv2)
                    tap("dbg_s0_t3", lambda _dbg=_dbg: _dbg, [128, 2048], F32)
                v1 = v_op(lambda e, c0=c0, cw=cw: e.tensor_scalar_mul(
                    cur[:, c0:c0 + cw], PS[4][:, 0:cw], IDXW[:, 0:1]), sig=True)
                P.wait("vector", v1)
                for h in range(1, 4):
                    v1 = v_op(lambda e, h=h, c0=c0, cw=cw: e.scalar_tensor_tensor(
                        cur[:, c0:c0 + cw], PS[4 + h][:, 0:cw], IDXW[:, h:h + 1],
                        cur[:, c0:c0 + cw], op0=ALU.mult, op1=ALU.add), sig=True)
                    P.wait("vector", v1)
                P.wait("tensor", v1)
            if tau == 0:
                vm = v_op(lambda e: e.tensor_tensor(
                    cur[:, 0:64], cur[:, 0:64], CS["M_TRI"][:, 64:128], op=ALU.add),
                    sig=True)
            else:
                vm = v_op(lambda e, RW=RW: e.tensor_tensor(
                    cur[:, RW - 128:RW], cur[:, RW - 128:RW], CS["M_TRI"][:],
                    op=ALU.add), sig=True)
            P.wait("vector", vm)

            if tau == 3:
                _it = IT[0]
                tap("ireg_t3", lambda _it=_it: _it, [128, 2048], F32)
                tap("qi0_t3", lambda: QIv[:, 0:2048], [128, 2048], BF16)
                tap("ki_t3", lambda: KIv[:, 0:2048], [128, 2048], BF16)
            for rd in range(4):
                nxt = IT[(rd + 1) % 2]
                src = IT[rd % 2]
                m1 = v_op(lambda e, src=src, RW=RW: e.max(
                    out=M8[:], in_=src[:, 0:RW]), sig=True)
                P.wait("vector", m1)
                m2 = v_op(lambda e, src=src, rd=rd: e.max_index(
                    out=TOPI[:, 8 * rd:8 * rd + 8].bitcast(U32) if False else TOPI[:, 8 * rd:8 * rd + 8],
                    in_max=M8[:], in_values=src[:, 0:RW]), sig=True)
                m3 = v_op(lambda e, rd=rd: e.tensor_copy(
                    TOPV[:, 8 * rd:8 * rd + 8], M8[:]), sig=True)
                P.wait("vector", m2)
                P.wait("vector", m3)
                if rd < 4 - 1:
                    m4 = v_op(lambda e, src=src, nxt=nxt, RW=RW: e.match_replace(
                        out=nxt[:, 0:RW], in_to_replace=M8[:], in_values=src[:, 0:RW],
                        imm_value=NEGM), sig=True)
                    P.wait("vector", m4)
            k1 = v_op(lambda e: e.tensor_copy(TOPF[:], TOPI[:]), sig=True)
            P.wait("vector", k1)
            k2 = v_op(lambda e: e.tensor_scalar_min(TOPF[:], TOPF[:], 2047.0), sig=True)
            P.wait("vector", k2)
            k3 = v_op(lambda e: e.tensor_copy(TOPI[:], TOPF[:]), sig=True)
            P.wait("vector", k3)
            c2 = v_op(lambda e: e.tensor_scalar(
                SMASK[:], TOPV[:], -1e29, 3.0e38, op0=ALU.is_gt, op1=ALU.bypass),
                sig=True)
            P.wait("vector", c2)
            c3 = v_op(lambda e: e.tensor_scalar(
                SMASK[:], SMASK[:], -1.0, 3.0e38, op0=ALU.add, op1=ALU.mult), sig=True)
            P.wait("gpsimd", c3)

            for h in range(8):
                tt = t_op(lambda e, h=h, q0=q0: e.transpose(
                    psbf(6)[:, 0:128], QB[:, 2048 * h + q0:2048 * h + q0 + 128],
                    IDENT[:]), sig=True)
                P.wait("vector", tt)
                ev = v_op(lambda e, h=h: e.tensor_copy(
                    QCTM[:, 64 * h:64 * h + 64], psbf(6)[:, 0:64]), sig=True)
                P.wait("tensor", ev)
            for ch in range(2):
                tt = t_op(lambda e, ch=ch, q0=q0: e.transpose(
                    psbf(6)[:, 0:128], QSv[:, 2048 * ch + q0:2048 * ch + q0 + 128],
                    IDENT[:]), sig=True)
                P.wait("vector", tt)
                ev = v_op(lambda e, ch=ch: e.tensor_copy(
                    QSTMt[:, 128 * ch:128 * ch + 128], psbf(6)[:, 0:128]), sig=True)
                P.wait("tensor", ev)
            P.barrier()

            G3 = GKV[:, 0:4352].rearrange("p (j e) -> p j e", j=8)
            prev_grp = None
            for g in range(4):
                if prev_grp is not None:
                    P.wait("gpsimd", prev_grp)
                gts = []
                for j in range(8):
                    sj = 8 * g + j
                    gts.append(dmaG(lambda e, sj=sj, j=j: e.indirect_dma_start(
                        out=GKV[:, 544 * j:544 * j + 544], out_offset=None,
                        in_=KVMU,
                        in_offset=IndirectOffsetOnAxis(ap=TOPI[:, sj:sj + 1], axis=0))))
                for t in gts:
                    P.wait("vector", t)
                T1 = CS["SLOT_T1"][:, 256 * g:256 * g + 256].rearrange(
                    "p (j d) -> p j d", j=8)
                T2 = CS["SLOT_T2"][:, 256 * g:256 * g + 256].rearrange(
                    "p (j d) -> p j d", j=8)
                GM3 = GMR[:].rearrange("p (j d) -> p j d", j=8)
                r1 = v_op(lambda e, T1=T1, GM3=GM3: e.tensor_tensor(
                    GM3, G3[:, :, 512:544], T1, op=ALU.mult), sig=True)
                r2 = v_op(lambda e, T2=T2: e.tensor_tensor(
                    PRD[:].rearrange("p (j d) -> p j d", j=8)[:, :, 0:16],
                    G3[:, :, 528:544], T2[:, :, 0:16], op=ALU.mult), sig=True)
                r3 = v_op(lambda e, T2=T2: e.tensor_tensor(
                    PRD[:].rearrange("p (j d) -> p j d", j=8)[:, :, 16:32],
                    G3[:, :, 512:528], T2[:, :, 16:32], op=ALU.mult), sig=True)
                P.wait("vector", r1)
                P.wait("vector", r2)
                P.wait("vector", r3)
                r4 = v_op(lambda e: e.tensor_tensor(
                    GMR[:], GMR[:], PRD[:], op=ALU.add), sig=True)
                P.wait("vector", r4)
                P3 = P3R.rearrange("p (j d) -> p j d", j=8)
                for h in range(8):
                    d1 = v_op(lambda e, h=h: e.tensor_tensor(
                        P3[:, :, 0:64], G3[:, :, 64 * h:64 * h + 64],
                        QCTM[:, 64 * h:64 * h + 64].unsqueeze(1).broadcast_to(
                            [128, 8, 64]),
                        op=ALU.mult), sig=True)
                    d2 = v_op(lambda e, h=h: e.tensor_tensor(
                        P3[:, :, 64:96], GM3,
                        QSTMt[:, 32 * h:32 * h + 32].unsqueeze(1).broadcast_to(
                            [128, 8, 32]),
                        op=ALU.mult), sig=True)
                    P.wait("vector", d1)
                    P.wait("vector", d2)
                    d3 = v_op(lambda e, h=h, g=g: e.tensor_reduce(
                        SSEL[:, 32 * h + 8 * g:32 * h + 8 * g + 8], P3,
                        axis=AX.X, op=ALU.add), sig=True)
                    P.wait("vector", d3)
                prev_grp = d3
            sm = v_op(lambda e: e.tensor_tensor(
                SSEL[:].rearrange("p (h j) -> p h j", h=8),
                SSEL[:].rearrange("p (h j) -> p h j", h=8),
                SMASK[:].unsqueeze(1).broadcast_to([128, 8, 32]),
                op=ALU.add), sig=True)
            P.wait("vector", sm)

            if tau == 3:
                tap("topi_t3", lambda: TOPI[:], [128, 32], U32)
                tap("topv_t3", lambda: TOPV[:], [128, 32], F32)
                tap("ssel_t3", lambda: SSEL[:], [128, 256], F32)
                tap("qctm_t3", lambda: QCTM[:], [128, 512], BF16)
                tap("qstm_t3", lambda: QSTMt[:], [128, 256], BF16)
                tap("gmr_t3", lambda: GMR[:], [128, 256], BF16)
            vl1 = dmaS(lambda e, tau=tau: e.dma_start(
                out=VST[:, 512:1024], in_=VDR[128 * tau:128 * tau + 128, :]))
            if tau > 0:
                vl0 = dmaS(lambda e, tau=tau: e.dma_start(
                    out=VST[:, 0:512], in_=VDR[128 * (tau - 1):128 * tau, :]))
            ws = q0 - 128
            btks = []
            for h in range(8):
                bank, colh = h % 4, 256 * (h // 4)
                if tau == 0:
                    btks.append(t_op(lambda e, h=h, bank=bank, colh=colh, q0=q0: e.matmul(
                        PS[bank][:, colh + 128:colh + 256],
                        QB[:, 2048 * h + q0:2048 * h + q0 + 128],
                        KBS[:, 2048 * h:2048 * h + 128],
                        start=True, stop=True), sig=True))
                    btks.append(t_op(lambda e, h=h: e.matmul(
                        PS[4 + h % 4][:, 256 * (h // 4) + 128:256 * (h // 4) + 256],
                        QFUT[:, 128 * h:128 * h + 128],
                        KBS[:, 2048 * h:2048 * h + 128],
                        start=True, stop=True), sig=True))
                else:
                    btks.append(t_op(lambda e, h=h, bank=bank, colh=colh, ws=ws, q0=q0: e.matmul(
                        PS[bank][:, colh:colh + 256],
                        QB[:, 2048 * h + q0:2048 * h + q0 + 128],
                        KBS[:, 2048 * h + ws:2048 * h + ws + 256],
                        start=True, stop=True), sig=True))
            for t in btks:
                P.wait("vector", t)

            MB = CS["M_BAND0"] if tau == 0 else CS["M_BAND"]
            for h in range(8):
                bank, colh = h % 4, 256 * (h // 4)
                if tau == 0:
                    v_op(lambda e, h=h: e.memset(SBND8[:, 256 * h:256 * h + 128], NEG))
                    v_op(lambda e, h=h, bank=bank, colh=colh, MB=MB: e.tensor_tensor(
                        SBND8[:, 256 * h + 128:256 * h + 256],
                        PS[bank][:, colh + 128:colh + 256], MB[:, 128:256],
                        op=ALU.add), sig=True)
                    v_op(lambda e, h=h: e.memset(SFUT8[:, 256 * h:256 * h + 128], NEG))
                    v_op(lambda e, h=h: e.tensor_tensor(
                        SFUT8[:, 256 * h + 128:256 * h + 256],
                        PS[4 + h % 4][:, 256 * (h // 4) + 128:256 * (h // 4) + 256],
                        CS["M_FUT0"][:, 128:256], op=ALU.add), sig=True)
                else:
                    v_op(lambda e, h=h, bank=bank, colh=colh, MB=MB: e.tensor_tensor(
                        SBND8[:, 256 * h:256 * h + 256],
                        PS[bank][:, colh:colh + 256], MB[:], op=ALU.add), sig=True)
            vv = P.sig_last("vector")
            P.wait("vector", vv)
            n1 = v_op(lambda e: e.tensor_reduce(
                NMX[:, 0:8], SBND8[:].rearrange("p (h c) -> p h c", h=8),
                axis=AX.X, op=ALU.max, negate=True), sig=True)
            n2 = v_op(lambda e: e.tensor_reduce(
                NMX[:, 8:16], SSEL[:].rearrange("p (h j) -> p h j", h=8),
                axis=AX.X, op=ALU.max, negate=True), sig=True)
            P.wait("vector", n1)
            P.wait("vector", n2)
            n3 = v_op(lambda e: e.tensor_tensor(
                NMX[:, 0:8], NMX[:, 0:8], NMX[:, 8:16], op=ALU.min), sig=True)
            if tau == 0:
                P.wait("vector", n3)
                nf = v_op(lambda e: e.tensor_reduce(
                    NMX[:, 8:16], SFUT8[:].rearrange("p (h c) -> p h c", h=8),
                    axis=AX.X, op=ALU.max, negate=True), sig=True)
                P.wait("vector", nf)
                n3 = v_op(lambda e: e.tensor_tensor(
                    NMX[:, 0:8], NMX[:, 0:8], NMX[:, 8:16], op=ALU.min), sig=True)
            P.wait("vector", n3)
            n4 = v_op(lambda e: e.tensor_scalar_mul(NMX[:, 0:8], NMX[:, 0:8], S96),
                      sig=True)
            P.wait("scalar", n4)
            for h in range(8):
                s_op(lambda e, h=h: e.activation(
                    out=ABND8[:, 256 * h:256 * h + 256],
                    in_=SBND8[:, 256 * h:256 * h + 256], func=AF.Exp,
                    bias=NMX[:, h:h + 1], scale=CCOL[:, 3:4],
                    accum_out=ZROW[:, h:h + 1]), sig=True)
                s_op(lambda e, h=h: e.activation(
                    out=ASEL8[:, 32 * h:32 * h + 32],
                    in_=SSEL[:, 32 * h:32 * h + 32], func=AF.Exp,
                    bias=NMX[:, h:h + 1], scale=CCOL[:, 3:4],
                    accum_out=ZROW[:, 8 + h:8 + h + 1]), sig=True)
                if tau == 0:
                    s_op(lambda e, h=h: e.activation(
                        out=AFUT8[:, 256 * h:256 * h + 256],
                        in_=SFUT8[:, 256 * h:256 * h + 256], func=AF.Exp,
                        bias=NMX[:, h:h + 1], scale=CCOL[:, 3:4],
                        accum_out=NMX[:, 8 + h:8 + h + 1]), sig=True)
            sv = P.sig_last("scalar")
            P.wait("vector", sv)
            z1 = v_op(lambda e: e.tensor_tensor(
                ZROW[:, 0:8], ZROW[:, 0:8], ZROW[:, 8:16], op=ALU.add), sig=True)
            P.wait("vector", z1)
            if tau == 0:
                z1 = v_op(lambda e: e.tensor_tensor(
                    ZROW[:, 0:8], ZROW[:, 0:8], NMX[:, 8:16], op=ALU.add), sig=True)
                P.wait("vector", z1)
            z2 = v_op(lambda e: e.reciprocal(ZROW[:, 8:16], ZROW[:, 0:8]), sig=True)
            P.wait("vector", z2)
            for h in range(8):
                v_op(lambda e, h=h: e.tensor_scalar_mul(
                    ABND8[:, 256 * h:256 * h + 256], ABND8[:, 256 * h:256 * h + 256],
                    ZROW[:, 8 + h:8 + h + 1]), sig=True)
                v_op(lambda e, h=h: e.tensor_scalar_mul(
                    ASEL8[:, 32 * h:32 * h + 32], ASEL8[:, 32 * h:32 * h + 32],
                    ZROW[:, 8 + h:8 + h + 1]), sig=True)
                if tau == 0:
                    v_op(lambda e, h=h: e.tensor_scalar_mul(
                        AFUT8[:, 256 * h:256 * h + 256],
                        AFUT8[:, 256 * h:256 * h + 256],
                        ZROW[:, 8 + h:8 + h + 1]), sig=True)
            va = P.sig_last("vector")
            P.wait("tensor", va)
            P.wait("tensor", vl1)
            if tau > 0:
                P.wait("tensor", vl0)

            if tau == 3:
                _sb = XNs.raw("sbnd8").bitcast(F32)
                tap("sbnd_t3", lambda _sb=_sb: _sb, [128, 2048], F32)
                tap("abnd_t3", lambda: ABND8[:], [128, 2048], BF16)
                tap("asel_t3", lambda: ASEL8[:], [128, 256], BF16)
                tap("zrow_t3", lambda: ZROW[:], [128, 16], F32)
            av_items = []
            for h in range(8):
                blks = (1,) if tau == 0 else (0, 1)
                for blk in blks:
                    av_items.append((
                        ABND8[:, 256 * h + 128 * blk:256 * h + 128 * blk + 128],
                        VST[:, 512 * blk + 64 * h:512 * blk + 64 * h + 64], h))
                if tau == 0:
                    av_items.append((
                        AFUT8[:, 256 * h + 128:256 * h + 256],
                        VST[:, 512 + 64 * h:512 + 64 * h + 64], h))
            n_av = len(av_items)
            prev_mm = {0: None, 1: None, 2: None}
            for i, (src, vsl, h) in enumerate(av_items):
                slot = i % 3
                tt = t_op(lambda e, src=src: e.transpose(
                    psbf(6)[:, 0:128], src, IDENT[:]), sig=True)
                P.wait("vector", tt)
                if prev_mm[slot] is not None:
                    P.wait("vector", prev_mm[slot])
                ev = v_op(lambda e, slot=slot: e.tensor_copy(
                    ATP[:, 128 * slot:128 * slot + 128], psbf(6)[:, 0:128]),
                    sig=True)
                P.wait("tensor", ev)
                mm = t_op(lambda e, slot=slot, vsl=vsl, h=h,
                          st=(i == 0), sp=(i == n_av - 1): e.matmul(
                    PS[4][:, 64 * h:64 * h + 64], ATP[:, 128 * slot:128 * slot + 128],
                    vsl, start=st, stop=sp), sig=True)
                prev_mm[slot] = mm
            tv = P.sig_last("tensor")
            P.wait("vector", tv)

            GV3 = GKV[:, 0:4096].rearrange("p (j d) -> p j d", j=8)
            sel_first = [True]
            P.wait("gpsimd", prev_grp)
            for g in range(4):
                if not sel_first[0] or g > 0:
                    P.wait("gpsimd", prev_grp)
                gts = []
                for j in range(8):
                    sj = 8 * g + j
                    gts.append(dmaG(lambda e, sj=sj, j=j: e.indirect_dma_start(
                        out=GKV[:, 512 * j:512 * j + 512], out_offset=None,
                        in_=VDR,
                        in_offset=IndirectOffsetOnAxis(ap=TOPI[:, sj:sj + 1], axis=0))))
                for t in gts:
                    P.wait("vector", t)
                in1 = (ASEL8[:].rearrange("p (h j) -> p h j", h=8)[:, :, 8 * g:8 * g + 8]
                       .transpose([0, 2, 1]).unsqueeze(3)
                       .broadcast_to([128, 8, 8, 64]))
                m1 = v_op(lambda e, in1=in1: e.tensor_tensor(
                    PRD8[:].rearrange("p (j h d) -> p j h d", j=8, h=8),
                    GV3.rearrange("p j (h d) -> p j h d", h=8), in1,
                    op=ALU.mult), sig=True)
                P.wait("vector", m1)
                red_in = PRD8[:].rearrange("p (j m) -> p j m", j=8).transpose([0, 2, 1])
                if sel_first[0]:
                    m2 = v_op(lambda e, red_in=red_in: e.tensor_reduce(
                        SELACC[:, 0:512], red_in, axis=AX.X, op=ALU.add), sig=True)
                    sel_first[0] = False
                else:
                    m2 = v_op(lambda e, red_in=red_in: e.tensor_reduce(
                        SELACC[:, 512:1024], red_in, axis=AX.X, op=ALU.add), sig=True)
                    P.wait("vector", m2)
                    m2 = v_op(lambda e: e.tensor_tensor(
                        SELACC[:, 0:512], SELACC[:, 0:512], SELACC[:, 512:1024],
                        op=ALU.add), sig=True)
                P.wait("vector", m2)
                prev_grp = m1

            av = v_op(lambda e: e.tensor_tensor(
                ATM[:], PS[4][:], SELACC[:, 0:512], op=ALU.add), sig=True)
            P.wait("tensor", av)
            if tau == 3:
                tap("atm_t3", lambda: ATM[:], [128, 512], BF16)
                tap("selacc_t3", lambda: SELACC[:], [128, 1024], F32)
            for c4 in range(4):
                tt = t_op(lambda e, c4=c4: e.transpose(
                    psbf(6)[:, 0:128], ATM[:, 128 * c4:128 * c4 + 128], IDENT[:]),
                    sig=True)
                P.wait("vector", tt)
                ev = v_op(lambda e, c4=c4: e.tensor_copy(
                    AFM[:, 128 * c4:128 * c4 + 128], psbf(6)[:, 0:128]), sig=True)
                P.wait("tensor", ev)
            va = P.sig_last("vector")
            for oc in range(8):
                for c4 in range(4):
                    tok = t_op(lambda e, oc=oc, c4=c4: e.matmul(
                        PS[5][:, 0:128],
                        V["WOUT"][:, 1024 * c4 + 128 * oc:1024 * c4 + 128 * oc + 128],
                        AFM[:, 128 * c4:128 * c4 + 128],
                        start=(c4 == 0), stop=(c4 == 3)), sig=(c4 == 3))
                P.wait("vector", tok)
                rv = v_op(lambda e, oc=oc, q0=q0: e.tensor_tensor(
                    XI[oc][:, 1 + q0:1 + q0 + 128], XI[oc][:, 1 + q0:1 + q0 + 128],
                    PS[5][:, 0:128], op=ALU.add), sig=True)
                P.wait("tensor", rv)
            P.barrier()

        for tau in range(16):
            emit_tau(tau)
        for nm in ("GKV", "QCTM", "QSTMt", "QFUT", "VST", "QS", "QI", "KI",
                   "WOUT", "WUV"):
            WSL.free(nm)
        for nm in ("itA", "itB", "sbnd8", "abnd8", "ssel", "asel8",
                   "atp", "prd"):
            XNs.free(nm)
        P.barrier()



    def phase_ffn(b, n_layers=6):
        WQKV = WSL.alloc("WQKV", 24576)
        WOsl = XNs.alloc("WO", 8192)
        XN2 = XNs.alloc("XN2", 8192)
        QW = QB[:, 0:2048]; QN = QB[:, 2048:4096]; QRT = QB[:, 4096:6144]
        SMF = QB[:, 6144:9216].bitcast(F32)
        AT = QB[:, 9216:10752]
        ATPF = QB[:, 10752:11136]
        AOW = QB[:, 11136:13184]
        AVE = QB[:, 13184:14208]
        KW = KBS[:, 0:2048]; KN = KBS[:, 2048:4096]
        KRT = KBS[:, 4096:8192]
        VW = KBS[:, 8192:12288]
        TBLQ = TBL.alloc("tblq", 1024)
        RST2 = KBS[:, 12288:16384].bitcast(F32)
        RST2B = TBL.tile[:, 1024:3072]
        SS2 = XNs.tile[:, 8192:12288].bitcast(F32)

        def emit_ln_one_tg(tg, shift, g_ap, b_ap, dst, scr):
            sl0 = 1 + 512 * tg
            prev_t2 = None
            for c in range(NCH):
                off = sl0 - 1 if (shift and c >= 4) else sl0
                src = XI[c][:, off:off + 512]
                if prev_t2 is not None:
                    P.wait("scalar", prev_t2)
                sq = s_op(lambda e, src=src, scr=scr: e.activation(
                    out=scr, in_=src, func=AF.Square), sig=True)
                P.wait("tensor", sq)
                t1 = t_op(lambda e, src=src, c=c: e.matmul(
                    PS[5][0:1, 0:512], ONESB[:, 0:1], src,
                    start=(c == 0), stop=(c == NCH - 1)), sig=(c == NCH - 1))
                t2 = t_op(lambda e, scr=scr, c=c: e.matmul(
                    PS[6][0:1, 0:512], ONESB[:, 1:2], scr,
                    start=(c == 0), stop=(c == NCH - 1)), sig=True)
                prev_t2 = t2
            P.wait("vector", t1)
            P.wait("vector", t2)
            r1 = v_op(lambda e: e.tensor_scalar_mul(
                SROW[0:1, :], PS[5][0:1, 0:512], -1.0 / 1024.0), sig=True)
            r2 = v_op(lambda e: e.tensor_scalar_mul(
                SROW2[0:1, :], PS[6][0:1, 0:512], 1.0 / 1024.0), sig=True)
            P.wait("scalar", r1)
            P.wait("vector", r2)
            r3 = s_op(lambda e: e.activation(out=PS[5][0:1, 0:512],
                                             in_=SROW[0:1, :], func=AF.Square),
                      sig=True)
            P.wait("vector", r3)
            r4 = v_op(lambda e: e.tensor_sub(SROW2[0:1, :], SROW2[0:1, :],
                                             PS[5][0:1, 0:512]), sig=True)
            P.wait("scalar", r4)
            r5 = s_op(lambda e: e.activation(out=SROW2[0:1, :], in_=SROW2[0:1, :],
                                             func=AF.Ln, bias=CCOL[0:1, 0:1]),
                      sig=True)
            P.wait("scalar", r5)
            r6 = s_op(lambda e: e.activation(out=SROW2[0:1, :], in_=SROW2[0:1, :],
                                             func=AF.Exp, scale=CCOL[0:1, 5:6]),
                      sig=True)
            P.wait("vector", r6)
            r7 = v_op(lambda e: e.tensor_copy(SROWB[0:1, 0:512], SROW[0:1, :]),
                      sig=True)
            P.wait("vector", r7)
            r8 = v_op(lambda e: e.tensor_copy(SROWB[0:1, 512:1024], SROW2[0:1, :]),
                      sig=True)
            P.wait("tensor", r7)
            P.wait("tensor", r8)
            b1t = t_op(lambda e: e.matmul(PS[5][:, 0:512], ONER[0:1, :],
                                          SROWB[0:1, 0:512], start=True, stop=True),
                       sig=True)
            b2t = t_op(lambda e: e.matmul(PS[6][:, 0:512], ONER[0:1, :],
                                          SROWB[0:1, 512:1024], start=True,
                                          stop=True), sig=True)
            P.wait("vector", b1t)
            P.wait("vector", b2t)
            lastv = None
            for c in range(NCH):
                off = sl0 - 1 if (shift and c >= 4) else sl0
                src = XI[c][:, off:off + 512]
                d = dst(c, tg)
                v1 = v_op(lambda e, src=src, scr=scr: e.tensor_add(
                    scr, src, PS[5][:, 0:512]), sig=True)
                P.wait("vector", v1)
                if g_ap is None:
                    v2 = v_op(lambda e, d=d, scr=scr: e.tensor_mul(
                        d, scr, PS[6][:, 0:512]), sig=True)
                else:
                    v2 = v_op(lambda e, scr=scr: e.tensor_mul(
                        scr, scr, PS[6][:, 0:512]), sig=True)
                    P.wait("vector", v2)
                    v2 = v_op(lambda e, d=d, scr=scr, c=c, g_ap=g_ap, b_ap=b_ap:
                              e.tensor_scalar(
                        d, scr, g_ap[:, c:c + 1], b_ap[:, c:c + 1],
                        op0=ALU.mult, op1=ALU.add), sig=True)
                P.wait("vector", v2)
                lastv = v2
            P.wait("tensor", lastv)
            P.wait("scalar", lastv)

        def load_layer_weights(l):
            toks = [dmaG(lambda e, l=l: e.dma_start(
                out=WQKV.rearrange("p (c n) -> p c n", c=8),
                in_=WD["wqkv"][l].transpose([1, 0, 2])))]
            toks.append(dmaG(lambda e, l=l: e.dma_start(
                out=WOsl.rearrange("p (c n) -> p c n", c=8),
                in_=WD["wo"][l].transpose([1, 0, 2]))))
            toks.append(dmaG(lambda e, l=l: e.dma_start(out=QSC[:, 0:1],
                                                        in_=WD["qs"][l])))
            toks.append(dmaG(lambda e, l=l: e.dma_start(out=QSC[:, 1:2],
                                                        in_=WD["ks"][l])))
            toks.append(dmaG(lambda e, l=l: e.dma_start(out=B1A[:], in_=WD["b1a"][l])))
            toks.append(dmaG(lambda e, l=l: e.dma_start(out=B1G[:], in_=WD["b1g"][l])))
            toks.append(dmaG(lambda e, l=l: e.dma_start(out=B2T[:], in_=WD["b2"][l])))
            toks.append(dmaG(lambda e, l=l: e.dma_start(out=LNP[:, 0:8],
                                                        in_=WD["lng"][l])))
            toks.append(dmaG(lambda e, l=l: e.dma_start(out=LNP[:, 8:16],
                                                        in_=WD["lnb"][l])))
            return toks

        def emit_attn_sub(l, w, sub, hg, first_avp, prev_mask, tstate):
            par = w % 2
            prev_par = 1 - par
            q0p = 128 * sub
            mmt = []
            if prev_mask[0] is not None:
                P.wait("tensor", prev_mask[0])
            for hh in range(4):
                h = 4 * hg + hh
                c = h // 2
                r0 = 64 * (h % 2)
                qs0 = 256 * c + q0p
                wA = 256 - 128 * sub
                pieces = []
                if w > 0:
                    pieces.append((512 * c + 256 * prev_par + 128 * sub, wA, 0))
                pieces.append((512 * c + 256 * par, 128 + 128 * sub, wA))
                for (kc, kw_, oc_) in pieces:
                    tok = t_op(lambda e, hh=hh, r0=r0, qs0=qs0, kc=kc, kw_=kw_,
                               oc_=oc_: e.matmul(
                        PS[hh][:, oc_:oc_ + kw_],
                        QRT[r0:r0 + 64, qs0:qs0 + 128],
                        KRT[r0:r0 + 64, kc:kc + kw_],
                        start=True, stop=True), sig=True)
                    mmt.append(tok)
            for t in mmt:
                P.wait("vector", t)
            mknm = "M384" if w > 0 else ("M384_W0S0" if sub == 0 else "M384_W0S1")
            MK = CS[mknm]
            v1 = None
            for hh in range(4):
                v1 = v_op(lambda e, hh=hh, MK=MK: e.tensor_add(
                    SMF[:, 384 * hh:384 * hh + 384], PS[hh][:, 0:384], MK[:]),
                    sig=True)
            prev_mask[0] = v1
            P.wait("vector", v1)
            n1 = v_op(lambda e: e.tensor_reduce(
                NMX[:, 0:4], SMF[:].rearrange("p (h c) -> p h c", h=4),
                axis=AX.X, op=ALU.max, negate=True), sig=True)
            P.wait("vector", n1)
            n2 = v_op(lambda e: e.tensor_scalar_mul(NMX[:, 0:4], NMX[:, 0:4], 8.0),
                      sig=True)
            P.wait("scalar", n2)
            ex = None
            for hh in range(4):
                ex = s_op(lambda e, hh=hh: e.activation(
                    out=AT[:, 384 * hh:384 * hh + 384],
                    in_=SMF[:, 384 * hh:384 * hh + 384], func=AF.Exp,
                    bias=NMX[:, hh:hh + 1], scale=CCOL[:, 4:5],
                    accum_out=ZROW[:, hh:hh + 1]), sig=True)
            P.wait("vector", ex)
            rdst = ZROW[:, 8 + 4 * hg:12 + 4 * hg] if hg < 2 else \
                NMX[:, 8 + 4 * (hg - 2):12 + 4 * (hg - 2)]
            rz = v_op(lambda e, rdst=rdst: e.reciprocal(rdst, ZROW[:, 0:4]),
                      sig=True)
            tstate["rz"] = rz
            P.wait("tensor", ex)
            blk0 = 2 * w - 2 + sub
            close_bank = hg in (1, 3)
            kk = 0
            last_mm = None
            for hh in range(4):
                h = 4 * hg + hh
                avbank = 4 + (h // 8)
                blks = [bi for bi in range(3) if blk0 + bi >= 0]
                for bn, bi in enumerate(blks):
                    blk = blk0 + bi
                    p = kk % 2
                    tb = 6 + p
                    if tstate["ev"][p] is not None:
                        P.wait("tensor", tstate["ev"][p])
                    tt = t_op(lambda e, hh=hh, bi=bi, tb=tb: e.transpose(
                        psbf(tb)[:, 0:128],
                        AT[:, 384 * hh + 128 * bi:384 * hh + 128 * bi + 128],
                        IDENT[:]), sig=True)
                    P.wait("vector", tt)
                    if tstate["mm"][p] is not None:
                        P.wait("vector", tstate["mm"][p])
                    ev = v_op(lambda e, p=p, tb=tb: e.tensor_copy(
                        ATPF[:, 128 * p:128 * p + 128], psbf(tb)[:, 0:128]),
                        sig=True)
                    tstate["ev"][p] = ev
                    P.wait("tensor", ev)
                    slot = blk % 4
                    st = first_avp[h // 8]
                    first_avp[h // 8] = False
                    sp = close_bank and hh == 3 and bn == len(blks) - 1
                    mm = t_op(lambda e, h=h, slot=slot, st=st, sp=sp, p=p,
                              avbank=avbank: e.matmul(
                        PS[avbank][:, 64 * (h % 8):64 * (h % 8) + 64],
                        ATPF[:, 128 * p:128 * p + 128],
                        VW[:, 1024 * slot + 64 * h:1024 * slot + 64 * h + 64],
                        start=st, stop=sp), sig=True)
                    tstate["mm"][p] = mm
                    last_mm = mm
                    kk += 1
            return last_mm

        def emit_window(l, w):
            par = w % 2
            t0 = 1 + 256 * w
            last_qk_evac = [None]
            ttk = []
            for i, nm in enumerate(("TF_QC", "TF_QS", "TF_KC", "TF_KS")):
                ttk.append(dmaS(lambda e, nm=nm, i=i, w=w: e.dma_start(
                    out=TBLQ[:, 256 * i:256 * i + 256],
                    in_=TD[nm][:, 256 * w:256 * w + 256])))
            last_ev = {6: None, 7: None, 3: None, 2: None}
            for oc in range(8):
                for part, ps_i, dst in ((0, 6, QW), (1024, 7, KW)):
                    if last_ev[ps_i] is not None:
                        P.wait("tensor", last_ev[ps_i])
                    for c in range(NCH):
                        off = t0 - 1 if c >= 4 else t0
                        tok = t_op(lambda e, c=c, oc=oc, part=part, ps_i=ps_i,
                                   off=off: e.matmul(
                            PS[ps_i][:, 0:256],
                            WQKV[:, 3072 * c + part + 128 * oc:
                                 3072 * c + part + 128 * oc + 128],
                            XI[c][:, off:off + 256],
                            start=(c == 0), stop=(c == NCH - 1)),
                            sig=(c == NCH - 1))
                    P.wait("vector", tok)
                    ev = v_op(lambda e, oc=oc, ps_i=ps_i, dst=dst: e.tensor_copy(
                        dst[:, 256 * oc:256 * oc + 256], PS[ps_i][:, 0:256]),
                        sig=True)
                    last_ev[ps_i] = ev
                    last_qk_evac[0] = ev
            for sub in range(2):
                for nh in range(2):
                    ps_i = 3 if (2 * sub + nh) % 2 == 0 else 2
                    if last_ev[ps_i] is not None:
                        P.wait("tensor", last_ev[ps_i])
                    for c in range(NCH):
                        off = t0 - 1 if c >= 4 else t0
                        tok = t_op(lambda e, c=c, sub=sub, nh=nh, ps_i=ps_i,
                                   off=off: e.matmul(
                            PS[ps_i][:, 0:512],
                            XI[c][:, off + 128 * sub:off + 128 * sub + 128],
                            WQKV[:, 3072 * c + 2048 + 512 * nh:
                                 3072 * c + 2048 + 512 * nh + 512],
                            start=(c == 0), stop=(c == NCH - 1)),
                            sig=(c == NCH - 1))
                    P.wait("vector", tok)
                    slot = (2 * w + sub) % 4
                    ev = v_op(lambda e, slot=slot, nh=nh, sub=sub,
                              ps_i=ps_i: e.tensor_copy(
                        VW[:, 1024 * slot + 512 * nh:1024 * slot + 512 * nh + 512],
                        PS[ps_i][:, 0:512]), sig=True)
                    last_ev[ps_i] = ev
            for t in ttk:
                P.wait("vector", t)
            P.wait("scalar", last_qk_evac[0])
            last_sum_ev = None
            last_sq_mm = [None, None]
            last_stt = {5: None, 6: None}
            last_r2 = {5: None, 6: None}
            last_r3 = None
            for qk, SRCW, DSTN, scl_col, tc0 in (
                    (0, QW, QN, 0, 0), (1, KW, KN, 1, 512)):
                if last_r3 is not None:
                    P.wait("scalar", last_r3)
                for i in range(4):
                    sl = 512 * (i % 2)
                    if last_sq_mm[i % 2] is not None:
                        P.wait("scalar", last_sq_mm[i % 2])
                    sq = s_op(lambda e, i=i, sl=sl, SRCW=SRCW: e.activation(
                        out=AVE[:, sl:sl + 512],
                        in_=SRCW[:, 512 * i:512 * i + 512],
                        func=AF.Square), sig=True)
                    P.wait("tensor", sq)
                    if last_sum_ev is not None:
                        P.wait("tensor", last_sum_ev)
                    mm = t_op(lambda e, sl=sl: e.matmul(
                        PS[4][0:2, 0:512], CS["IND2"][:], AVE[:, sl:sl + 512],
                        start=True, stop=True), sig=True)
                    last_sq_mm[i % 2] = mm
                    P.wait("vector", mm)
                    ev = v_op(lambda e, i=i: e.tensor_copy(
                        SS2[0:2, 512 * i:512 * i + 512], PS[4][0:2, 0:512]),
                        sig=True)
                    last_sum_ev = ev
                P.wait("vector", last_sum_ev)
                s2 = v_op(lambda e: e.tensor_scalar_max(SS2[0:2, :], SS2[0:2, :],
                                                        1e-24), sig=True)
                P.wait("scalar", s2)
                s3 = s_op(lambda e: e.activation(out=RST2[0:2, :], in_=SS2[0:2, :],
                                                 func=AF.Ln), sig=True)
                P.wait("scalar", s3)
                s4 = s_op(lambda e: e.activation(out=RST2B[0:2, :],
                                                 in_=RST2[0:2, :], func=AF.Exp,
                                                 scale=CCOL[0:2, 5:6]), sig=True)
                P.wait("tensor", s4)
                for c in range(NCH):
                    bank = 5 + (c % 2)
                    if last_stt[bank] is not None:
                        P.wait("tensor", last_stt[bank])
                    bc = t_op(lambda e, c=c, bank=bank: e.matmul(
                        PS[bank][:, 0:256], CS["IND2T"][:],
                        RST2B[0:2, 256 * c:256 * c + 256], start=True, stop=True),
                        sig=True)
                    P.wait("vector", bc)
                    stt = v_op(lambda e, c=c, bank=bank, SRCW=SRCW, DSTN=DSTN,
                               scl_col=scl_col: e.scalar_tensor_tensor(
                        DSTN[:, 256 * c:256 * c + 256], PS[bank][:, 0:256],
                        QSC[:, scl_col:scl_col + 1],
                        SRCW[:, 256 * c:256 * c + 256],
                        op0=ALU.mult, op1=ALU.mult), sig=True)
                    last_stt[bank] = stt
                    P.wait("tensor", stt)
                    if last_r2[bank] is not None:
                        P.wait("tensor", last_r2[bank])
                    rt = t_op(lambda e, c=c, bank=bank, DSTN=DSTN: e.matmul(
                        PS[bank][:, 256:512], CS["P_ROT64"][:],
                        DSTN[:, 256 * c:256 * c + 256], start=True, stop=True),
                        sig=True)
                    a0 = 512 * (c % 2)
                    P.wait("vector", stt)
                    r1 = v_op(lambda e, c=c, a0=a0, DSTN=DSTN, tc0=tc0:
                              e.tensor_tensor(
                        AVE[:, a0:a0 + 256], DSTN[:, 256 * c:256 * c + 256],
                        TBLQ[:, tc0:tc0 + 256], op=ALU.mult), sig=True)
                    P.wait("vector", rt)
                    r2 = v_op(lambda e, a0=a0, bank=bank, tc0=tc0:
                              e.tensor_tensor(
                        AVE[:, a0 + 256:a0 + 512], PS[bank][:, 256:512],
                        TBLQ[:, tc0 + 256:tc0 + 512], op=ALU.mult), sig=True)
                    last_r2[bank] = r2
                    dcol = (512 * c + 256 * par) if qk == 1 else 256 * c
                    DSTR = KRT if qk == 1 else QRT
                    P.wait("vector", r2)
                    r3 = v_op(lambda e, dcol=dcol, a0=a0, DSTR=DSTR: e.tensor_add(
                        DSTR[:, dcol:dcol + 256], AVE[:, a0:a0 + 256],
                        AVE[:, a0 + 256:a0 + 512]), sig=True)
                    last_r3 = r3
            if l == 0 and w == 1:
                tap("qrt_w1", lambda: QRT[:], [128, 2048], BF16)
                tap("krt_w1", lambda: KRT[:], [128, 4096], BF16)
                tap("vw_w1", lambda: VW[:], [128, 4096], BF16)
                tap("qn_w1", lambda: QN[:], [128, 2048], BF16)
            P.barrier()
            prev_mask = [None]
            tstate = {"ev": [None, None], "mm": [None, None]}
            for sub in range(2):
                first_avp = [True, True]
                last_mm = None
                for hg in range(4):
                    last_mm = emit_attn_sub(l, w, sub, hg, first_avp, prev_mask,
                                            tstate)
                P.wait("scalar", last_mm)
                P.wait("scalar", tstate["rz"])
                ecs = None
                for h in range(16):
                    bank = 4 + h // 8
                    rcol = ZROW[:, 8 + h:9 + h] if h < 8 else \
                        NMX[:, h:h + 1]
                    ecs = s_op(lambda e, h=h, bank=bank, rcol=rcol: e.activation(
                        out=AVE[:, 64 * h:64 * h + 64],
                        in_=PS[bank][:, 64 * (h % 8):64 * (h % 8) + 64],
                        func=AF.Copy, scale=rcol), sig=True)
                P.wait("tensor", ecs)
                for c4 in range(8):
                    p = c4 % 2
                    tb = 6 + p
                    if tstate["ev"][p] is not None:
                        P.wait("tensor", tstate["ev"][p])
                    tt = t_op(lambda e, c4=c4, tb=tb: e.transpose(
                        psbf(tb)[:, 0:128], AVE[:, 128 * c4:128 * c4 + 128],
                        IDENT[:]), sig=True)
                    P.wait("vector", tt)
                    e3 = v_op(lambda e, c4=c4, sub=sub, tb=tb: e.tensor_copy(
                        AOW[:, 256 * c4 + 128 * sub:256 * c4 + 128 * sub + 128],
                        psbf(tb)[:, 0:128]), sig=True)
                    tstate["ev"][p] = e3
            if l == 0 and w == 1:
                tap("aow_w1", lambda: AOW[:], [128, 2048], BF16)
            P.barrier()
            last_rv = {6: None, 7: None}
            for oc in range(8):
                wb = 6 + oc % 2
                if last_rv[wb] is not None:
                    P.wait("tensor", last_rv[wb])
                for ic in range(8):
                    tok = t_op(lambda e, oc=oc, ic=ic, wb=wb: e.matmul(
                        PS[wb][:, 0:256],
                        WOsl[:, 1024 * ic + 128 * oc:1024 * ic + 128 * oc + 128],
                        AOW[:, 256 * ic:256 * ic + 256],
                        start=(ic == 0), stop=(ic == 7)), sig=(ic == 7))
                P.wait("vector", tok)
                rv = v_op(lambda e, oc=oc, t0=t0, wb=wb: e.tensor_add(
                    XI[oc][:, t0:t0 + 256], XI[oc][:, t0:t0 + 256],
                    PS[wb][:, 0:256]), sig=True)
                last_rv[wb] = rv
            P.barrier()

        def emit_mlp_group(l, g):
            MSCR = XNs.tile[:, 0:2048]
            lnscr2 = MSCR[:, 0:512]
            def dstfn(c, tg):
                return XN2[:, 1024 * c + 512 * (tg - 2 * g):
                           1024 * c + 512 * (tg - 2 * g) + 512]
            for tg in (2 * g, 2 * g + 1):
                emit_ln_one_tg(tg, True, LNP[:, 0:8], LNP[:, 8:16], dstfn, lnscr2)
            P.barrier()
            if l == 0 and g == 0:
                tap("xn2_g0", lambda: XN2[:], [128, 8192], BF16)
            w1ring = [KBS[:, 6144 + 1024 * i:6144 + 1024 * (i + 1)]
                      for i in range(4)]
            w2ring = [KBS[:, 10240 + 2816 * i:10240 + 2816 * (i + 1)]
                      for i in range(2)]
            def hsrc_ap(cc, th):
                if cc < 16:
                    return QB[:, 1024 * cc + 512 * th:1024 * cc + 512 * th + 512]
                return KBS[:, 1024 * (cc - 16) + 512 * th:
                           1024 * (cc - 16) + 512 * th + 512]
            def load_w1(cc):
                slot = cc % 2
                tA = dmaG(lambda e, cc=cc, slot=slot: e.dma_start(
                    out=w1ring[2 * slot].rearrange("p (c n) -> p c n", c=8),
                    in_=WD["w1"][l, cc, 0].transpose([1, 0, 2])))
                tG = dmaG(lambda e, cc=cc, slot=slot: e.dma_start(
                    out=w1ring[2 * slot + 1].rearrange("p (c n) -> p c n", c=8),
                    in_=WD["w1"][l, cc, 1].transpose([1, 0, 2])))
                return (tA, tG)
            w1tok = {0: load_w1(0)}
            prev_cons = None
            ABANKS = (6, 2)
            GBANKS = (7, 3)
            GSLOT = (slice(512, 1024), slice(1536, 2048))
            last_hv = {0: None, 1: None}
            for cc in range(22):
                slot = cc % 2
                tA, tG = w1tok[cc]
                P.wait("tensor", tA)
                P.wait("tensor", tG)
                for th in range(2):
                    par = th
                    ab, gb = ABANKS[par], GBANKS[par]
                    if last_hv[par] is not None:
                        P.wait("tensor", last_hv[par])
                    for ag, ps_i in ((0, ab), (1, gb)):
                        ring = w1ring[2 * slot + ag]
                        for c in range(NCH):
                            tok = t_op(lambda e, c=c, ring=ring, ps_i=ps_i, th=th:
                                       e.matmul(
                                PS[ps_i][:, 0:512],
                                ring[:, 128 * c:128 * c + 128],
                                XN2[:, 1024 * c + 512 * th:
                                    1024 * c + 512 * th + 512],
                                start=(c == 0), stop=(c == NCH - 1)),
                                sig=(c == NCH - 1))
                        if ag == 0:
                            atok = tok
                        else:
                            gtok = tok
                    P.wait("scalar", gtok)
                    if last_hv[par] is not None:
                        P.wait("scalar", last_hv[par])
                    gl = s_op(lambda e, cc=cc, gb=gb, gs=GSLOT[par]: e.activation(
                        out=MSCR[:, gs], in_=PS[gb][:, 0:512],
                        func=AF.Gelu, bias=B1G[:, cc:cc + 1]), sig=True)
                    P.wait("vector", gl)
                    P.wait("vector", atok)
                    hv = v_op(lambda e, cc=cc, th=th, ab=ab, gs=GSLOT[par]:
                              e.scalar_tensor_tensor(
                        hsrc_ap(cc, th), PS[ab][:, 0:512], B1A[:, cc:cc + 1],
                        MSCR[:, gs], op0=ALU.add, op1=ALU.mult), sig=True)
                    last_hv[par] = hv
                    prev_cons = atok
                if cc + 1 < 22:
                    P.wait("gpsimd", prev_cons)
                    w1tok[cc + 1] = load_w1(cc + 1)
            P.wait("tensor", last_hv[0])
            P.wait("tensor", last_hv[1])
            P.barrier()
            def load_w2(o):
                slot = o % 2
                return dmaG(lambda e, o=o, slot=slot: e.dma_start(
                    out=w2ring[slot].rearrange("p (c n) -> p c n", c=22),
                    in_=WD["w2"][l, o].transpose([1, 0, 2])))
            w2tok = {0: load_w2(0)}
            prev_w2 = None
            last_rc = {0: None, 1: None}
            W2SLOT = (slice(1024, 1536), slice(0, 512))
            for o in range(8):
                slot = o % 2
                P.wait("tensor", w2tok[o])
                for th in range(2):
                    par = th
                    wb = 4 + par
                    tcol = 1 + 1024 * g + 512 * th
                    if last_rc[par] is not None:
                        P.wait("tensor", last_rc[par])
                    for cc in range(22):
                        tok = t_op(lambda e, cc=cc, slot=slot, th=th, wb=wb: e.matmul(
                            PS[wb][:, 0:512],
                            w2ring[slot][:, 128 * cc:128 * cc + 128],
                            hsrc_ap(cc, th), start=(cc == 0), stop=(cc == 21)),
                            sig=(cc == 21))
                    P.wait("vector", tok)
                    rv = v_op(lambda e, o=o, tcol=tcol, wb=wb, ws=W2SLOT[par]:
                              e.scalar_tensor_tensor(
                        MSCR[:, ws], PS[wb][:, 0:512], B2T[:, o:o + 1],
                        XI[o][:, tcol:tcol + 512], op0=ALU.add, op1=ALU.add),
                        sig=True)
                    P.wait("vector", rv)
                    rc = v_op(lambda e, o=o, tcol=tcol, ws=W2SLOT[par]: e.tensor_copy(
                        XI[o][:, tcol:tcol + 512], MSCR[:, ws]), sig=True)
                    last_rc[par] = rc
                    prev_w2 = tok
                if o + 1 < 8:
                    P.wait("gpsimd", prev_w2)
                    w2tok[o + 1] = load_w2(o + 1)
            P.wait("tensor", last_rc[0])
            P.wait("tensor", last_rc[1])
            P.barrier()

        ptoks = []
        for c in range(NCH):
            ptoks.append(dmaG(lambda e, c=c: e.dma_start(
                out=X2D[c], in_=XI[c][:, 1:2049])))
        for t in ptoks:
            P.wait("vector", t)
        lnscr = AVE[:, 0:512]
        for tg in range(4):
            emit_ln_one_tg(tg, False, None, None,
                           lambda c, tg: XI[c][:, 1 + 512 * tg:1 + 512 * tg + 512],
                           lnscr)
        P.barrier()
        for c in range(NCH):
            tap("xinner_c%d_b%d" % (c, b),
                lambda c=c: XI[c][:, 1:2049], [128, 2048], BF16)
        wtoks = load_layer_weights(0)
        for t in wtoks:
            for en in ("tensor", "vector", "scalar"):
                P.wait(en, t)
        for l in range(n_layers):
            for w in range(8):
                emit_window(l, w)
            for c in range(NCH):
                tap("attn_l%d_c%d_b%d" % (l, c, b),
                    lambda c=c: XI[c][:, 1:2049], [128, 2048], BF16)
            for g in range(2):
                emit_mlp_group(l, g)
            if l + 1 < n_layers:
                wtoks = load_layer_weights(l + 1)
                for t in wtoks:
                    for en in ("tensor", "vector", "scalar"):
                        P.wait(en, t)
            tap("ffn_l%d_b%d" % (l, b), lambda: XI[0][:, 1:2049], [128, 2048], BF16)
        WSL.free("WQKV")
        XNs.free("WO"); XNs.free("XN2")
        TBL.free("tblq")
        P.barrier()

    def phase_out(b):
        oscr = XNs.alloc("oscr", 2048, F32)
        x2t = XNs.alloc("x2t", 512, F32)
        prev = None
        prev_tt = None
        for j in range(16):
            for c in range(NCH):
                if prev is not None:
                    P.wait("sync", prev)
                ld = dmaS(lambda e, c=c, j=j: e.dma_start(
                    out=x2t[:, 0:128], in_=X2D[c][:, 128 * j:128 * j + 128]))
                P.wait("vector", ld)
                if prev_tt is not None:
                    P.wait("vector", prev_tt)
                ad = v_op(lambda e, c=c, j=j: e.tensor_add(
                    x2t[:, 128:256], x2t[:, 0:128],
                    XI[c][:, 1 + 128 * j:1 + 128 * j + 128]), sig=True)
                prev = ad
                P.wait("tensor", ad)
                tt = t_op(lambda e: e.transpose(
                    PS[5][:, 0:128], x2t[:, 128:256], CS["IDF32"][:]), sig=True)
                prev_tt = tt
                P.wait("vector", tt)
                ev = v_op(lambda e, c=c: e.tensor_copy(
                    oscr[:, 128 * c:128 * c + 128], PS[5][:, 0:128]), sig=True)
                P.wait("tensor", ev)
                P.wait("sync", ev)
            dd = dmaS(lambda e, b=b, j=j: e.dma_start(
                out=OUT[b, 128 * j:128 * j + 128, :], in_=oscr[:, 0:1024]))
            P.wait("vector", dd)
        XNs.free("oscr"); XNs.free("x2t")
        P.barrier()

    xn_slab_v = XNs.alloc("xn", 16384)
    xn_v = [xn_slab_v[:, 2048 * c:2048 * (c + 1)] for c in range(NCH)]
    XNs.free("xn")

    phase_delta()
    tap("dcol", lambda: DCOL[:], [128, 2], F32)
    for b in range(B_CORE):
        XNs.alloc("xn", 16384)
        phase_ln1(b, xn_v)
        tap("xn0_b%d" % b, lambda: xn_v[0][:], [128, 2048], BF16)
        tap("xiraw0_b%d" % b, lambda: XI[0][:, 1:2049], [128, 2048], BF16)
        H = phase_mla_proj(b, xn_v)
        XNs.free("xn")
        phase_mla_attend(b, H)
        tap("xi0_b%d" % b, lambda: XI[0][:, 1:2049], [128, 2048], BF16)
        phase_ffn(b, n_layers=n_ffn_layers)
        phase_out(b)
    fin = P.sig_last("sync")
    P.wait("sync", fin)

    with nc.Block() as block:
        P.run(block, sems)
    B.finalize()
    return B, TAPS


def _in_maps(inputs, consts=None):
    consts = consts or _host_consts()
    lay = _host_weight_layouts(inputs)
    x = np.ascontiguousarray(np.asarray(inputs["L0_attn_x"], np.float32))
    maps = []
    for k in range(8):
        m = {"x": x[2 * k:2 * k + 2]}
        for key, v in lay.items():
            m[key] = v
        for key, v in consts.items():
            if key in CONST_SPECS:
                m["c_" + key] = v
            else:
                m["t_" + key] = v
        maps.append(m)
    return maps


def kernel(**inputs):
    Bobj, _ = build_program()
    res = run_bass_kernel_spmd(Bobj.nc, _in_maps(inputs), core_ids=list(range(8)))
    return np.concatenate([r["o"] for r in res.results], axis=0)


# version 07
# speedup vs baseline: 1.1854x; 1.0021x over previous
import numpy as np
import ml_dtypes
import concourse.bass as bass
import concourse.mybir as mybir
from concourse.bass import IndirectOffsetOnAxis
from concourse.bass_utils import run_bass_kernel_spmd

F32 = mybir.dt.float32
BF16 = mybir.dt.bfloat16
U32 = mybir.dt.uint32
AF = mybir.ActivationFunctionType
ALU = mybir.AluOpType
AX = mybir.AxisListType
BF = ml_dtypes.bfloat16

B_CORE = 2
L = 2048
NCH = 8
TCOL = 2056
NEG = -3.0e38
NEGM = -1.0e30
S96 = float((64 + 32) ** (-0.5))

def _host_consts():
    c = {}
    t = np.arange(L, dtype=np.float64)
    theta = 1.0 / 10000.0 ** (2.0 * np.arange(16, dtype=np.float64) / 32.0)
    fr32 = np.tile(np.concatenate([theta, theta]), 4)
    c["TQS_C"] = np.cos(t[None, :] * fr32[:, None]).astype(BF)
    c["TQS_S"] = np.sin(t[None, :] * fr32[:, None]).astype(BF)
    ab = (2.0 * t - 63.0)[None, :] * fr32[:, None]
    c["TQB_C"] = np.cos(ab).astype(BF)
    c["TQB_S"] = np.sin(ab).astype(BF)

    P32 = np.zeros((128, 128), np.float32)
    for blk in range(4):
        for j in range(16):
            P32[32 * blk + j + 16, 32 * blk + j] = -1.0
            P32[32 * blk + j, 32 * blk + j + 16] = 1.0
    c["P_ROT32"] = P32.astype(BF)
    P64 = np.zeros((128, 128), np.float32)
    for blk in range(2):
        for j in range(32):
            P64[64 * blk + j + 32, 64 * blk + j] = -1.0
            P64[64 * blk + j, 64 * blk + j + 32] = 1.0
    c["P_ROT64"] = P64.astype(BF)

    halfa = np.arange(0, 64, 2, dtype=np.float64)
    inv_freq = 1.0 / 10000.0 ** (halfa / 64.0)
    fr64 = np.tile(np.concatenate([inv_freq, inv_freq]), 2)
    bs64 = np.tile(np.concatenate([(halfa + 25.6) / 89.6] * 2), 2)
    ang = t[None, :] * fr64[:, None]
    scl = bs64[:, None] ** ((t - L // 2) / 128.0)[None, :]
    c["TF_QC"] = (np.cos(ang) * scl).astype(BF)
    c["TF_QS"] = (np.sin(ang) * scl).astype(BF)
    c["TF_KC"] = (np.cos(ang) / scl).astype(BF)
    c["TF_KS"] = (np.sin(ang) / scl).astype(BF)

    r = np.arange(128)[:, None]
    cc = np.arange(256)[None, :]
    c["M_BAND"] = np.where((cc - r >= 65) & (cc - r <= 128), 0.0, NEG).astype(np.float32)
    p0 = cc - 128
    c["M_BAND0"] = np.where((p0 >= 0) & (r >= 64) & (p0 <= r) & (p0 >= r - 63),
                            0.0, NEG).astype(np.float32)
    c["M_FUT0"] = np.where((p0 >= 0) & (p0 <= 95) &
                           ((r <= 63) | ((r <= 94) & (p0 > r))),
                           0.0, NEG).astype(np.float32)
    c128 = np.arange(128)[None, :]
    c["M_TRI"] = np.where(c128 <= r, 0.0, NEGM).astype(np.float32)
    c["MF0"] = np.where(c128 >= r, 0.0, NEG).astype(np.float32)
    c["MF2"] = np.where(c128 <= r, 0.0, NEG).astype(np.float32)

    slot = 64.0 + np.arange(32, dtype=np.float64)
    a_sl = slot[:, None] * theta[None, :]
    cj, sj = np.cos(a_sl), np.sin(a_sl)
    T1 = np.concatenate([cj, cj], axis=1).reshape(-1).astype(np.float32)
    T2 = np.concatenate([-sj, sj], axis=1).reshape(-1).astype(np.float32)
    c["SLOT_T1"] = np.broadcast_to(T1.astype(BF), (128, 1024)).copy()
    c["SLOT_T2"] = np.broadcast_to(T2.astype(BF), (128, 1024)).copy()

    c["IDENT"] = np.eye(128, dtype=np.float32).astype(BF)
    c["IDF32"] = np.eye(128, dtype=np.float32)
    r1 = np.arange(128)[:, None]
    c1 = np.arange(384)[None, :]
    m = np.zeros((128, 384), np.float32)
    m[:, 0:128][~(c1[:, 0:128] >= r1)] = NEG
    m[:, 256:384][~(c1[:, 0:128] <= r1)] = NEG
    c["M384"] = m
    m0 = np.full((128, 384), NEG, np.float32)
    m0[:, 256:384] = np.where(c1[:, 0:128] <= r1, 0.0, NEG)
    c["M384_W0S0"] = m0
    m1 = np.full((128, 384), NEG, np.float32)
    m1[:, 128:256] = 0.0
    m1[:, 256:384] = np.where(c1[:, 0:128] <= r1, 0.0, NEG)
    c["M384_W0S1"] = m1
    ind = np.zeros((128, 2), np.float32)
    ind[0:64, 0] = 1.0
    ind[64:128, 1] = 1.0
    c["IND2"] = ind.astype(BF)
    c["IND2T"] = np.ascontiguousarray(ind.T).astype(BF)
    return c

CONST_SPECS = {
    "P_ROT32": ([128, 128], BF16), "P_ROT64": ([128, 128], BF16),
    "M_BAND": ([128, 256], F32), "M_BAND0": ([128, 256], F32),
    "M_FUT0": ([128, 256], F32), "M_TRI": ([128, 128], F32),
    "MF0": ([128, 128], F32), "MF2": ([128, 128], F32),
    "SLOT_T1": ([128, 1024], BF16), "SLOT_T2": ([128, 1024], BF16),
    "IDENT": ([128, 128], BF16), "IDF32": ([128, 128], F32),
    "IND2": ([128, 2], BF16), "IND2T": ([2, 128], BF16),
    "M384": ([128, 384], F32), "M384_W0S0": ([128, 384], F32),
    "M384_W0S1": ([128, 384], F32),
}
TABLE_NAMES = ["TQS_C", "TQS_S", "TQB_C", "TQB_S",
               "TF_QC", "TF_QS", "TF_KC", "TF_KS"]


def _host_weight_layouts(inputs):
    m = {}
    def chunked(w):
        din, dout = w.shape
        return np.ascontiguousarray(w.reshape(din // 128, 128, dout))
    for k in ("W_dkv", "W_uk", "W_uv", "W_dq", "W_uq", "W_qr", "W_kr",
              "W_out", "idx_wq", "idx_wk"):
        m["mla_" + k] = chunked(np.asarray(inputs["L0_attn_" + k], np.float32))
    m["mla_raw_delta"] = np.ascontiguousarray(
        np.asarray(inputs["L0_attn_raw_delta"], np.float32).reshape(1, 16))
    m["mla_idx_w"] = np.ascontiguousarray(np.broadcast_to(
        np.asarray(inputs["L0_attn_idx_w"], np.float32)[None, :], (128, 4)))

    qkv = np.asarray(inputs["L0_ffn_Wqkv"], np.float32)
    m["ffn_wqkv"] = np.ascontiguousarray(qkv.reshape(6, 8, 128, 3072))
    wo = np.asarray(inputs["L0_ffn_Wo"], np.float32)
    m["ffn_wo"] = np.ascontiguousarray(wo.reshape(6, 8, 128, 1024))
    w1 = np.asarray(inputs["L0_ffn_W1"], np.float32)
    w1p = np.zeros((6, 1024, 2, 22, 128), np.float32)
    w1p[:, :, 0, :, :] = np.pad(w1[:, :, :2730], ((0, 0), (0, 0), (0, 86))).reshape(6, 1024, 22, 128)
    w1p[:, :, 1, :, :] = np.pad(w1[:, :, 2730:], ((0, 0), (0, 0), (0, 86))).reshape(6, 1024, 22, 128)
    m["ffn_w1"] = np.ascontiguousarray(
        w1p.reshape(6, 8, 128, 2, 22, 128).transpose(0, 4, 3, 1, 2, 5))
    w2 = np.asarray(inputs["L0_ffn_W2"], np.float32)
    w2p = np.pad(w2, ((0, 0), (0, 86), (0, 0))).reshape(6, 22, 128, 8, 128)
    m["ffn_w2"] = np.ascontiguousarray(w2p.transpose(0, 3, 1, 2, 4))
    b1 = np.asarray(inputs["L0_ffn_b1"], np.float32)
    m["ffn_b1a"] = np.ascontiguousarray(
        np.pad(b1[:, :2730], ((0, 0), (0, 86))).reshape(6, 22, 128).transpose(0, 2, 1))
    m["ffn_b1g"] = np.ascontiguousarray(
        np.pad(b1[:, 2730:], ((0, 0), (0, 86))).reshape(6, 22, 128).transpose(0, 2, 1))
    for k, nm in (("b2", "b2"), ("ln_g", "lng"), ("ln_b", "lnb")):
        v = np.asarray(inputs["L0_ffn_" + k], np.float32)
        m["ffn_" + nm] = np.ascontiguousarray(v.reshape(6, 8, 128).transpose(0, 2, 1))
    for k, nm in (("q_scale", "qs"), ("k_scale", "ks")):
        v = np.asarray(inputs["L0_ffn_" + k], np.float32)
        m["ffn_" + nm] = np.ascontiguousarray(np.tile(v, (1, 2)).reshape(6, 128, 1))
    return m


ENGS = ("tensor", "vector", "scalar", "gpsimd", "sync")

class Plan:
    def __init__(self):
        self.ops = {e: [] for e in ENGS}
        self.sig_count = {e: 0 for e in ENGS}
        self.dma_count = {"dmaS": 0, "dmaG": 0, "dmaC": 0}
        self.hwm = {e: {} for e in ENGS}
        self._open_run = {"dmaS": None, "dmaG": None, "dmaC": None}
        self._q_of = {"sync": "dmaS", "gpsimd": "dmaG", "scalar": "dmaC"}

    def op(self, e, fn, sig=False):
        tok = None
        if sig:
            self.sig_count[e] += 1
            tok = (e, self.sig_count[e])
        self.ops[e].append(["op", fn, tok])
        return tok

    def _close_run(self, q):
        box = self._open_run[q]
        if box is not None:
            box["v"] = self.dma_count[q]
            self._open_run[q] = None

    def dma(self, e, fn):
        q = self._q_of[e]
        if self._open_run[q] is None:
            if self.dma_count[q] > 0:
                self.ops[e].append(["wait", (q, self.dma_count[q]), None])
            self._open_run[q] = {"v": None}
        self.dma_count[q] += 16
        tok = (q, self._open_run[q])
        self.ops[e].append(["dma", fn, (q, self.dma_count[q])])
        return tok

    def close_all_runs(self):
        for q in self._open_run:
            self._close_run(q)

    def wait(self, e, tok):
        if tok is None:
            return
        s, v = tok
        if e in self._q_of:
            self._close_run(self._q_of[e])
        if isinstance(v, int) and self.hwm[e].get(s, 0) >= v:
            return
        if isinstance(v, int):
            self.hwm[e][s] = v
        self.ops[e].append(["wait", (s, v), None])

    def sig_last(self, e):
        for ent in reversed(self.ops[e]):
            if ent[0] in ("op", "dma"):
                if ent[2] is not None:
                    return ent[2]
                self.sig_count[e] += 1
                ent[2] = (e, self.sig_count[e])
                return ent[2]
        return None

    def barrier(self, engs=ENGS):
        toks = []
        for e in engs:
            t = self.sig_last(e)
            if t is None:
                t = self.op(e, lambda eng: eng.nop(), sig=True)
            toks.append(t)
        for e in engs:
            for t in toks:
                self.wait(e, t)

    def run(self, block, sems):
        self.close_all_runs()
        def mk(e):
            def body(eng):
                hwm = {}
                for kind, payload, tok in self.ops[e]:
                    if kind == "wait":
                        s, v = payload
                        if not isinstance(v, int):
                            v = v["v"]
                            assert v is not None
                        if hwm.get(s, 0) >= v:
                            continue
                        hwm[s] = v
                        eng.wait_ge(sems[s], v)
                    else:
                        ins = payload(eng)
                        if tok is not None:
                            s, v = tok
                            ins.then_inc(sems[s], 16 if kind == "dma" else 1)
            return body
        block.tensor(mk("tensor"))
        block.vector(mk("vector"))
        block.scalar(mk("scalar"))
        block.gpsimd(mk("gpsimd"))
        block.sync(mk("sync"))


class Slab:
    def __init__(self, tile, ncols):
        self.tile = tile
        self.n = ncols
        self.free_segs = [(0, ncols)]
        self.named = {}

    def alloc(self, name, cols, dtype=BF16, top=False):
        w = cols if dtype == BF16 else 2 * cols
        idxs = range(len(self.free_segs) - 1, -1, -1) if top else range(len(self.free_segs))
        for i in idxs:
            o, ln = self.free_segs[i]
            if ln >= w:
                if top:
                    self.free_segs[i] = (o, ln - w)
                    o = o + ln - w
                else:
                    self.free_segs[i] = (o + w, ln - w)
                if self.free_segs[i][1] == 0:
                    self.free_segs.pop(i)
                self.named[name] = (o, w, dtype, cols)
                return self.view(name)
        raise RuntimeError(f"slab OOM for {name} ({w} cols); free={self.free_segs}")

    def view(self, name):
        o, w, dtype, cols = self.named[name]
        ap = self.tile[:, o:o + w]
        if dtype != BF16:
            ap = ap.bitcast(dtype)
        return ap

    def raw(self, name):
        o, w, _, _ = self.named[name]
        return self.tile[:, o:o + w]

    def free(self, name):
        o, w, _, _ = self.named.pop(name)
        segs = self.free_segs + [(o, w)]
        segs.sort()
        merged = []
        for s in segs:
            if merged and merged[-1][0] + merged[-1][1] == s[0]:
                merged[-1] = (merged[-1][0], merged[-1][1] + s[1])
            else:
                merged.append(list(s) if False else (s[0], s[1]))
        self.free_segs = [(a, b) for (a, b) in merged]


class Builder:
    def __init__(self, debug_taps=()):
        self.nc = bass.Bass("TRN2", target_bir_lowering=False)
        self.debug_taps = list(debug_taps)
        self.taps = []
        self._ctx = []
        self.P = Plan()

    def sb(self, name, shape, dt):
        h = self.nc.sbuf_tensor(name, list(shape), dt)
        t = h.__enter__()
        self._ctx.append(h)
        return t

    def ps_t(self, name, shape, dt=F32):
        h = self.nc.psum_tensor(name, list(shape), dt)
        t = h.__enter__()
        self._ctx.append(h)
        return t

    def sem(self, name):
        h = self.nc.semaphore(name)
        s = h.__enter__()
        self._ctx.append(h)
        return s

    def dram(self, name, shape, dt, kind=None):
        if kind:
            return self.nc.dram_tensor(name, list(shape), dt, kind=kind).ap()
        return self.nc.dram_tensor(name, list(shape), dt).ap()

    def finalize(self):
        for h in reversed(self._ctx):
            h.__exit__(None, None, None)


def build_program(debug_taps=(), n_ffn_layers=6):
    B = Builder(debug_taps)
    nc, P = B.nc, B.P

    XH = B.dram("x", [B_CORE, L, 1024], F32, kind="ExternalInput")
    WD = {}
    for k, s in (("W_dkv", [8, 128, 256]), ("W_uk", [2, 128, 512]),
                 ("W_uv", [2, 128, 512]), ("W_dq", [8, 128, 512]),
                 ("W_uq", [4, 128, 512]), ("W_qr", [4, 128, 256]),
                 ("W_kr", [8, 128, 32]), ("W_out", [4, 128, 1024]),
                 ("idx_wq", [8, 128, 256]), ("idx_wk", [8, 128, 64]),
                 ("idx_w", [128, 4]), ("raw_delta", [1, 16])):
        WD[k] = B.dram("mla_" + k, s, F32, kind="ExternalInput")
    for k, s in (("wqkv", [6, 8, 128, 3072]), ("wo", [6, 8, 128, 1024]),
                 ("w1", [6, 22, 2, 8, 128, 128]), ("w2", [6, 8, 22, 128, 128]),
                 ("b1a", [6, 128, 22]), ("b1g", [6, 128, 22]),
                 ("b2", [6, 128, 8]), ("lng", [6, 128, 8]), ("lnb", [6, 128, 8]),
                 ("qs", [6, 128, 1]), ("ks", [6, 128, 1])):
        WD[k] = B.dram("ffn_" + k, s, F32, kind="ExternalInput")
    CD = {k: B.dram("c_" + k, s, d, kind="ExternalInput")
          for k, (s, d) in CONST_SPECS.items()}
    TD = {k: B.dram("t_" + k, [128, 2048], BF16, kind="ExternalInput")
          for k in TABLE_NAMES}
    OUT = B.dram("o", [B_CORE, L, 1024], F32, kind="ExternalOutput")
    KVMU = B.dram("kvmu", [L, 544], BF16)
    VDR = B.dram("vdr", [L, 512], BF16)
    X2D = B.dram("x2d", [NCH, 128, L], F32)

    CS = {k: B.sb("s_" + k, s, d) for k, (s, d) in CONST_SPECS.items()}
    XI = [B.sb(f"xi{c}", [128, TCOL], BF16) for c in range(NCH)]
    XNs = Slab(B.sb("xn_slab", [128, 16384], BF16), 16384)
    TBL = Slab(B.sb("tbl_slab", [128, 4096], BF16), 4096)
    QB = B.sb("qb_slab", [128, 16384], BF16)
    KBS = B.sb("kb_slab", [128, 16384], BF16)
    WSL = Slab(B.sb("w_slab", [128, 24576], BF16), 24576)
    BSt = B.sb("bst", [128, 12], F32)
    MV = B.sb("mv", [128, 2], F32)
    RST = B.sb("rst", [128, 2], F32)
    M8 = B.sb("m8t", [128, 8], F32)
    TOPV = B.sb("topv", [128, 32], F32)
    TOPI = B.sb("topi", [128, 32], U32)
    TOPF = B.sb("topf", [128, 32], F32)
    SMASK = B.sb("smask", [128, 32], F32)
    IDXW = B.sb("idxw", [128, 4], F32)
    ZROW = B.sb("zrow", [128, 16], F32)
    NMX = B.sb("nmx", [128, 16], F32)
    SLG = B.sb("slg", [6, 128, 1], F32) if False else None
    QSC = B.sb("qsc", [128, 2], F32)
    LNP = B.sb("lnp", [128, 16], F32)
    B1A = B.sb("b1at", [128, 22], F32)
    B1G = B.sb("b1gt", [128, 22], F32)
    B2T = B.sb("b2t", [128, 8], F32)
    SROW = B.sb("srow", [1, 512], F32)
    SROW2 = B.sb("srow2", [1, 512], F32)
    SROWB = B.sb("srowb", [1, 1024], BF16)
    ONER = B.sb("oner", [1, 128], BF16)
    ONESB = B.sb("onesb", [128, 2], BF16)
    CCOL = B.sb("ccol", [128, 8], F32)
    DCOL = B.sb("dcol", [128, 2], F32)

    PS = [B.ps_t(f"ps{i}", [128, 512]) for i in range(8)]
    PSB = [p.ap().bitcast(BF16) if False else p for p in PS]

    sems = {k: B.sem("sm_" + k) for k in
            ("tensor", "vector", "scalar", "gpsimd", "sync", "dmaS", "dmaG", "dmaC")}

    dmaS = lambda fn: P.dma("sync", fn)
    dmaG = lambda fn: P.dma("gpsimd", fn)
    t_op = lambda fn, sig=False: P.op("tensor", fn, sig)
    v_op = lambda fn, sig=False: P.op("vector", fn, sig)
    s_op = lambda fn, sig=False: P.op("scalar", fn, sig)
    g_op = lambda fn, sig=False: P.op("gpsimd", fn, sig)

    def psbf(i):
        return PS[i][:].bitcast(BF16)

    TAPS = []
    _tapseen = {}
    def tap(name, ap_fn, shape, dt=F32):
        if name not in B.debug_taps:
            return
        if name in _tapseen:
            return
        _tapseen[name] = 1
        d = B.dram("tap_" + name, shape, dt, kind="ExternalOutput")
        P.barrier()
        dmaS(lambda e, d=d, f=ap_fn: e.dma_start(out=d, in_=f()))
        P.barrier()
        TAPS.append(name)

    itoks = [dmaS(lambda e, a=CS[k], d=CD[k]: e.dma_start(out=a[:], in_=d))
             for k in CONST_SPECS]
    itoks.append(dmaG(lambda e: e.dma_start(out=IDXW[:], in_=WD["idx_w"])))
    for t in itoks:
        for e in ENGS:
            P.wait(e, t)
    v_op(lambda e: e.memset(ONESB[:], 1.0))
    v_op(lambda e: e.memset(CCOL[:, 0:1], 1e-5))
    v_op(lambda e: e.memset(CCOL[:, 1:2], 1.0))
    v_op(lambda e: e.memset(CCOL[:, 2:3], -1.0))
    v_op(lambda e: e.memset(CCOL[:, 3:4], S96))
    v_op(lambda e: e.memset(CCOL[:, 4:5], 8.0))
    v_op(lambda e: e.memset(CCOL[:, 5:6], -0.5))
    v_op(lambda e: e.memset(CCOL[:, 6:7], float(np.pi)))
    v_op(lambda e: e.memset(CCOL[:, 7:8], float(-np.pi / 2)))
    v_op(lambda e: e.memset(ONER[:], 1.0))
    for c in range(NCH):
        v_op(lambda e, a=XI[c]: e.memset(a[:, 0:1], 0.0))
        v_op(lambda e, a=XI[c]: e.memset(a[:, 2049:TCOL], 0.0))
    P.barrier()
    IDENT = CS["IDENT"]

    def proj_fm(lhs_fn, srcs, n_oc, evac, T=2048, tgw=512, banks=(0, 1),
                ocw=128):
        pend = {i: [] for i in banks}
        k = 0
        for oc in range(n_oc):
            for tg in range(T // tgw):
                bk = banks[k % len(banks)]
                k += 1
                for tk in pend[bk]:
                    P.wait("tensor", tk)
                pend[bk] = []
                n = len(srcs)
                for c in range(n):
                    tok = t_op(lambda e, w=lhs_fn(c, oc), s=srcs[c],
                               t0=tg * tgw, tw=tgw, p=PS[bk],
                               st=(c == 0), sp=(c == n - 1), ocw=ocw:
                               e.matmul(p[0:ocw, 0:tw], w, s[:, t0:t0 + tw],
                                        start=st, stop=sp), sig=(c == n - 1))
                P.wait("vector", tok)
                P.wait("scalar", tok)
                pend[bk] = evac(oc, tg * tgw, tgw, bk)
        P.barrier()


    def phase_delta():
        DSCR = XNs.tile[:, 0:256].bitcast(F32)
        ld = dmaS(lambda e: e.dma_start(out=DSCR[0:1, 0:16], in_=WD["raw_delta"]))
        mz = v_op(lambda e: e.memset(DSCR[:, 16:128], 0.0), sig=True)
        P.wait("scalar", ld)
        P.wait("scalar", mz)
        s1 = s_op(lambda e: e.activation(out=DSCR[0:1, 16:32], in_=DSCR[0:1, 0:16],
                                         func=AF.Sigmoid), sig=True)
        P.wait("scalar", s1)
        s2 = s_op(lambda e: e.activation(out=DSCR[0:1, 32:48], in_=DSCR[0:1, 16:32],
                                         func=AF.Sin, scale=CCOL[0:1, 6:7],
                                         bias=CCOL[0:1, 7:8]), sig=True)
        P.wait("scalar", s2)
        P.wait("vector", s2)
        s3 = s_op(lambda e: e.activation(out=DSCR[0:1, 48:64], in_=DSCR[0:1, 32:48],
                                         func=AF.Square), sig=True)
        P.wait("vector", s3)
        v1 = v_op(lambda e: e.tensor_scalar(DSCR[0:1, 64:80], DSCR[0:1, 48:64],
                                            2.0, -1.0, op0=ALU.mult, op1=ALU.add),
                  sig=True)
        v2 = v_op(lambda e: e.tensor_scalar(DSCR[0:1, 80:96], DSCR[0:1, 48:64],
                                            -1.0, 1.0, op0=ALU.mult, op1=ALU.add),
                  sig=True)
        P.wait("scalar", v2)
        s4 = s_op(lambda e: e.activation(out=DSCR[0:1, 96:112], in_=DSCR[0:1, 80:96],
                                         func=AF.Sqrt), sig=True)
        P.wait("vector", s4)
        v3 = v_op(lambda e: e.tensor_mul(DSCR[0:1, 112:128], DSCR[0:1, 96:112],
                                         DSCR[0:1, 32:48]), sig=True)
        P.wait("vector", v3)
        v4 = v_op(lambda e: e.tensor_scalar_mul(DSCR[0:1, 112:128],
                                                DSCR[0:1, 112:128], 2.0), sig=True)
        P.wait("vector", v1)
        P.wait("vector", v4)
        v5 = v_op(lambda e: e.tensor_copy(DSCR[0:1, 0:16], DSCR[0:1, 64:80]), sig=True)
        v6 = v_op(lambda e: e.tensor_copy(DSCR[0:1, 16:32], DSCR[0:1, 112:128]), sig=True)
        P.wait("tensor", v5)
        P.wait("tensor", v6)
        tt = t_op(lambda e: e.transpose(PS[5][:, 0:128], DSCR[:, 0:128],
                                        CS["IDF32"][:]), sig=True)
        P.wait("vector", tt)
        ev = v_op(lambda e: e.tensor_copy(DCOL[0:32, 0:1],
                                          PS[5][0:32, 0:1]), sig=True)
        P.wait("gpsimd", ev)
        g1 = dmaG(lambda e: e.dma_start(out=DCOL[0:16, 1:2], in_=DCOL[16:32, 0:1]))
        P.wait("gpsimd", g1)
        for b0 in (16, 32, 64):
            g2 = dmaG(lambda e, b0=b0: e.dma_start(
                out=DCOL[b0:2 * b0, 0:2], in_=DCOL[0:b0, 0:2]))
            P.wait("gpsimd", g2)
        for en in ENGS:
            P.wait(en, g2)


    def phase_ln1(b, xn_v):
        XTa = TBL.alloc("xt", 2048, F32)
        XNT = QB[:, 0:2048]
        prev_tp = [None, None]
        prev_use = [None, None]
        for j in range(16):
            o = j % 2
            xt = XTa[:, o * 1024:(o + 1) * 1024]
            if prev_use[o] is not None:
                for tk in prev_use[o]:
                    P.wait("sync", tk)
            ld = dmaS(lambda e, b=b, j=j, xt=xt: e.dma_start(
                out=xt, in_=XH[b, 128 * j:128 * j + 128, :]))
            P.wait("vector", ld)
            if prev_tp[o] is not None:
                P.wait("vector", prev_tp[o])
            v1 = v_op(lambda e, xt=xt: e.bn_stats(BSt[:, 0:6], xt[:, 0:512]), sig=True)
            v2 = v_op(lambda e, xt=xt: e.bn_stats(BSt[:, 6:12], xt[:, 512:1024]), sig=True)
            P.wait("vector", v1)
            P.wait("vector", v2)
            v3 = v_op(lambda e: e.bn_aggr(MV[:], BSt[:]), sig=True)
            P.wait("scalar", v3)
            s1 = s_op(lambda e: e.activation(out=RST[:, 0:1], in_=MV[:, 1:2],
                                             func=AF.Sqrt, bias=CCOL[:, 0:1]), sig=True)
            P.wait("vector", s1)
            v4 = v_op(lambda e: e.reciprocal(RST[:, 1:2], RST[:, 0:1]), sig=True)
            v5 = v_op(lambda e: e.tensor_scalar_mul(MV[:, 0:1], MV[:, 0:1], -1.0), sig=True)
            P.wait("vector", v4)
            P.wait("vector", v5)
            xnt = XNT[:, o * 1024:(o + 1) * 1024]
            v6 = v_op(lambda e, xt=xt, xnt=xnt: e.tensor_scalar(
                xnt, xt, MV[:, 0:1], RST[:, 1:2], op0=ALU.add, op1=ALU.mult),
                sig=True)
            P.wait("tensor", v6)
            last_ev = None
            last_tt2 = None
            for c in range(NCH):
                pb = 6 + (c % 2)
                tt = t_op(lambda e, xnt=xnt, c=c, pb=pb: e.transpose(
                    psbf(pb)[:, 0:128], xnt[:, 128 * c:128 * c + 128], IDENT[:]),
                    sig=True)
                P.wait("vector", tt)
                last_ev = v_op(lambda e, c=c, j=j, pb=pb: e.tensor_copy(
                    xn_v[c][:, 128 * j:128 * j + 128], psbf(pb)[:, 0:128]), sig=True)
                P.wait("tensor", last_ev)
                tt2 = t_op(lambda e, xt=xt, c=c: e.transpose(
                    PS[5][:, 0:128], xt[:, 128 * c:128 * c + 128],
                    CS["IDF32"][:]), sig=True)
                P.wait("vector", tt2)
                last_ev = v_op(lambda e, c=c, j=j: e.tensor_copy(
                    XI[c][:, 1 + 128 * j:1 + 128 * j + 128], PS[5][:, 0:128]),
                    sig=True)
                P.wait("tensor", last_ev)
                last_tt2 = tt2
            prev_tp[o] = last_ev
            prev_use[o] = (v6, last_tt2)
        TBL.free("xt")
        P.barrier()

    def phase_mla_proj(b, xn_v):
        segs = [("WOUT", 4096), ("WUV", 1024), ("WDQ", 4096), ("WUQ", 2048),
                ("WQR", 1024), ("WDKV", 2048), ("WUK", 1024), ("WKR", 256)]
        V = {}
        for nm, cols in segs:
            V[nm] = WSL.alloc(nm, cols)
        wmap = {"WOUT": "W_out", "WUV": "W_uv", "WDKV": "W_dkv", "WUK": "W_uk",
                "WKR": "W_kr", "WDQ": "W_dq", "WUQ": "W_uq", "WQR": "W_qr"}
        wt = []
        for nm, _ in segs:
            d0 = WD[wmap[nm]]
            nch_, _, nn = d0.shape
            wt.append(dmaG(lambda e, d=d0, a=V[nm], c=nch_, n=nn: e.dma_start(
                out=a.rearrange("p (c n) -> p c n", c=c),
                in_=d.transpose([1, 0, 2]))))
        for t in wt:
            P.wait("tensor", t)
        P.barrier()

        CKV = WSL.alloc("CKV", 4096)
        ckv = [CKV[:, 0:2048], CKV[:, 2048:4096]]

        def ev_ckv(oc, t0, tw, bk):
            return [v_op(lambda e, oc=oc, t0=t0, tw=tw, p=PS[bk]: e.tensor_copy(
                ckv[oc][:, t0:t0 + tw], p[:, 0:tw]), sig=True)]
        proj_fm(lambda c, oc: V["WDKV"][:, c * 256 + oc * 128:c * 256 + oc * 128 + 128],
                xn_v, 2, ev_ckv)
        tap("ckv0", lambda: ckv[0][:], [128, 2048], BF16)

        for h in range(8):
            v_op(lambda e, h=h: e.memset(KBS[64:128, 2048 * h:2048 * (h + 1)], 0.0))
        def ev_kfm(oc, t0, tw, bk):
            t1 = v_op(lambda e, h=2 * oc, t0=t0, tw=tw, p=PS[bk]: e.tensor_copy(
                KBS[0:64, 2048 * h + t0:2048 * h + t0 + tw], p[0:64, 0:tw]), sig=True)
            t2 = v_op(lambda e, h=2 * oc + 1, t0=t0, tw=tw, p=PS[bk]: e.tensor_copy(
                KBS[0:64, 2048 * h + t0:2048 * h + t0 + tw], p[64:128, 0:tw]), sig=True)
            return [t1, t2]
        proj_fm(lambda c, oc: V["WUK"][:, c * 512 + oc * 128:c * 512 + oc * 128 + 128],
                ckv, 4, ev_kfm)

        STG = WSL.alloc("STG", 1280)
        vtt, ktt, ctt = STG[:, 0:512], STG[:, 512:1024], STG[:, 1024:1280]
        pdma = []
        prev_ev = []
        for j in range(16):
            for t in pdma:
                P.wait("vector", t)
            for t in prev_ev:
                P.wait("tensor", t)
            pdma = []
            toks = []
            for tgt, ps_i, rhs, nsrc, lhs_src in (
                    ("v", 2, V["WUV"], 2, ckv), ("k", 3, V["WUK"], 2, ckv)):
                for c in range(nsrc):
                    tok = t_op(lambda e, c=c, j=j, p=PS[ps_i], rhs=rhs,
                               st=(c == 0), sp=(c == nsrc - 1), ls=lhs_src:
                               e.matmul(p[:], ls[c][:, 128 * j:128 * j + 128],
                                        rhs[:, 512 * c:512 * c + 512],
                                        start=st, stop=sp), sig=(c == nsrc - 1))
                toks.append(tok)
            for t in toks:
                P.wait("vector", t)
            e1 = v_op(lambda e: e.tensor_copy(vtt[:], PS[2][:]), sig=True)
            e2 = v_op(lambda e: e.tensor_copy(ktt[:], PS[3][:]), sig=True)
            prev_ev = [e1, e2]
            P.wait("sync", e1)
            P.wait("sync", e2)
            d1 = dmaS(lambda e, j=j: e.dma_start(
                out=VDR[128 * j:128 * j + 128, :], in_=vtt[:]))
            d2 = dmaS(lambda e, j=j: e.dma_start(
                out=KVMU[128 * j:128 * j + 128, 0:512], in_=ktt[:]))
            pdma = [d1, d2]
        for t in pdma:
            P.wait("vector", t)
        P.barrier()

        tq_c = TBL.alloc("tq_c", 2048)
        tq_s = TBL.alloc("tq_s", 2048)
        lt1 = dmaS(lambda e: e.dma_start(out=tq_c, in_=TD["TQS_C"]))
        lt2 = dmaS(lambda e: e.dma_start(out=tq_s, in_=TD["TQS_S"]))
        MK = WSL.alloc("MK", 2048)
        TMPK = QB[:, 2048:4096]
        TMPK2 = QB[:, 4096:6144]
        v_op(lambda e: e.memset(MK[64:128, :], 0.0))
        for tg in range(4):
            toks = []
            for c in range(NCH):
                tok = t_op(lambda e, c=c, tg=tg: e.matmul(
                    PS[0][0:32, 0:512], V["WKR"][:, 32 * c:32 * c + 32],
                    xn_v[c][:, 512 * tg:512 * tg + 512],
                    start=(c == 0), stop=(c == NCH - 1)), sig=(c == NCH - 1))
            P.wait("scalar", tok)
            P.wait("vector", tok)
            a1 = s_op(lambda e: e.activation(out=PS[1][0:32, 0:512],
                                             in_=PS[0][0:32, 0:512], func=AF.Abs), sig=True)
            P.wait("scalar", a1)
            a2 = s_op(lambda e: e.activation(out=PS[2][0:32, 0:512],
                                             in_=PS[1][0:32, 0:512], func=AF.Exp,
                                             scale=CCOL[0:32, 2:3]), sig=True)
            P.wait("scalar", a2)
            a3 = s_op(lambda e, tg=tg: e.activation(out=QB[0:32, 0:512],
                                                    in_=PS[2][0:32, 0:512], func=AF.Ln,
                                                    bias=CCOL[0:32, 1:2]), sig=True)
            P.wait("vector", a3)
            v1 = v_op(lambda e, tg=tg: e.scalar_tensor_tensor(
                MK[0:32, 512 * tg:512 * tg + 512], PS[0][0:32, 0:512], 0.0,
                QB[0:32, 0:512], op0=ALU.max, op1=ALU.add), sig=True)
            P.wait("tensor", v1)
            P.wait("scalar", v1)
        P.wait("vector", lt1)
        P.wait("vector", lt2)
        P.wait("tensor", lt1)
        mz = v_op(lambda e: e.memset(MK[32:64, :], 0.0), sig=True)
        P.wait("tensor", mz)
        for tg in range(4):
            tok = t_op(lambda e, tg=tg: e.matmul(
                PS[3][:], CS["P_ROT32"][:], MK[:, 512 * tg:512 * tg + 512],
                start=True, stop=True), sig=True)
            P.wait("vector", tok)
            d1 = v_op(lambda e, tg=tg: e.tensor_scalar_mul(
                TMPK[0:32, 512 * tg:512 * tg + 512],
                MK[0:32, 512 * tg:512 * tg + 512], DCOL[0:32, 0:1]), sig=True)
            P.wait("vector", d1)
            d2 = v_op(lambda e, tg=tg: e.scalar_tensor_tensor(
                MK[0:32, 512 * tg:512 * tg + 512], PS[3][0:32, 0:512],
                DCOL[0:32, 1:2], TMPK[0:32, 512 * tg:512 * tg + 512],
                op0=ALU.mult, op1=ALU.add), sig=True)
            P.wait("tensor", d2)
        for tg in range(4):
            tok = t_op(lambda e, tg=tg: e.matmul(
                PS[3][:], CS["P_ROT32"][:], MK[:, 512 * tg:512 * tg + 512],
                start=True, stop=True), sig=True)
            P.wait("vector", tok)
            v1 = v_op(lambda e, tg=tg: e.tensor_tensor(
                TMPK[0:32, 512 * tg:512 * tg + 512], MK[0:32, 512 * tg:512 * tg + 512],
                tq_c[0:32, 512 * tg:512 * tg + 512], op=ALU.mult), sig=True)
            v2 = v_op(lambda e, tg=tg: e.tensor_tensor(
                TMPK2[0:32, 512 * tg:512 * tg + 512],
                PS[3][0:32, 0:512],
                tq_s[0:32, 512 * tg:512 * tg + 512], op=ALU.mult), sig=True)
            P.wait("vector", v1)
            P.wait("vector", v2)
            v3 = v_op(lambda e, tg=tg: e.tensor_tensor(
                KBS[64:96, 512 * tg:512 * tg + 512], TMPK2[0:32, 512 * tg:512 * tg + 512],
                TMPK[0:32, 512 * tg:512 * tg + 512], op=ALU.add), sig=True)
            P.wait("tensor", v3)
        P.wait("vector", v3)
        ctk = []
        for h in range(1, 8):
            ctk.append(v_op(lambda e, h=h: e.tensor_copy(
                KBS[64:96, 2048 * h:2048 * h + 2048], KBS[64:96, 0:2048]), sig=True))
        mut = STG[:, 0:512]
        for j in range(16):
            tt = t_op(lambda e, j=j: e.transpose(
                psbf(5)[:, 0:128], MK[:, 128 * j:128 * j + 128], IDENT[:]), sig=True)
            P.wait("vector", tt)
            ev = v_op(lambda e, j=j: e.tensor_copy(
                mut[:, 0:32], psbf(5)[:, 0:32]), sig=True)
            P.wait("sync", ev)
            P.wait("tensor", ev)
            dd = dmaS(lambda e, j=j: e.dma_start(
                out=KVMU[128 * j:128 * j + 128, 512:544], in_=mut[:, 0:32]))
            P.wait("tensor", dd)
            P.wait("vector", dd)
        for t in ctk:
            P.wait("tensor", t)
        P.barrier()
        tap("kb8_0", lambda: KBS[:, 0:2048], [128, 2048], BF16)
        WSL.free("MK"); WSL.free("WKR"); WSL.free("WDKV"); WSL.free("WUK")
        WSL.free("CKV"); WSL.free("STG")
        TBL.free("tq_c"); TBL.free("tq_s")

        CQ = WSL.alloc("CQ", 8192)
        cq = [CQ[:, 2048 * i:2048 * (i + 1)] for i in range(4)]
        def ev_cq(oc, t0, tw, bk):
            return [v_op(lambda e, oc=oc, t0=t0, tw=tw, p=PS[bk]: e.tensor_copy(
                cq[oc][:, t0:t0 + tw], p[:, 0:tw]), sig=True)]
        proj_fm(lambda c, oc: V["WDQ"][:, c * 512 + oc * 128:c * 512 + oc * 128 + 128],
                xn_v, 4, ev_cq)

        MQ = WSL.alloc("MQ", 4096)
        mq = [MQ[:, 0:2048], MQ[:, 2048:4096]]
        spq2 = TBL.tile[:, 2048:3072]
        spq = TBL.alloc("spq", 1024, F32)
        def ev_qr(oc, t0, tw, bk):
            a1 = s_op(lambda e, p=PS[bk], tw=tw: e.activation(
                out=PS[2][:, 0:tw], in_=p[:, 0:tw], func=AF.Abs), sig=True)
            P.wait("scalar", a1)
            a2 = s_op(lambda e, tw=tw: e.activation(
                out=PS[3][:, 0:tw], in_=PS[2][:, 0:tw], func=AF.Exp,
                scale=CCOL[:, 2:3]), sig=True)
            P.wait("scalar", a2)
            a3 = s_op(lambda e, tw=tw: e.activation(
                out=spq[:, 0:tw], in_=PS[3][:, 0:tw], func=AF.Ln,
                bias=CCOL[:, 1:2]), sig=True)
            P.wait("vector", a3)
            v1 = v_op(lambda e, oc=oc, t0=t0, tw=tw, p=PS[bk]: e.scalar_tensor_tensor(
                mq[oc][:, t0:t0 + tw], p[:, 0:tw], 0.0, spq[:, 0:tw],
                op0=ALU.max, op1=ALU.add), sig=True)
            P.wait("scalar", v1)
            return [v1]
        proj_fm(lambda c, oc: V["WQR"][:, c * 256 + oc * 128:c * 256 + oc * 128 + 128],
                cq, 2, ev_qr, banks=(0, 1))
        WSL.free("WQR")
        TBL.free("spq")
        for ch in range(2):
            for tg in range(4):
                sl = slice(512 * tg, 512 * tg + 512)
                tok = t_op(lambda e, ch=ch, sl=sl: e.matmul(
                    PS[3][:], CS["P_ROT32"][:], mq[ch][:, sl],
                    start=True, stop=True), sig=True)
                P.wait("vector", tok)
                d1 = v_op(lambda e, ch=ch, sl=sl: e.tensor_scalar_mul(
                    spq2[:, 0:512], mq[ch][:, sl], DCOL[:, 0:1]), sig=True)
                P.wait("vector", d1)
                d2 = v_op(lambda e, ch=ch, sl=sl: e.scalar_tensor_tensor(
                    mq[ch][:, sl], PS[3][:], DCOL[:, 1:2], spq2[:, 0:512],
                    op0=ALU.mult, op1=ALU.add), sig=True)
                P.wait("tensor", d2)
                P.wait("vector", d2)
        tap("mq0rot", lambda: mq[0][:], [128, 2048], BF16)

        for h in range(8):
            v_op(lambda e, h=h: e.memset(QB[64:128, 2048 * h:2048 * (h + 1)], 0.0))
        def ev_qc(oc, t0, tw, bk):
            t1 = v_op(lambda e, h=2 * oc, t0=t0, tw=tw, p=PS[bk]: e.tensor_copy(
                QB[0:64, 2048 * h + t0:2048 * h + t0 + tw], p[0:64, 0:tw]), sig=True)
            t2 = v_op(lambda e, h=2 * oc + 1, t0=t0, tw=tw, p=PS[bk]: e.tensor_copy(
                QB[0:64, 2048 * h + t0:2048 * h + t0 + tw], p[64:128, 0:tw]), sig=True)
            return [t1, t2]
        proj_fm(lambda c, oc: V["WUQ"][:, c * 512 + oc * 128:c * 512 + oc * 128 + 128],
                cq, 4, ev_qc)
        WSL.free("CQ"); WSL.free("WDQ"); WSL.free("WUQ")
        tap("qb8_0", lambda: QB[:, 0:2048], [128, 2048], BF16)

        TMPB = WSL.alloc("TMPB", 4096)
        tq_c = TBL.alloc("tq_c", 2048)
        tq_s = TBL.alloc("tq_s", 2048)

        def build_rot(tc_name, ts_name, dst_fn):
            l1 = dmaS(lambda e: e.dma_start(out=tq_c, in_=TD[tc_name]))
            l2 = dmaS(lambda e: e.dma_start(out=tq_s, in_=TD[ts_name]))
            P.wait("vector", l1)
            P.wait("vector", l2)
            for ch in range(2):
                dst = dst_fn(ch)
                for tg in range(4):
                    sl = slice(512 * tg, 512 * tg + 512)
                    tok = t_op(lambda e, ch=ch, sl=sl: e.matmul(
                        PS[3][:], CS["P_ROT32"][:], mq[ch][:, sl],
                        start=True, stop=True), sig=True)
                    P.wait("vector", tok)
                    v1 = v_op(lambda e, ch=ch, sl=sl: e.tensor_tensor(
                        TMPB[:, sl], mq[ch][:, sl], tq_c[:, sl], op=ALU.mult),
                        sig=True)
                    v2 = v_op(lambda e, sl=sl, dst=dst: e.tensor_tensor(
                        dst[:, sl], PS[3][:, 0:512], tq_s[:, sl], op=ALU.mult),
                        sig=True)
                    P.wait("vector", v1)
                    P.wait("vector", v2)
                    v3 = v_op(lambda e, sl=sl, dst=dst: e.tensor_tensor(
                        dst[:, sl], dst[:, sl], TMPB[:, sl], op=ALU.add), sig=True)
                    P.wait("tensor", v3)
            P.barrier()

        QS = WSL.alloc("QS", 4096, top=True)
        build_rot("TQB_C", "TQB_S", lambda ch: QS[:, 2048 * ch:2048 * (ch + 1)])
        for h in range(8):
            src = QS[:, 2048 * (h // 4):2048 * (h // 4) + 2048]
            r0 = 32 * (h % 4)
            dg = dmaG(lambda e, h=h, src=src, r0=r0: e.dma_start(
                out=QB[64:96, 2048 * h:2048 * h + 2048], in_=src[r0:r0 + 32, :]))
            P.wait("tensor", dg)
            P.wait("vector", dg)
        P.barrier()
        build_rot("TQS_C", "TQS_S", lambda ch: QS[:, 2048 * ch:2048 * (ch + 1)])
        WSL.free("MQ"); WSL.free("TMPB")

        TBL.free("tq_c"); TBL.free("tq_s")
        tap("qb8_full_0", lambda: QB[:, 0:2048], [128, 2048], BF16)
        tap("qhs_0", lambda: QS[:, 0:2048], [128, 2048], BF16)

        WIQ = WSL.alloc("WIQ", 2048)
        WIK = WSL.alloc("WIK", 512)
        l1 = dmaG(lambda e: e.dma_start(
            out=WIQ.rearrange("p (c n) -> p c n", c=8),
            in_=WD["idx_wq"].transpose([1, 0, 2])))
        l2 = dmaG(lambda e: e.dma_start(
            out=WIK.rearrange("p (c n) -> p c n", c=8),
            in_=WD["idx_wk"].transpose([1, 0, 2])))
        P.wait("tensor", l1)
        P.wait("tensor", l2)
        QI = WSL.alloc("QI", 4096, top=True)
        qi = [QI[:, 0:2048], QI[:, 2048:4096]]
        def ev_qi(oc, t0, tw, bk):
            return [v_op(lambda e, oc=oc, t0=t0, tw=tw, p=PS[bk]: e.tensor_copy(
                qi[oc][:, t0:t0 + tw], p[:, 0:tw]), sig=True)]
        proj_fm(lambda c, oc: WIQ[:, c * 256 + oc * 128:c * 256 + oc * 128 + 128],
                xn_v, 2, ev_qi)
        KI = WSL.alloc("KI", 2048, top=True)
        def ev_ki(oc, t0, tw, bk):
            t1 = v_op(lambda e, t0=t0, tw=tw, p=PS[bk]: e.tensor_copy(
                KI[0:64, t0:t0 + tw], p[0:64, 0:tw]), sig=True)
            t2 = v_op(lambda e, t0=t0, tw=tw, p=PS[bk]: e.tensor_copy(
                KI[64:128, t0:t0 + tw], p[0:64, 0:tw]), sig=True)
            return [t1, t2]
        proj_fm(lambda c, oc: WIK[:, c * 64:c * 64 + 64], xn_v, 1, ev_ki, ocw=64)
        WSL.free("WIQ"); WSL.free("WIK")
        tap("qi0", lambda: qi[0][:], [128, 2048], BF16)
        tap("qi1", lambda: qi[1][:], [128, 2048], BF16)
        tap("idxw", lambda: IDXW[:], [128, 4], F32)
        tap("ki", lambda: KI[:], [128, 2048], BF16)
        P.barrier()
        return {"QS": QS, "QI": QI, "KI": KI, "V": V}

    def phase_mla_attend(b, H):
        QSv, QIv, KIv = H["QS"], H["QI"], H["KI"]
        V = H["V"]
        GKV = WSL.alloc("GKV", 6400)
        QCTM = WSL.alloc("QCTM", 512)
        QSTMt = WSL.alloc("QSTMt", 256)
        QFUT = WSL.alloc("QFUT", 1024)
        VST = WSL.alloc("VST", 1024)
        ITA = XNs.alloc("itA", 4096)
        ITB = XNs.alloc("itB", 4096)
        IT = [ITA.bitcast(F32), ITB.bitcast(F32)]
        SFUT8 = IT[0]
        AFUT8 = ITB[:, 0:2048]
        SELACC = ITB[:, 2048:4096].bitcast(F32)
        SBND8 = XNs.alloc("sbnd8", 2048, F32)
        ABND8 = XNs.alloc("abnd8", 2048)
        SSEL = XNs.alloc("ssel", 256, F32)
        ASEL8 = XNs.alloc("asel8", 256)
        ATP = XNs.alloc("atp", 512)
        PRD = XNs.alloc("prd", 256)
        PRD8 = XNs.raw("sbnd8")
        AB_RAW = XNs.raw("abnd8")
        P3R = AB_RAW[:, 0:768]
        GMR = ATP[:, 0:256]
        ATM = AB_RAW[:, 0:512]
        AFM = AB_RAW[:, 512:1024]

        v_op(lambda e: e.memset(GKV[:], 0.0))
        for h in range(8):
            v_op(lambda e, h=h: e.memset(QFUT[96:128, 128 * h:128 * h + 128], 0.0))
            v_op(lambda e, h=h: e.tensor_copy(
                QFUT[0:64, 128 * h:128 * h + 128], QB[0:64, 2048 * h:2048 * h + 128]))
            dg = dmaG(lambda e, h=h: e.dma_start(
                out=QFUT[64:96, 128 * h:128 * h + 128],
                in_=QSv[32 * (h % 4):32 * (h % 4) + 32,
                        2048 * (h // 4):2048 * (h // 4) + 128]))
            P.wait("tensor", dg)
        P.barrier()

        def emit_tau(tau):
            q0 = 128 * tau
            RW = 64 + 128 * tau
            cur = IT[0]
            nrc = (RW + 511) // 512
            for rc in range(nrc):
                c0 = 512 * rc
                cw = min(512, RW - c0)
                mtk = []
                for h in range(4):
                    r0 = 64 * (h % 2)
                    mtk.append(t_op(lambda e, h=h, r0=r0, c0=c0, cw=cw, q0=q0: e.matmul(
                        PS[h][0:128, 0:cw],
                        QIv[r0:r0 + 64, 2048 * (h // 2) + q0:2048 * (h // 2) + q0 + 128],
                        KIv[r0:r0 + 64, c0:c0 + cw],
                        start=True, stop=True), sig=True))
                for t in mtk:
                    P.wait("scalar", t)
                rtk = []
                for h in range(4):
                    rtk.append(s_op(lambda e, h=h, cw=cw: e.activation(
                        out=PS[4 + h][:, 0:cw], in_=PS[h][:, 0:cw], func=AF.Relu),
                        sig=True))
                for t in rtk:
                    P.wait("vector", t)
                if tau == 3 and c0 == 0:
                    _dbg = XNs.raw("sbnd8").bitcast(F32)
                    for t in mtk:
                        P.wait("vector", t)
                    dv = v_op(lambda e, cw=cw, _dbg=_dbg: e.tensor_copy(
                        _dbg[:, 0:cw], PS[0][:, 0:cw]), sig=True)
                    P.wait("vector", dv)
                    for t in rtk:
                        P.wait("vector", t)
                    dv2 = v_op(lambda e, cw=cw, _dbg=_dbg: e.tensor_copy(
                        _dbg[:, 512:512 + cw], PS[4][:, 0:cw]), sig=True)
                    P.wait("vector", dv2)
                    tap("dbg_s0_t3", lambda _dbg=_dbg: _dbg, [128, 2048], F32)
                v1 = v_op(lambda e, c0=c0, cw=cw: e.tensor_scalar_mul(
                    cur[:, c0:c0 + cw], PS[4][:, 0:cw], IDXW[:, 0:1]), sig=True)
                P.wait("vector", v1)
                for h in range(1, 4):
                    v1 = v_op(lambda e, h=h, c0=c0, cw=cw: e.scalar_tensor_tensor(
                        cur[:, c0:c0 + cw], PS[4 + h][:, 0:cw], IDXW[:, h:h + 1],
                        cur[:, c0:c0 + cw], op0=ALU.mult, op1=ALU.add), sig=True)
                    P.wait("vector", v1)
                P.wait("tensor", v1)
            if tau == 0:
                vm = v_op(lambda e: e.tensor_tensor(
                    cur[:, 0:64], cur[:, 0:64], CS["M_TRI"][:, 64:128], op=ALU.add),
                    sig=True)
            else:
                vm = v_op(lambda e, RW=RW: e.tensor_tensor(
                    cur[:, RW - 128:RW], cur[:, RW - 128:RW], CS["M_TRI"][:],
                    op=ALU.add), sig=True)
            P.wait("vector", vm)

            if tau == 3:
                _it = IT[0]
                tap("ireg_t3", lambda _it=_it: _it, [128, 2048], F32)
                tap("qi0_t3", lambda: QIv[:, 0:2048], [128, 2048], BF16)
                tap("ki_t3", lambda: KIv[:, 0:2048], [128, 2048], BF16)
            for rd in range(4):
                nxt = IT[(rd + 1) % 2]
                src = IT[rd % 2]
                m1 = v_op(lambda e, src=src, RW=RW: e.max(
                    out=M8[:], in_=src[:, 0:RW]), sig=True)
                P.wait("vector", m1)
                m2 = v_op(lambda e, src=src, rd=rd: e.max_index(
                    out=TOPI[:, 8 * rd:8 * rd + 8].bitcast(U32) if False else TOPI[:, 8 * rd:8 * rd + 8],
                    in_max=M8[:], in_values=src[:, 0:RW]), sig=True)
                m3 = v_op(lambda e, rd=rd: e.tensor_copy(
                    TOPV[:, 8 * rd:8 * rd + 8], M8[:]), sig=True)
                P.wait("vector", m2)
                P.wait("vector", m3)
                if rd < 4 - 1:
                    m4 = v_op(lambda e, src=src, nxt=nxt, RW=RW: e.match_replace(
                        out=nxt[:, 0:RW], in_to_replace=M8[:], in_values=src[:, 0:RW],
                        imm_value=NEGM), sig=True)
                    P.wait("vector", m4)
            k1 = v_op(lambda e: e.tensor_copy(TOPF[:], TOPI[:]), sig=True)
            P.wait("vector", k1)
            k2 = v_op(lambda e: e.tensor_scalar_min(TOPF[:], TOPF[:], 2047.0), sig=True)
            P.wait("vector", k2)
            k3 = v_op(lambda e: e.tensor_copy(TOPI[:], TOPF[:]), sig=True)
            P.wait("vector", k3)
            c2 = v_op(lambda e: e.tensor_scalar(
                SMASK[:], TOPV[:], -1e29, 3.0e38, op0=ALU.is_gt, op1=ALU.bypass),
                sig=True)
            P.wait("vector", c2)
            c3 = v_op(lambda e: e.tensor_scalar(
                SMASK[:], SMASK[:], -1.0, 3.0e38, op0=ALU.add, op1=ALU.mult), sig=True)
            P.wait("gpsimd", c3)

            for h in range(8):
                tt = t_op(lambda e, h=h, q0=q0: e.transpose(
                    psbf(6)[:, 0:128], QB[:, 2048 * h + q0:2048 * h + q0 + 128],
                    IDENT[:]), sig=True)
                P.wait("vector", tt)
                ev = v_op(lambda e, h=h: e.tensor_copy(
                    QCTM[:, 64 * h:64 * h + 64], psbf(6)[:, 0:64]), sig=True)
                P.wait("tensor", ev)
            for ch in range(2):
                tt = t_op(lambda e, ch=ch, q0=q0: e.transpose(
                    psbf(6)[:, 0:128], QSv[:, 2048 * ch + q0:2048 * ch + q0 + 128],
                    IDENT[:]), sig=True)
                P.wait("vector", tt)
                ev = v_op(lambda e, ch=ch: e.tensor_copy(
                    QSTMt[:, 128 * ch:128 * ch + 128], psbf(6)[:, 0:128]), sig=True)
                P.wait("tensor", ev)
            P.barrier()

            G3 = GKV[:, 0:4352].rearrange("p (j e) -> p j e", j=8)
            prev_grp = None
            for g in range(4):
                if prev_grp is not None:
                    P.wait("gpsimd", prev_grp)
                gts = []
                for j in range(8):
                    sj = 8 * g + j
                    gts.append(dmaG(lambda e, sj=sj, j=j: e.indirect_dma_start(
                        out=GKV[:, 544 * j:544 * j + 544], out_offset=None,
                        in_=KVMU,
                        in_offset=IndirectOffsetOnAxis(ap=TOPI[:, sj:sj + 1], axis=0))))
                for t in gts:
                    P.wait("vector", t)
                T1 = CS["SLOT_T1"][:, 256 * g:256 * g + 256].rearrange(
                    "p (j d) -> p j d", j=8)
                T2 = CS["SLOT_T2"][:, 256 * g:256 * g + 256].rearrange(
                    "p (j d) -> p j d", j=8)
                GM3 = GMR[:].rearrange("p (j d) -> p j d", j=8)
                r1 = v_op(lambda e, T1=T1, GM3=GM3: e.tensor_tensor(
                    GM3, G3[:, :, 512:544], T1, op=ALU.mult), sig=True)
                r2 = v_op(lambda e, T2=T2: e.tensor_tensor(
                    PRD[:].rearrange("p (j d) -> p j d", j=8)[:, :, 0:16],
                    G3[:, :, 528:544], T2[:, :, 0:16], op=ALU.mult), sig=True)
                r3 = v_op(lambda e, T2=T2: e.tensor_tensor(
                    PRD[:].rearrange("p (j d) -> p j d", j=8)[:, :, 16:32],
                    G3[:, :, 512:528], T2[:, :, 16:32], op=ALU.mult), sig=True)
                P.wait("vector", r1)
                P.wait("vector", r2)
                P.wait("vector", r3)
                r4 = v_op(lambda e: e.tensor_tensor(
                    GMR[:], GMR[:], PRD[:], op=ALU.add), sig=True)
                P.wait("vector", r4)
                P3 = P3R.rearrange("p (j d) -> p j d", j=8)
                for h in range(8):
                    d1 = v_op(lambda e, h=h: e.tensor_tensor(
                        P3[:, :, 0:64], G3[:, :, 64 * h:64 * h + 64],
                        QCTM[:, 64 * h:64 * h + 64].unsqueeze(1).broadcast_to(
                            [128, 8, 64]),
                        op=ALU.mult), sig=True)
                    d2 = v_op(lambda e, h=h: e.tensor_tensor(
                        P3[:, :, 64:96], GM3,
                        QSTMt[:, 32 * h:32 * h + 32].unsqueeze(1).broadcast_to(
                            [128, 8, 32]),
                        op=ALU.mult), sig=True)
                    P.wait("vector", d1)
                    P.wait("vector", d2)
                    d3 = v_op(lambda e, h=h, g=g: e.tensor_reduce(
                        SSEL[:, 32 * h + 8 * g:32 * h + 8 * g + 8], P3,
                        axis=AX.X, op=ALU.add), sig=True)
                    P.wait("vector", d3)
                prev_grp = d3
            sm = v_op(lambda e: e.tensor_tensor(
                SSEL[:].rearrange("p (h j) -> p h j", h=8),
                SSEL[:].rearrange("p (h j) -> p h j", h=8),
                SMASK[:].unsqueeze(1).broadcast_to([128, 8, 32]),
                op=ALU.add), sig=True)
            P.wait("vector", sm)

            if tau == 3:
                tap("topi_t3", lambda: TOPI[:], [128, 32], U32)
                tap("topv_t3", lambda: TOPV[:], [128, 32], F32)
                tap("ssel_t3", lambda: SSEL[:], [128, 256], F32)
                tap("qctm_t3", lambda: QCTM[:], [128, 512], BF16)
                tap("qstm_t3", lambda: QSTMt[:], [128, 256], BF16)
                tap("gmr_t3", lambda: GMR[:], [128, 256], BF16)
            vl1 = dmaS(lambda e, tau=tau: e.dma_start(
                out=VST[:, 512:1024], in_=VDR[128 * tau:128 * tau + 128, :]))
            if tau > 0:
                vl0 = dmaS(lambda e, tau=tau: e.dma_start(
                    out=VST[:, 0:512], in_=VDR[128 * (tau - 1):128 * tau, :]))
            ws = q0 - 128
            btks = []
            for h in range(8):
                bank, colh = h % 4, 256 * (h // 4)
                if tau == 0:
                    btks.append(t_op(lambda e, h=h, bank=bank, colh=colh, q0=q0: e.matmul(
                        PS[bank][:, colh + 128:colh + 256],
                        QB[:, 2048 * h + q0:2048 * h + q0 + 128],
                        KBS[:, 2048 * h:2048 * h + 128],
                        start=True, stop=True), sig=True))
                    btks.append(t_op(lambda e, h=h: e.matmul(
                        PS[4 + h % 4][:, 256 * (h // 4) + 128:256 * (h // 4) + 256],
                        QFUT[:, 128 * h:128 * h + 128],
                        KBS[:, 2048 * h:2048 * h + 128],
                        start=True, stop=True), sig=True))
                else:
                    btks.append(t_op(lambda e, h=h, bank=bank, colh=colh, ws=ws, q0=q0: e.matmul(
                        PS[bank][:, colh:colh + 256],
                        QB[:, 2048 * h + q0:2048 * h + q0 + 128],
                        KBS[:, 2048 * h + ws:2048 * h + ws + 256],
                        start=True, stop=True), sig=True))
            for t in btks:
                P.wait("vector", t)

            MB = CS["M_BAND0"] if tau == 0 else CS["M_BAND"]
            for h in range(8):
                bank, colh = h % 4, 256 * (h // 4)
                if tau == 0:
                    v_op(lambda e, h=h: e.memset(SBND8[:, 256 * h:256 * h + 128], NEG))
                    v_op(lambda e, h=h, bank=bank, colh=colh, MB=MB: e.tensor_tensor(
                        SBND8[:, 256 * h + 128:256 * h + 256],
                        PS[bank][:, colh + 128:colh + 256], MB[:, 128:256],
                        op=ALU.add), sig=True)
                    v_op(lambda e, h=h: e.memset(SFUT8[:, 256 * h:256 * h + 128], NEG))
                    v_op(lambda e, h=h: e.tensor_tensor(
                        SFUT8[:, 256 * h + 128:256 * h + 256],
                        PS[4 + h % 4][:, 256 * (h // 4) + 128:256 * (h // 4) + 256],
                        CS["M_FUT0"][:, 128:256], op=ALU.add), sig=True)
                else:
                    v_op(lambda e, h=h, bank=bank, colh=colh, MB=MB: e.tensor_tensor(
                        SBND8[:, 256 * h:256 * h + 256],
                        PS[bank][:, colh:colh + 256], MB[:], op=ALU.add), sig=True)
            vv = P.sig_last("vector")
            P.wait("vector", vv)
            n1 = v_op(lambda e: e.tensor_reduce(
                NMX[:, 0:8], SBND8[:].rearrange("p (h c) -> p h c", h=8),
                axis=AX.X, op=ALU.max, negate=True), sig=True)
            n2 = v_op(lambda e: e.tensor_reduce(
                NMX[:, 8:16], SSEL[:].rearrange("p (h j) -> p h j", h=8),
                axis=AX.X, op=ALU.max, negate=True), sig=True)
            P.wait("vector", n1)
            P.wait("vector", n2)
            n3 = v_op(lambda e: e.tensor_tensor(
                NMX[:, 0:8], NMX[:, 0:8], NMX[:, 8:16], op=ALU.min), sig=True)
            if tau == 0:
                P.wait("vector", n3)
                nf = v_op(lambda e: e.tensor_reduce(
                    NMX[:, 8:16], SFUT8[:].rearrange("p (h c) -> p h c", h=8),
                    axis=AX.X, op=ALU.max, negate=True), sig=True)
                P.wait("vector", nf)
                n3 = v_op(lambda e: e.tensor_tensor(
                    NMX[:, 0:8], NMX[:, 0:8], NMX[:, 8:16], op=ALU.min), sig=True)
            P.wait("vector", n3)
            n4 = v_op(lambda e: e.tensor_scalar_mul(NMX[:, 0:8], NMX[:, 0:8], S96),
                      sig=True)
            P.wait("scalar", n4)
            for h in range(8):
                s_op(lambda e, h=h: e.activation(
                    out=ABND8[:, 256 * h:256 * h + 256],
                    in_=SBND8[:, 256 * h:256 * h + 256], func=AF.Exp,
                    bias=NMX[:, h:h + 1], scale=CCOL[:, 3:4],
                    accum_out=ZROW[:, h:h + 1]), sig=True)
                s_op(lambda e, h=h: e.activation(
                    out=ASEL8[:, 32 * h:32 * h + 32],
                    in_=SSEL[:, 32 * h:32 * h + 32], func=AF.Exp,
                    bias=NMX[:, h:h + 1], scale=CCOL[:, 3:4],
                    accum_out=ZROW[:, 8 + h:8 + h + 1]), sig=True)
                if tau == 0:
                    s_op(lambda e, h=h: e.activation(
                        out=AFUT8[:, 256 * h:256 * h + 256],
                        in_=SFUT8[:, 256 * h:256 * h + 256], func=AF.Exp,
                        bias=NMX[:, h:h + 1], scale=CCOL[:, 3:4],
                        accum_out=NMX[:, 8 + h:8 + h + 1]), sig=True)
            sv = P.sig_last("scalar")
            P.wait("vector", sv)
            z1 = v_op(lambda e: e.tensor_tensor(
                ZROW[:, 0:8], ZROW[:, 0:8], ZROW[:, 8:16], op=ALU.add), sig=True)
            P.wait("vector", z1)
            if tau == 0:
                z1 = v_op(lambda e: e.tensor_tensor(
                    ZROW[:, 0:8], ZROW[:, 0:8], NMX[:, 8:16], op=ALU.add), sig=True)
                P.wait("vector", z1)
            z2 = v_op(lambda e: e.reciprocal(ZROW[:, 8:16], ZROW[:, 0:8]), sig=True)
            P.wait("vector", z2)
            for h in range(8):
                v_op(lambda e, h=h: e.tensor_scalar_mul(
                    ABND8[:, 256 * h:256 * h + 256], ABND8[:, 256 * h:256 * h + 256],
                    ZROW[:, 8 + h:8 + h + 1]), sig=True)
                v_op(lambda e, h=h: e.tensor_scalar_mul(
                    ASEL8[:, 32 * h:32 * h + 32], ASEL8[:, 32 * h:32 * h + 32],
                    ZROW[:, 8 + h:8 + h + 1]), sig=True)
                if tau == 0:
                    v_op(lambda e, h=h: e.tensor_scalar_mul(
                        AFUT8[:, 256 * h:256 * h + 256],
                        AFUT8[:, 256 * h:256 * h + 256],
                        ZROW[:, 8 + h:8 + h + 1]), sig=True)
            va = P.sig_last("vector")
            P.wait("tensor", va)
            P.wait("tensor", vl1)
            if tau > 0:
                P.wait("tensor", vl0)

            if tau == 3:
                _sb = XNs.raw("sbnd8").bitcast(F32)
                tap("sbnd_t3", lambda _sb=_sb: _sb, [128, 2048], F32)
                tap("abnd_t3", lambda: ABND8[:], [128, 2048], BF16)
                tap("asel_t3", lambda: ASEL8[:], [128, 256], BF16)
                tap("zrow_t3", lambda: ZROW[:], [128, 16], F32)
            av_items = []
            for h in range(8):
                blks = (1,) if tau == 0 else (0, 1)
                for blk in blks:
                    av_items.append((
                        ABND8[:, 256 * h + 128 * blk:256 * h + 128 * blk + 128],
                        VST[:, 512 * blk + 64 * h:512 * blk + 64 * h + 64], h))
                if tau == 0:
                    av_items.append((
                        AFUT8[:, 256 * h + 128:256 * h + 256],
                        VST[:, 512 + 64 * h:512 + 64 * h + 64], h))
            n_av = len(av_items)
            prev_mm = {0: None, 1: None, 2: None}
            for i, (src, vsl, h) in enumerate(av_items):
                slot = i % 3
                tt = t_op(lambda e, src=src: e.transpose(
                    psbf(6)[:, 0:128], src, IDENT[:]), sig=True)
                P.wait("vector", tt)
                if prev_mm[slot] is not None:
                    P.wait("vector", prev_mm[slot])
                ev = v_op(lambda e, slot=slot: e.tensor_copy(
                    ATP[:, 128 * slot:128 * slot + 128], psbf(6)[:, 0:128]),
                    sig=True)
                P.wait("tensor", ev)
                mm = t_op(lambda e, slot=slot, vsl=vsl, h=h,
                          st=(i == 0), sp=(i == n_av - 1): e.matmul(
                    PS[4][:, 64 * h:64 * h + 64], ATP[:, 128 * slot:128 * slot + 128],
                    vsl, start=st, stop=sp), sig=True)
                prev_mm[slot] = mm
            tv = P.sig_last("tensor")
            P.wait("vector", tv)

            GV3 = GKV[:, 0:4096].rearrange("p (j d) -> p j d", j=8)
            sel_first = [True]
            P.wait("gpsimd", prev_grp)
            for g in range(4):
                if not sel_first[0] or g > 0:
                    P.wait("gpsimd", prev_grp)
                gts = []
                for j in range(8):
                    sj = 8 * g + j
                    gts.append(dmaG(lambda e, sj=sj, j=j: e.indirect_dma_start(
                        out=GKV[:, 512 * j:512 * j + 512], out_offset=None,
                        in_=VDR,
                        in_offset=IndirectOffsetOnAxis(ap=TOPI[:, sj:sj + 1], axis=0))))
                for t in gts:
                    P.wait("vector", t)
                in1 = (ASEL8[:].rearrange("p (h j) -> p h j", h=8)[:, :, 8 * g:8 * g + 8]
                       .transpose([0, 2, 1]).unsqueeze(3)
                       .broadcast_to([128, 8, 8, 64]))
                m1 = v_op(lambda e, in1=in1: e.tensor_tensor(
                    PRD8[:].rearrange("p (j h d) -> p j h d", j=8, h=8),
                    GV3.rearrange("p j (h d) -> p j h d", h=8), in1,
                    op=ALU.mult), sig=True)
                P.wait("vector", m1)
                red_in = PRD8[:].rearrange("p (j m) -> p j m", j=8).transpose([0, 2, 1])
                if sel_first[0]:
                    m2 = v_op(lambda e, red_in=red_in: e.tensor_reduce(
                        SELACC[:, 0:512], red_in, axis=AX.X, op=ALU.add), sig=True)
                    sel_first[0] = False
                else:
                    m2 = v_op(lambda e, red_in=red_in: e.tensor_reduce(
                        SELACC[:, 512:1024], red_in, axis=AX.X, op=ALU.add), sig=True)
                    P.wait("vector", m2)
                    m2 = v_op(lambda e: e.tensor_tensor(
                        SELACC[:, 0:512], SELACC[:, 0:512], SELACC[:, 512:1024],
                        op=ALU.add), sig=True)
                P.wait("vector", m2)
                prev_grp = m1

            av = v_op(lambda e: e.tensor_tensor(
                ATM[:], PS[4][:], SELACC[:, 0:512], op=ALU.add), sig=True)
            P.wait("tensor", av)
            if tau == 3:
                tap("atm_t3", lambda: ATM[:], [128, 512], BF16)
                tap("selacc_t3", lambda: SELACC[:], [128, 1024], F32)
            for c4 in range(4):
                tt = t_op(lambda e, c4=c4: e.transpose(
                    psbf(6)[:, 0:128], ATM[:, 128 * c4:128 * c4 + 128], IDENT[:]),
                    sig=True)
                P.wait("vector", tt)
                ev = v_op(lambda e, c4=c4: e.tensor_copy(
                    AFM[:, 128 * c4:128 * c4 + 128], psbf(6)[:, 0:128]), sig=True)
                P.wait("tensor", ev)
            va = P.sig_last("vector")
            for oc in range(8):
                for c4 in range(4):
                    tok = t_op(lambda e, oc=oc, c4=c4: e.matmul(
                        PS[5][:, 0:128],
                        V["WOUT"][:, 1024 * c4 + 128 * oc:1024 * c4 + 128 * oc + 128],
                        AFM[:, 128 * c4:128 * c4 + 128],
                        start=(c4 == 0), stop=(c4 == 3)), sig=(c4 == 3))
                P.wait("vector", tok)
                rv = v_op(lambda e, oc=oc, q0=q0: e.tensor_tensor(
                    XI[oc][:, 1 + q0:1 + q0 + 128], XI[oc][:, 1 + q0:1 + q0 + 128],
                    PS[5][:, 0:128], op=ALU.add), sig=True)
                P.wait("tensor", rv)
            P.barrier()

        for tau in range(16):
            emit_tau(tau)
        for nm in ("GKV", "QCTM", "QSTMt", "QFUT", "VST", "QS", "QI", "KI",
                   "WOUT", "WUV"):
            WSL.free(nm)
        for nm in ("itA", "itB", "sbnd8", "abnd8", "ssel", "asel8",
                   "atp", "prd"):
            XNs.free(nm)
        P.barrier()



    def phase_ffn(b, n_layers=6):
        WQKV = WSL.alloc("WQKV", 24576)
        WOsl = XNs.alloc("WO", 8192)
        XN2 = XNs.alloc("XN2", 8192)
        QW = QB[:, 0:2048]; QN = QB[:, 2048:4096]; QRT = QB[:, 4096:6144]
        SMF = QB[:, 6144:9216].bitcast(F32)
        AT = QB[:, 9216:10752]
        ATPF = QB[:, 10752:11136]
        AOW = QB[:, 11136:13184]
        AVE = QB[:, 13184:14208]
        KW = KBS[:, 0:2048]; KN = KBS[:, 2048:4096]
        KRT = KBS[:, 4096:8192]
        VW = KBS[:, 8192:12288]
        TBLQ = TBL.alloc("tblq", 1024)
        RST2 = KBS[:, 12288:16384].bitcast(F32)
        RST2B = TBL.tile[:, 1024:3072]
        SS2 = XNs.tile[:, 8192:12288].bitcast(F32)

        def emit_ln_one_tg(tg, shift, g_ap, b_ap, dst, scr):
            sl0 = 1 + 512 * tg
            prev_t2 = None
            for c in range(NCH):
                off = sl0 - 1 if (shift and c >= 4) else sl0
                src = XI[c][:, off:off + 512]
                if prev_t2 is not None:
                    P.wait("scalar", prev_t2)
                sq = s_op(lambda e, src=src, scr=scr: e.activation(
                    out=scr, in_=src, func=AF.Square), sig=True)
                P.wait("tensor", sq)
                t1 = t_op(lambda e, src=src, c=c: e.matmul(
                    PS[5][0:1, 0:512], ONESB[:, 0:1], src,
                    start=(c == 0), stop=(c == NCH - 1)), sig=(c == NCH - 1))
                t2 = t_op(lambda e, scr=scr, c=c: e.matmul(
                    PS[6][0:1, 0:512], ONESB[:, 1:2], scr,
                    start=(c == 0), stop=(c == NCH - 1)), sig=True)
                prev_t2 = t2
            P.wait("vector", t1)
            P.wait("vector", t2)
            r1 = v_op(lambda e: e.tensor_scalar_mul(
                SROW[0:1, :], PS[5][0:1, 0:512], -1.0 / 1024.0), sig=True)
            r2 = v_op(lambda e: e.tensor_scalar_mul(
                SROW2[0:1, :], PS[6][0:1, 0:512], 1.0 / 1024.0), sig=True)
            P.wait("scalar", r1)
            P.wait("vector", r2)
            r3 = s_op(lambda e: e.activation(out=PS[5][0:1, 0:512],
                                             in_=SROW[0:1, :], func=AF.Square),
                      sig=True)
            P.wait("vector", r3)
            r4 = v_op(lambda e: e.tensor_sub(SROW2[0:1, :], SROW2[0:1, :],
                                             PS[5][0:1, 0:512]), sig=True)
            P.wait("scalar", r4)
            r5 = s_op(lambda e: e.activation(out=SROW2[0:1, :], in_=SROW2[0:1, :],
                                             func=AF.Ln, bias=CCOL[0:1, 0:1]),
                      sig=True)
            P.wait("scalar", r5)
            r6 = s_op(lambda e: e.activation(out=SROW2[0:1, :], in_=SROW2[0:1, :],
                                             func=AF.Exp, scale=CCOL[0:1, 5:6]),
                      sig=True)
            P.wait("vector", r6)
            r7 = v_op(lambda e: e.tensor_copy(SROWB[0:1, 0:512], SROW[0:1, :]),
                      sig=True)
            P.wait("vector", r7)
            r8 = v_op(lambda e: e.tensor_copy(SROWB[0:1, 512:1024], SROW2[0:1, :]),
                      sig=True)
            P.wait("tensor", r7)
            P.wait("tensor", r8)
            b1t = t_op(lambda e: e.matmul(PS[5][:, 0:512], ONER[0:1, :],
                                          SROWB[0:1, 0:512], start=True, stop=True),
                       sig=True)
            b2t = t_op(lambda e: e.matmul(PS[6][:, 0:512], ONER[0:1, :],
                                          SROWB[0:1, 512:1024], start=True,
                                          stop=True), sig=True)
            P.wait("vector", b1t)
            P.wait("vector", b2t)
            lastv = None
            for c in range(NCH):
                off = sl0 - 1 if (shift and c >= 4) else sl0
                src = XI[c][:, off:off + 512]
                d = dst(c, tg)
                v1 = v_op(lambda e, src=src, scr=scr: e.tensor_add(
                    scr, src, PS[5][:, 0:512]), sig=True)
                P.wait("vector", v1)
                if g_ap is None:
                    v2 = v_op(lambda e, d=d, scr=scr: e.tensor_mul(
                        d, scr, PS[6][:, 0:512]), sig=True)
                else:
                    v2 = v_op(lambda e, scr=scr: e.tensor_mul(
                        scr, scr, PS[6][:, 0:512]), sig=True)
                    P.wait("vector", v2)
                    v2 = v_op(lambda e, d=d, scr=scr, c=c, g_ap=g_ap, b_ap=b_ap:
                              e.tensor_scalar(
                        d, scr, g_ap[:, c:c + 1], b_ap[:, c:c + 1],
                        op0=ALU.mult, op1=ALU.add), sig=True)
                P.wait("vector", v2)
                lastv = v2
            P.wait("tensor", lastv)
            P.wait("scalar", lastv)

        def load_layer_weights(l):
            toks = [dmaG(lambda e, l=l: e.dma_start(
                out=WQKV.rearrange("p (c n) -> p c n", c=8),
                in_=WD["wqkv"][l].transpose([1, 0, 2])))]
            toks.append(dmaG(lambda e, l=l: e.dma_start(
                out=WOsl.rearrange("p (c n) -> p c n", c=8),
                in_=WD["wo"][l].transpose([1, 0, 2]))))
            toks.append(dmaG(lambda e, l=l: e.dma_start(out=QSC[:, 0:1],
                                                        in_=WD["qs"][l])))
            toks.append(dmaG(lambda e, l=l: e.dma_start(out=QSC[:, 1:2],
                                                        in_=WD["ks"][l])))
            toks.append(dmaG(lambda e, l=l: e.dma_start(out=B1A[:], in_=WD["b1a"][l])))
            toks.append(dmaG(lambda e, l=l: e.dma_start(out=B1G[:], in_=WD["b1g"][l])))
            toks.append(dmaG(lambda e, l=l: e.dma_start(out=B2T[:], in_=WD["b2"][l])))
            toks.append(dmaG(lambda e, l=l: e.dma_start(out=LNP[:, 0:8],
                                                        in_=WD["lng"][l])))
            toks.append(dmaG(lambda e, l=l: e.dma_start(out=LNP[:, 8:16],
                                                        in_=WD["lnb"][l])))
            return toks

        def emit_attn_sub(l, w, sub, hg, first_avp, prev_mask, tstate):
            par = w % 2
            prev_par = 1 - par
            q0p = 128 * sub
            mmt = []
            if prev_mask[0] is not None:
                P.wait("tensor", prev_mask[0])
            for hh in range(4):
                h = 4 * hg + hh
                c = h // 2
                r0 = 64 * (h % 2)
                qs0 = 256 * c + q0p
                wA = 256 - 128 * sub
                pieces = []
                if w > 0:
                    pieces.append((512 * c + 256 * prev_par + 128 * sub, wA, 0))
                pieces.append((512 * c + 256 * par, 128 + 128 * sub, wA))
                for (kc, kw_, oc_) in pieces:
                    tok = t_op(lambda e, hh=hh, r0=r0, qs0=qs0, kc=kc, kw_=kw_,
                               oc_=oc_: e.matmul(
                        PS[hh][:, oc_:oc_ + kw_],
                        QRT[r0:r0 + 64, qs0:qs0 + 128],
                        KRT[r0:r0 + 64, kc:kc + kw_],
                        start=True, stop=True), sig=True)
                    mmt.append(tok)
            for t in mmt:
                P.wait("vector", t)
            mknm = "M384" if w > 0 else ("M384_W0S0" if sub == 0 else "M384_W0S1")
            MK = CS[mknm]
            v1 = None
            for hh in range(4):
                v1 = v_op(lambda e, hh=hh, MK=MK: e.tensor_add(
                    SMF[:, 384 * hh:384 * hh + 384], PS[hh][:, 0:384], MK[:]),
                    sig=True)
            prev_mask[0] = v1
            P.wait("vector", v1)
            n1 = v_op(lambda e: e.tensor_reduce(
                NMX[:, 0:4], SMF[:].rearrange("p (h c) -> p h c", h=4),
                axis=AX.X, op=ALU.max, negate=True), sig=True)
            P.wait("vector", n1)
            n2 = v_op(lambda e: e.tensor_scalar_mul(NMX[:, 0:4], NMX[:, 0:4], 8.0),
                      sig=True)
            P.wait("scalar", n2)
            ex = None
            for hh in range(4):
                ex = s_op(lambda e, hh=hh: e.activation(
                    out=AT[:, 384 * hh:384 * hh + 384],
                    in_=SMF[:, 384 * hh:384 * hh + 384], func=AF.Exp,
                    bias=NMX[:, hh:hh + 1], scale=CCOL[:, 4:5],
                    accum_out=ZROW[:, hh:hh + 1]), sig=True)
            P.wait("vector", ex)
            rdst = ZROW[:, 8 + 4 * hg:12 + 4 * hg] if hg < 2 else \
                NMX[:, 8 + 4 * (hg - 2):12 + 4 * (hg - 2)]
            rz = v_op(lambda e, rdst=rdst: e.reciprocal(rdst, ZROW[:, 0:4]),
                      sig=True)
            tstate["rz"] = rz
            P.wait("tensor", ex)
            blk0 = 2 * w - 2 + sub
            close_bank = hg in (1, 3)
            kk = 0
            last_mm = None
            for hh in range(4):
                h = 4 * hg + hh
                avbank = 4 + (h // 8)
                blks = [bi for bi in range(3) if blk0 + bi >= 0]
                for bn, bi in enumerate(blks):
                    blk = blk0 + bi
                    p = kk % 2
                    tb = 6 + p
                    if tstate["ev"][p] is not None:
                        P.wait("tensor", tstate["ev"][p])
                    tt = t_op(lambda e, hh=hh, bi=bi, tb=tb: e.transpose(
                        psbf(tb)[:, 0:128],
                        AT[:, 384 * hh + 128 * bi:384 * hh + 128 * bi + 128],
                        IDENT[:]), sig=True)
                    P.wait("vector", tt)
                    if tstate["mm"][p] is not None:
                        P.wait("vector", tstate["mm"][p])
                    ev = v_op(lambda e, p=p, tb=tb: e.tensor_copy(
                        ATPF[:, 128 * p:128 * p + 128], psbf(tb)[:, 0:128]),
                        sig=True)
                    tstate["ev"][p] = ev
                    P.wait("tensor", ev)
                    slot = blk % 4
                    st = first_avp[h // 8]
                    first_avp[h // 8] = False
                    sp = close_bank and hh == 3 and bn == len(blks) - 1
                    mm = t_op(lambda e, h=h, slot=slot, st=st, sp=sp, p=p,
                              avbank=avbank: e.matmul(
                        PS[avbank][:, 64 * (h % 8):64 * (h % 8) + 64],
                        ATPF[:, 128 * p:128 * p + 128],
                        VW[:, 1024 * slot + 64 * h:1024 * slot + 64 * h + 64],
                        start=st, stop=sp), sig=True)
                    tstate["mm"][p] = mm
                    last_mm = mm
                    kk += 1
            return last_mm

        def emit_window(l, w):
            par = w % 2
            t0 = 1 + 256 * w
            last_qk_evac = [None]
            ttk = []
            for i, nm in enumerate(("TF_QC", "TF_QS", "TF_KC", "TF_KS")):
                ttk.append(dmaS(lambda e, nm=nm, i=i, w=w: e.dma_start(
                    out=TBLQ[:, 256 * i:256 * i + 256],
                    in_=TD[nm][:, 256 * w:256 * w + 256])))
            last_ev = {6: None, 7: None, 3: None, 2: None}
            for oc in range(8):
                for part, ps_i, dst in ((0, 6, QW), (1024, 7, KW)):
                    if last_ev[ps_i] is not None:
                        P.wait("tensor", last_ev[ps_i])
                    for c in range(NCH):
                        off = t0 - 1 if c >= 4 else t0
                        tok = t_op(lambda e, c=c, oc=oc, part=part, ps_i=ps_i,
                                   off=off: e.matmul(
                            PS[ps_i][:, 0:256],
                            WQKV[:, 3072 * c + part + 128 * oc:
                                 3072 * c + part + 128 * oc + 128],
                            XI[c][:, off:off + 256],
                            start=(c == 0), stop=(c == NCH - 1)),
                            sig=(c == NCH - 1))
                    P.wait("vector", tok)
                    ev = v_op(lambda e, oc=oc, ps_i=ps_i, dst=dst: e.tensor_copy(
                        dst[:, 256 * oc:256 * oc + 256], PS[ps_i][:, 0:256]),
                        sig=True)
                    last_ev[ps_i] = ev
                    last_qk_evac[0] = ev
            for sub in range(2):
                for nh in range(2):
                    ps_i = 3 if (2 * sub + nh) % 2 == 0 else 2
                    if last_ev[ps_i] is not None:
                        P.wait("tensor", last_ev[ps_i])
                    for c in range(NCH):
                        off = t0 - 1 if c >= 4 else t0
                        tok = t_op(lambda e, c=c, sub=sub, nh=nh, ps_i=ps_i,
                                   off=off: e.matmul(
                            PS[ps_i][:, 0:512],
                            XI[c][:, off + 128 * sub:off + 128 * sub + 128],
                            WQKV[:, 3072 * c + 2048 + 512 * nh:
                                 3072 * c + 2048 + 512 * nh + 512],
                            start=(c == 0), stop=(c == NCH - 1)),
                            sig=(c == NCH - 1))
                    P.wait("vector", tok)
                    slot = (2 * w + sub) % 4
                    ev = v_op(lambda e, slot=slot, nh=nh, sub=sub,
                              ps_i=ps_i: e.tensor_copy(
                        VW[:, 1024 * slot + 512 * nh:1024 * slot + 512 * nh + 512],
                        PS[ps_i][:, 0:512]), sig=True)
                    last_ev[ps_i] = ev
            for t in ttk:
                P.wait("vector", t)
            P.wait("scalar", last_qk_evac[0])
            last_sum_ev = None
            last_sq_mm = [None, None]
            BCSLOT = ((5, 0), (6, 0), (5, 256), (6, 256))
            RTSLOT = ((4, 0), (2, 0), (4, 256), (2, 256))
            last_stt4 = [None, None, None, None]
            last_r24 = [None, last_ev[2], None, last_ev[2]]
            last_r3 = None
            for qk, SRCW, DSTN, scl_col, tc0 in (
                    (0, QW, QN, 0, 0), (1, KW, KN, 1, 512)):
                if last_r3 is not None:
                    P.wait("scalar", last_r3)
                if qk == 1:
                    P.wait("tensor", last_r24[0])
                    P.wait("tensor", last_r24[2])
                for i in range(4):
                    sl = 512 * (i % 2)
                    if last_sq_mm[i % 2] is not None:
                        P.wait("scalar", last_sq_mm[i % 2])
                    sq = s_op(lambda e, i=i, sl=sl, SRCW=SRCW: e.activation(
                        out=AVE[:, sl:sl + 512],
                        in_=SRCW[:, 512 * i:512 * i + 512],
                        func=AF.Square), sig=True)
                    P.wait("tensor", sq)
                    if last_sum_ev is not None:
                        P.wait("tensor", last_sum_ev)
                    mm = t_op(lambda e, sl=sl: e.matmul(
                        PS[4][0:2, 0:512], CS["IND2"][:], AVE[:, sl:sl + 512],
                        start=True, stop=True), sig=True)
                    last_sq_mm[i % 2] = mm
                    P.wait("vector", mm)
                    ev = v_op(lambda e, i=i: e.tensor_copy(
                        SS2[0:2, 512 * i:512 * i + 512], PS[4][0:2, 0:512]),
                        sig=True)
                    last_sum_ev = ev
                P.wait("vector", last_sum_ev)
                s2 = v_op(lambda e: e.tensor_scalar_max(SS2[0:2, :], SS2[0:2, :],
                                                        1e-24), sig=True)
                P.wait("scalar", s2)
                s3 = s_op(lambda e: e.activation(out=RST2[0:2, :], in_=SS2[0:2, :],
                                                 func=AF.Ln), sig=True)
                P.wait("scalar", s3)
                s4 = s_op(lambda e: e.activation(out=RST2B[0:2, :],
                                                 in_=RST2[0:2, :], func=AF.Exp,
                                                 scale=CCOL[0:2, 5:6]), sig=True)
                P.wait("tensor", s4)
                LKA = 1
                pend = {}
                for cc in range(NCH + LKA):
                    if cc < NCH:
                        s = cc % 4
                        bank, c0 = BCSLOT[s]
                        if last_stt4[s] is not None:
                            P.wait("tensor", last_stt4[s])
                        bc = t_op(lambda e, cc=cc, bank=bank, c0=c0: e.matmul(
                            PS[bank][:, c0:c0 + 256], CS["IND2T"][:],
                            RST2B[0:2, 256 * cc:256 * cc + 256],
                            start=True, stop=True), sig=True)
                        pend[cc] = (bc, bank, c0, s)
                    c = cc - LKA
                    if c < 0:
                        continue
                    bc, bank, c0, s = pend.pop(c)
                    P.wait("vector", bc)
                    stt = v_op(lambda e, c=c, bank=bank, c0=c0, SRCW=SRCW,
                               DSTN=DSTN, scl_col=scl_col: e.scalar_tensor_tensor(
                        DSTN[:, 256 * c:256 * c + 256], PS[bank][:, c0:c0 + 256],
                        QSC[:, scl_col:scl_col + 1],
                        SRCW[:, 256 * c:256 * c + 256],
                        op0=ALU.mult, op1=ALU.mult), sig=True)
                    last_stt4[s] = stt
                    rs = c % 4
                    rbank, r0 = RTSLOT[rs]
                    P.wait("tensor", stt)
                    if last_r24[rs] is not None:
                        P.wait("tensor", last_r24[rs])
                    rt = t_op(lambda e, c=c, rbank=rbank, r0=r0, DSTN=DSTN:
                              e.matmul(
                        PS[rbank][:, r0:r0 + 256], CS["P_ROT64"][:],
                        DSTN[:, 256 * c:256 * c + 256], start=True, stop=True),
                        sig=True)
                    a0 = 512 * (c % 2)
                    P.wait("vector", stt)
                    r1 = v_op(lambda e, c=c, a0=a0, DSTN=DSTN, tc0=tc0:
                              e.tensor_tensor(
                        AVE[:, a0:a0 + 256], DSTN[:, 256 * c:256 * c + 256],
                        TBLQ[:, tc0:tc0 + 256], op=ALU.mult), sig=True)
                    P.wait("vector", rt)
                    r2 = v_op(lambda e, a0=a0, rbank=rbank, r0=r0, tc0=tc0:
                              e.tensor_tensor(
                        AVE[:, a0 + 256:a0 + 512], PS[rbank][:, r0:r0 + 256],
                        TBLQ[:, tc0 + 256:tc0 + 512], op=ALU.mult), sig=True)
                    last_r24[rs] = r2
                    dcol = (512 * c + 256 * par) if qk == 1 else 256 * c
                    DSTR = KRT if qk == 1 else QRT
                    P.wait("vector", r2)
                    r3 = v_op(lambda e, dcol=dcol, a0=a0, DSTR=DSTR: e.tensor_add(
                        DSTR[:, dcol:dcol + 256], AVE[:, a0:a0 + 256],
                        AVE[:, a0 + 256:a0 + 512]), sig=True)
                    last_r3 = r3
            if l == 0 and w == 1:
                tap("qrt_w1", lambda: QRT[:], [128, 2048], BF16)
                tap("krt_w1", lambda: KRT[:], [128, 4096], BF16)
                tap("vw_w1", lambda: VW[:], [128, 4096], BF16)
                tap("qn_w1", lambda: QN[:], [128, 2048], BF16)
            P.barrier()
            prev_mask = [None]
            tstate = {"ev": [None, None], "mm": [None, None]}
            for sub in range(2):
                first_avp = [True, True]
                last_mm = None
                for hg in range(4):
                    last_mm = emit_attn_sub(l, w, sub, hg, first_avp, prev_mask,
                                            tstate)
                P.wait("scalar", last_mm)
                P.wait("scalar", tstate["rz"])
                ecs = None
                for h in range(16):
                    bank = 4 + h // 8
                    rcol = ZROW[:, 8 + h:9 + h] if h < 8 else \
                        NMX[:, h:h + 1]
                    ecs = s_op(lambda e, h=h, bank=bank, rcol=rcol: e.activation(
                        out=AVE[:, 64 * h:64 * h + 64],
                        in_=PS[bank][:, 64 * (h % 8):64 * (h % 8) + 64],
                        func=AF.Copy, scale=rcol), sig=True)
                P.wait("tensor", ecs)
                for c4 in range(8):
                    p = c4 % 2
                    tb = 6 + p
                    if tstate["ev"][p] is not None:
                        P.wait("tensor", tstate["ev"][p])
                    tt = t_op(lambda e, c4=c4, tb=tb: e.transpose(
                        psbf(tb)[:, 0:128], AVE[:, 128 * c4:128 * c4 + 128],
                        IDENT[:]), sig=True)
                    P.wait("vector", tt)
                    e3 = v_op(lambda e, c4=c4, sub=sub, tb=tb: e.tensor_copy(
                        AOW[:, 256 * c4 + 128 * sub:256 * c4 + 128 * sub + 128],
                        psbf(tb)[:, 0:128]), sig=True)
                    tstate["ev"][p] = e3
            if l == 0 and w == 1:
                tap("aow_w1", lambda: AOW[:], [128, 2048], BF16)
            P.barrier()
            last_rv = {6: None, 7: None}
            for oc in range(8):
                wb = 6 + oc % 2
                if last_rv[wb] is not None:
                    P.wait("tensor", last_rv[wb])
                for ic in range(8):
                    tok = t_op(lambda e, oc=oc, ic=ic, wb=wb: e.matmul(
                        PS[wb][:, 0:256],
                        WOsl[:, 1024 * ic + 128 * oc:1024 * ic + 128 * oc + 128],
                        AOW[:, 256 * ic:256 * ic + 256],
                        start=(ic == 0), stop=(ic == 7)), sig=(ic == 7))
                P.wait("vector", tok)
                rv = v_op(lambda e, oc=oc, t0=t0, wb=wb: e.tensor_add(
                    XI[oc][:, t0:t0 + 256], XI[oc][:, t0:t0 + 256],
                    PS[wb][:, 0:256]), sig=True)
                last_rv[wb] = rv
            P.barrier()

        def emit_mlp_group(l, g):
            MSCR = XNs.tile[:, 0:2048]
            lnscr2 = MSCR[:, 0:512]
            def dstfn(c, tg):
                return XN2[:, 1024 * c + 512 * (tg - 2 * g):
                           1024 * c + 512 * (tg - 2 * g) + 512]
            for tg in (2 * g, 2 * g + 1):
                emit_ln_one_tg(tg, True, LNP[:, 0:8], LNP[:, 8:16], dstfn, lnscr2)
            P.barrier()
            if l == 0 and g == 0:
                tap("xn2_g0", lambda: XN2[:], [128, 8192], BF16)
            w1ring = [KBS[:, 6144 + 1024 * i:6144 + 1024 * (i + 1)]
                      for i in range(4)]
            w2ring = [KBS[:, 10240 + 2816 * i:10240 + 2816 * (i + 1)]
                      for i in range(2)]
            def hsrc_ap(cc, th):
                if cc < 16:
                    return QB[:, 1024 * cc + 512 * th:1024 * cc + 512 * th + 512]
                return KBS[:, 1024 * (cc - 16) + 512 * th:
                           1024 * (cc - 16) + 512 * th + 512]
            def load_w1(cc):
                slot = cc % 2
                tA = dmaG(lambda e, cc=cc, slot=slot: e.dma_start(
                    out=w1ring[2 * slot].rearrange("p (c n) -> p c n", c=8),
                    in_=WD["w1"][l, cc, 0].transpose([1, 0, 2])))
                tG = dmaG(lambda e, cc=cc, slot=slot: e.dma_start(
                    out=w1ring[2 * slot + 1].rearrange("p (c n) -> p c n", c=8),
                    in_=WD["w1"][l, cc, 1].transpose([1, 0, 2])))
                return (tA, tG)
            w1tok = {0: load_w1(0)}
            prev_cons = None
            ABANKS = (6, 2)
            GBANKS = (7, 3)
            GSLOT = (slice(512, 1024), slice(1536, 2048))
            last_hv = {0: None, 1: None}
            for cc in range(22):
                slot = cc % 2
                tA, tG = w1tok[cc]
                P.wait("tensor", tA)
                P.wait("tensor", tG)
                for th in range(2):
                    par = th
                    ab, gb = ABANKS[par], GBANKS[par]
                    if last_hv[par] is not None:
                        P.wait("tensor", last_hv[par])
                    for ag, ps_i in ((0, ab), (1, gb)):
                        ring = w1ring[2 * slot + ag]
                        for c in range(NCH):
                            tok = t_op(lambda e, c=c, ring=ring, ps_i=ps_i, th=th:
                                       e.matmul(
                                PS[ps_i][:, 0:512],
                                ring[:, 128 * c:128 * c + 128],
                                XN2[:, 1024 * c + 512 * th:
                                    1024 * c + 512 * th + 512],
                                start=(c == 0), stop=(c == NCH - 1)),
                                sig=(c == NCH - 1))
                        if ag == 0:
                            atok = tok
                        else:
                            gtok = tok
                    P.wait("scalar", gtok)
                    if last_hv[par] is not None:
                        P.wait("scalar", last_hv[par])
                    gl = s_op(lambda e, cc=cc, gb=gb, gs=GSLOT[par]: e.activation(
                        out=MSCR[:, gs], in_=PS[gb][:, 0:512],
                        func=AF.Gelu, bias=B1G[:, cc:cc + 1]), sig=True)
                    P.wait("vector", gl)
                    P.wait("vector", atok)
                    hv = v_op(lambda e, cc=cc, th=th, ab=ab, gs=GSLOT[par]:
                              e.scalar_tensor_tensor(
                        hsrc_ap(cc, th), PS[ab][:, 0:512], B1A[:, cc:cc + 1],
                        MSCR[:, gs], op0=ALU.add, op1=ALU.mult), sig=True)
                    last_hv[par] = hv
                    prev_cons = atok
                if cc + 1 < 22:
                    P.wait("gpsimd", prev_cons)
                    w1tok[cc + 1] = load_w1(cc + 1)
            P.wait("tensor", last_hv[0])
            P.wait("tensor", last_hv[1])
            P.barrier()
            def load_w2(o):
                slot = o % 2
                return dmaG(lambda e, o=o, slot=slot: e.dma_start(
                    out=w2ring[slot].rearrange("p (c n) -> p c n", c=22),
                    in_=WD["w2"][l, o].transpose([1, 0, 2])))
            w2tok = {0: load_w2(0)}
            prev_w2 = None
            last_rc = {0: None, 1: None}
            W2SLOT = (slice(1024, 1536), slice(0, 512))
            for o in range(8):
                slot = o % 2
                P.wait("tensor", w2tok[o])
                for th in range(2):
                    par = th
                    wb = 4 + par
                    tcol = 1 + 1024 * g + 512 * th
                    if last_rc[par] is not None:
                        P.wait("tensor", last_rc[par])
                    for cc in range(22):
                        tok = t_op(lambda e, cc=cc, slot=slot, th=th, wb=wb: e.matmul(
                            PS[wb][:, 0:512],
                            w2ring[slot][:, 128 * cc:128 * cc + 128],
                            hsrc_ap(cc, th), start=(cc == 0), stop=(cc == 21)),
                            sig=(cc == 21))
                    P.wait("vector", tok)
                    rv = v_op(lambda e, o=o, tcol=tcol, wb=wb, ws=W2SLOT[par]:
                              e.scalar_tensor_tensor(
                        MSCR[:, ws], PS[wb][:, 0:512], B2T[:, o:o + 1],
                        XI[o][:, tcol:tcol + 512], op0=ALU.add, op1=ALU.add),
                        sig=True)
                    P.wait("vector", rv)
                    rc = v_op(lambda e, o=o, tcol=tcol, ws=W2SLOT[par]: e.tensor_copy(
                        XI[o][:, tcol:tcol + 512], MSCR[:, ws]), sig=True)
                    last_rc[par] = rc
                    prev_w2 = tok
                if o + 1 < 8:
                    P.wait("gpsimd", prev_w2)
                    w2tok[o + 1] = load_w2(o + 1)
            P.wait("tensor", last_rc[0])
            P.wait("tensor", last_rc[1])
            P.barrier()

        ptoks = []
        for c in range(NCH):
            ptoks.append(dmaG(lambda e, c=c: e.dma_start(
                out=X2D[c], in_=XI[c][:, 1:2049])))
        for t in ptoks:
            P.wait("vector", t)
        lnscr = AVE[:, 0:512]
        for tg in range(4):
            emit_ln_one_tg(tg, False, None, None,
                           lambda c, tg: XI[c][:, 1 + 512 * tg:1 + 512 * tg + 512],
                           lnscr)
        P.barrier()
        for c in range(NCH):
            tap("xinner_c%d_b%d" % (c, b),
                lambda c=c: XI[c][:, 1:2049], [128, 2048], BF16)
        wtoks = load_layer_weights(0)
        for t in wtoks:
            for en in ("tensor", "vector", "scalar"):
                P.wait(en, t)
        for l in range(n_layers):
            for w in range(8):
                emit_window(l, w)
            for c in range(NCH):
                tap("attn_l%d_c%d_b%d" % (l, c, b),
                    lambda c=c: XI[c][:, 1:2049], [128, 2048], BF16)
            for g in range(2):
                emit_mlp_group(l, g)
            if l + 1 < n_layers:
                wtoks = load_layer_weights(l + 1)
                for t in wtoks:
                    for en in ("tensor", "vector", "scalar"):
                        P.wait(en, t)
            tap("ffn_l%d_b%d" % (l, b), lambda: XI[0][:, 1:2049], [128, 2048], BF16)
        WSL.free("WQKV")
        XNs.free("WO"); XNs.free("XN2")
        TBL.free("tblq")
        P.barrier()

    def phase_out(b):
        oscr = XNs.alloc("oscr", 2048, F32)
        x2t = XNs.alloc("x2t", 512, F32)
        prev = None
        prev_tt = None
        for j in range(16):
            for c in range(NCH):
                if prev is not None:
                    P.wait("sync", prev)
                ld = dmaS(lambda e, c=c, j=j: e.dma_start(
                    out=x2t[:, 0:128], in_=X2D[c][:, 128 * j:128 * j + 128]))
                P.wait("vector", ld)
                if prev_tt is not None:
                    P.wait("vector", prev_tt)
                ad = v_op(lambda e, c=c, j=j: e.tensor_add(
                    x2t[:, 128:256], x2t[:, 0:128],
                    XI[c][:, 1 + 128 * j:1 + 128 * j + 128]), sig=True)
                prev = ad
                P.wait("tensor", ad)
                tt = t_op(lambda e: e.transpose(
                    PS[5][:, 0:128], x2t[:, 128:256], CS["IDF32"][:]), sig=True)
                prev_tt = tt
                P.wait("vector", tt)
                ev = v_op(lambda e, c=c: e.tensor_copy(
                    oscr[:, 128 * c:128 * c + 128], PS[5][:, 0:128]), sig=True)
                P.wait("tensor", ev)
                P.wait("sync", ev)
            dd = dmaS(lambda e, b=b, j=j: e.dma_start(
                out=OUT[b, 128 * j:128 * j + 128, :], in_=oscr[:, 0:1024]))
            P.wait("vector", dd)
        XNs.free("oscr"); XNs.free("x2t")
        P.barrier()

    xn_slab_v = XNs.alloc("xn", 16384)
    xn_v = [xn_slab_v[:, 2048 * c:2048 * (c + 1)] for c in range(NCH)]
    XNs.free("xn")

    phase_delta()
    tap("dcol", lambda: DCOL[:], [128, 2], F32)
    for b in range(B_CORE):
        XNs.alloc("xn", 16384)
        phase_ln1(b, xn_v)
        tap("xn0_b%d" % b, lambda: xn_v[0][:], [128, 2048], BF16)
        tap("xiraw0_b%d" % b, lambda: XI[0][:, 1:2049], [128, 2048], BF16)
        H = phase_mla_proj(b, xn_v)
        XNs.free("xn")
        phase_mla_attend(b, H)
        tap("xi0_b%d" % b, lambda: XI[0][:, 1:2049], [128, 2048], BF16)
        phase_ffn(b, n_layers=n_ffn_layers)
        phase_out(b)
    fin = P.sig_last("sync")
    P.wait("sync", fin)

    with nc.Block() as block:
        P.run(block, sems)
    B.finalize()
    return B, TAPS


def _in_maps(inputs, consts=None):
    consts = consts or _host_consts()
    lay = _host_weight_layouts(inputs)
    x = np.ascontiguousarray(np.asarray(inputs["L0_attn_x"], np.float32))
    maps = []
    for k in range(8):
        m = {"x": x[2 * k:2 * k + 2]}
        for key, v in lay.items():
            m[key] = v
        for key, v in consts.items():
            if key in CONST_SPECS:
                m["c_" + key] = v
            else:
                m["t_" + key] = v
        maps.append(m)
    return maps


def kernel(**inputs):
    Bobj, _ = build_program()
    res = run_bass_kernel_spmd(Bobj.nc, _in_maps(inputs), core_ids=list(range(8)))
    return np.concatenate([r["o"] for r in res.results], axis=0)
